# version 05
# baseline (speedup 1.0000x reference)
import contextlib
import math
import numpy as np
import ml_dtypes
import concourse.bass as bass
import concourse.mybir as mybir
from concourse.bass import IndirectOffsetOnAxis
from concourse.bass_utils import run_bass_kernel_spmd

F32 = mybir.dt.float32
BF16 = mybir.dt.bfloat16
I32 = mybir.dt.int32
U32 = mybir.dt.uint32
U8 = mybir.dt.uint8
AF = mybir.ActivationFunctionType
ALU = mybir.AluOpType
NCORES = 8
D = 1024
SEQ = 8192
B = 2
TOKG = B * SEQ
TOKC = TOKG // NCORES
SLOC = SEQ // NCORES
NBF = ml_dtypes.bfloat16

ENGS = ("sync", "tensor", "vector", "scalar", "gpsimd")


def cdiv(a, b):
    return -(-a // b)


class Sem:
    def __init__(self, name, h):
        self.name = name
        self.total = 0
        self.h = h


class Op:
    __slots__ = ("eng", "fn", "waits", "incs", "tag", "is_dma")
    def __init__(self, eng, fn, waits, incs, tag, is_dma=False):
        self.eng = eng
        self.fn = fn
        self.waits = waits
        self.incs = incs
        self.tag = tag
        self.is_dma = is_dma


class Prog:

    def __init__(self, name="prog"):
        self.nc = bass.Bass("TRN2", target_bir_lowering=False)
        self.name = name
        self.ops = {e: [] for e in ENGS}
        self.stack = contextlib.ExitStack()
        self.sems = []
        self.inputs = {}
        self.out_names = []
        self._pending_waits = {e: [] for e in ENGS}
        self._active = set()
        self._built = False

    def inp(self, name, arrays):
        if isinstance(arrays, np.ndarray):
            arrays = [arrays] * NCORES
        if not hasattr(self, "_inp_aps"):
            self._inp_aps = {}
        if name in self.inputs:
            assert all(np.array_equal(a, b) for a, b in zip(self.inputs[name], arrays)), \
                f"input name collision with different data: {name}"
            return self._inp_aps[name]
        a0 = arrays[0]
        dt = {np.dtype(np.float32): F32, np.dtype(NBF): BF16,
              np.dtype(np.int32): I32, np.dtype(np.uint32): U32,
              np.dtype(np.uint8): U8}[a0.dtype]
        t = self.nc.dram_tensor(name, list(a0.shape), dt, kind="ExternalInput")
        self.inputs[name] = [np.ascontiguousarray(a) for a in arrays]
        self._inp_aps[name] = t.ap()
        return self._inp_aps[name]

    def out(self, name, shape, dtype=F32):
        t = self.nc.dram_tensor(name, list(shape), dtype, kind="ExternalOutput")
        self.out_names.append(name)
        return t.ap()

    def dram(self, name, shape, dtype=F32, **kw):
        return self.nc.dram_tensor(name, list(shape), dtype, **kw)

    def sbuf(self, name, shape, dtype=F32):
        if not hasattr(self, "_sbuf_cms"):
            self._sbuf_cms = {}
            self._sbuf_stack = []
            self._name_ctr = {}
            self._alias = {}
        uname = name
        if name in self._name_ctr:
            self._name_ctr[name] += 1
            uname = f"{name}__{self._name_ctr[name]}"
        else:
            self._name_ctr[name] = 0
        uname = "SB." + uname
        b0 = self.nc.sbuf_base
        cm = self.nc.sbuf_tensor(uname, list(shape), dtype)
        h = cm.__enter__()
        if not hasattr(self, "_sbuf_addr"):
            self._sbuf_addr = {}
        self._sbuf_addr[uname] = (b0, self.nc.sbuf_base)
        self._sbuf_cms[uname] = cm
        self._sbuf_stack.append(uname)
        self._alias[name] = uname
        return h

    def sbuf_free(self, *names):
        for n in names:
            u = self._alias.get(n, n)
            self._sbuf_cms.pop(u).__exit__(None, None, None)
            self._sbuf_stack.remove(u)

    def sbuf_mark(self):
        if not hasattr(self, "_sbuf_cms"):
            self._sbuf_cms = {}
            self._sbuf_stack = []
            self._name_ctr = {}
            self._alias = {}
        if not hasattr(self, "_psum_stack"):
            self._psum_stack = []
        return (len(self._sbuf_stack), len(self._psum_stack))

    def sbuf_release(self, mark):
        smark, pmark = mark
        while len(self._psum_stack) > pmark:
            u = self._psum_stack.pop()
            self._psum_cms.pop(u).__exit__(None, None, None)
        while len(self._sbuf_stack) > smark:
            u = self._sbuf_stack.pop()
            self._sbuf_cms.pop(u).__exit__(None, None, None)

    def psum(self, name, shape, dtype=F32):
        shape = list(shape)
        esz = {F32: 4, BF16: 2}.get(dtype, 4)
        per = shape[-1] * esz
        if per % 2048:
            shape[-1] = ((per + 2047) // 2048) * 2048 // esz
        if not hasattr(self, "_name_ctr"):
            self._name_ctr = {}
            self._alias = {}
        uname = name
        if name in self._name_ctr:
            self._name_ctr[name] += 1
            uname = f"{name}__{self._name_ctr[name]}"
        else:
            self._name_ctr[name] = 0
        uname = "PS." + uname
        cm = self.nc.psum_tensor(uname, shape, dtype)
        h = cm.__enter__()
        if not hasattr(self, "_psum_cms"):
            self._psum_cms = {}
        if not hasattr(self, "_psum_stack"):
            self._psum_stack = []
        self._psum_cms[uname] = cm
        self._psum_stack.append(uname)
        self._alias[name] = uname
        return h

    def psum_free(self, *names):
        for n in names:
            u = self._alias.get(n, n)
            self._psum_cms.pop(u).__exit__(None, None, None)
            self._psum_stack.remove(u)

    def sem(self, name=None):
        nm = name or f"s{len(self.sems)}"
        s = Sem(nm, self.stack.enter_context(self.nc.semaphore(nm)))
        self.sems.append(s)
        return s

    def emit(self, eng, fn, wait=(), inc=(), tag="", is_dma=False):
        waits = list(self._pending_waits[eng])
        self._pending_waits[eng] = []
        for w in wait:
            waits.extend(self._toks(w))
        incs = []
        for s, n in inc:
            s.total += n
            incs.append((s, n))
        self.ops[eng].append(Op(eng, fn, waits, incs, tag, is_dma=is_dma))
        self._active.add(eng)
        return [(s, s.total) for s, n in incs]

    @staticmethod
    def _toks(w):
        if isinstance(w, tuple):
            return [w]
        if isinstance(w, list):
            return w
        raise ValueError(f"bad wait {w}")

    def dma(self, eng, out_ap, in_ap, sem, wait=(), tag="dma", **kw):
        def fn(e, out_ap=out_ap, in_ap=in_ap, kw=kw):
            return e.dma_start(out=out_ap, in_=in_ap, **kw)
        return self.emit(eng, fn, wait=wait, inc=[(sem, 16)], tag=tag, is_dma=True)

    def barrier(self, tag="bar"):
        if not hasattr(self, "_bar_sem"):
            self._bar_sem = self.sem("bar_shared")
        bar = self._bar_sem
        n = 0
        for e in ENGS:
            if e in self._active and self.ops[e]:
                last = self.ops[e][-1]
                if last.incs:
                    s0, n0 = last.incs[0]
                    wtok = (s0, s0.total)
                    def nopfn(e_):
                        return e_.nop()
                    self.ops[e].append(Op(e, nopfn, [wtok], [(bar, 1)], f"barnop_{tag}"))
                    bar.total += 1
                else:
                    last.incs.append((bar, 1))
                    bar.total += 1
                n += 1
        if n == 0:
            return
        for e in ENGS:
            self._pending_waits[e].append((bar, bar.total))
        self._active = set()

    def build(self):
        assert not self._built
        self._built = True
        nc = self.nc
        with nc.Block() as block:
            for ename in ENGS:
                ops = self.ops[ename]
                if not ops:
                    continue
                def make(ops):
                    def body(engine):
                        for op in ops:
                            for s, v in op.waits:
                                if v > 0:
                                    engine.wait_ge(s.h, v)
                            try:
                                ins = op.fn(engine)
                            except Exception:
                                import sys
                                print(f"BUILD FAIL at op tag={op.tag} eng={op.eng}", file=sys.stderr)
                                raise
                            for s, n in op.incs:
                                assert ins is not None, f"no inst for incs at {op.tag}"
                                ins.then_inc(s.h, n)
                    return body
                getattr(block, ename)(make(ops))
        return nc

    def run(self, **kw):
        nc = self.build()
        names = list(self.inputs)
        in_maps = [{n: self.inputs[n][c] for n in names} for c in range(NCORES)]
        return run_bass_kernel_spmd(nc, in_maps, core_ids=list(range(NCORES)), **kw)



class Ctx:
    def __init__(self, p: Prog):
        self.p = p
        self.ld = p.sem("ld")
        self.st = p.sem("st")
        self.mm = p.sem("mm")
        self.ve = p.sem("ve")
        self.sc = p.sem("sc")
        self.gp = p.sem("gp")


CONSTS = {"eps_ln": 1e-5, "one": 1.0, "neg1": -1.0, "big": 1e30,
          "eps12": 1e-12, "thr20": 20.0, "half": 0.5, "zero": 0.0}
CONST_IDX = {k: i for i, k in enumerate(CONSTS)}


def load_consts(p, ctx):
    if hasattr(p, "_consts_cache"):
        return p._consts_cache
    arr = np.tile(np.array([list(CONSTS.values())], np.float32), (128, 1))
    ap = p.inp("cconsts", np.ascontiguousarray(arr))
    sb = p.sbuf("cconsts_sb", [128, len(CONSTS)], F32)
    tok = p.dma("sync", sb[:, :], ap, ctx.ld, tag="consts")

    def col(name, nrows=128):
        j = CONST_IDX[name]
        return sb[0:nrows, j:j + 1]
    p._consts_cache = (col, tok)
    return col, tok


def fm_dma_in(p, ctx, dram_ap, sb, ncols, col0=0, nch=8, dtype_rows=D, eng="sync"):
    src = dram_ap[0:dtype_rows, col0:col0 + ncols].rearrange("(k p) n -> p k n", p=128)
    dst = sb[:, 0:nch * ncols].rearrange("p (k n) -> p k n", k=nch)
    return p.dma(eng, dst, src, ctx.ld, tag="fm_in")


def fm_dma_out(p, ctx, sb, dram_ap, ncols, col0=0, nch=8, eng="gpsimd", wait=()):
    dst = dram_ap[0:nch * 128, col0:col0 + ncols].rearrange("(k p) n -> p k n", p=128)
    src = sb[:, 0:nch * ncols].rearrange("p (k n) -> p k n", k=nch)
    return p.dma(eng, dst, src, ctx.st, wait=wait, tag="fm_out")


def fm_dma_out_bf(p, ctx, sb, dram_ap, ncols, col0=0, nch=8, eng="gpsimd", wait=()):
    return p.dma(eng, dram_ap[0:nch * 128, col0:col0 + ncols].rearrange(
        "(k p) n -> p k n", p=128),
        sb[:, :].rearrange("p (k n) -> p k n", k=nch), ctx.st, wait=wait,
        tag="fmout_bf")


def cast_tiles(p, ctx, src, dst, nch, ncols, eng="vector", wait=()):
    def fn(e):
        return e.tensor_copy(out=dst[:, 0:nch * ncols], in_=src[:, 0:nch * ncols])
    s = ctx.ve if eng == "vector" else ctx.sc
    if eng == "vector":
        return p.emit(eng, fn, wait=wait, inc=[(s, 1)], tag="cast")
    def fn2(e):
        return e.copy(out=dst[:, 0:nch * ncols], in_=src[:, 0:nch * ncols])
    return p.emit(eng, fn2, wait=wait, inc=[(s, 1)], tag="cast")


def linear_fm(p, ctx, *, wsb, xsb, osb, kch, fch, ntok, nslice=512,
              bias=None, act=None, scale=None, psums, wait=(),
              out_col0=0, x_col0=0, evac_eng="scalar", obf=None,
              extra_evac=None, x_stride=None, out_stride=None,
              w_stride=None, w_col0=0):
    F = fch * 128
    WS = w_stride if w_stride is not None else F
    xs = x_stride if x_stride is not None else ntok
    os_ = out_stride if out_stride is not None else ntok
    toks = []
    mm_tok = None
    evac_toks = {}
    pi = 0
    for fc in range(fch):
        for n0 in range(0, ntok, nslice):
            ns = min(nslice, ntok - n0)
            ps = psums[pi % len(psums)]
            ev_prev = evac_toks.get(pi % len(psums))
            def mmfn(e, fc=fc, n0=n0, ns=ns, ps=ps):
                ins = None
                for kc in range(kch):
                    ins = e.matmul(out=ps[:, 0:ns],
                                   lhsT=wsb[:, kc * WS + w_col0 + fc * 128: kc * WS + w_col0 + fc * 128 + 128],
                                   rhs=xsb[:, kc * xs + x_col0 + n0: kc * xs + x_col0 + n0 + ns],
                                   start=(kc == 0), stop=(kc == kch - 1))
                return ins
            w = list(wait)
            if ev_prev:
                w.append(ev_prev)
            mm_tok = p.emit("tensor", mmfn, wait=w, inc=[(ctx.mm, 1)], tag=f"mm_f{fc}_n{n0}")
            od = obf if obf is not None else osb
            def evfn(e, fc=fc, n0=n0, ns=ns, ps=ps, od=od):
                out_ap = od[:, out_col0 + fc * os_ + n0: out_col0 + fc * os_ + n0 + ns]
                ps_ap = ps[:, 0:ns]
                if extra_evac is not None:
                    return extra_evac(e, fc, n0, ns, ps_ap, out_ap)
                if act is not None:
                    b = bias[:, fc:fc + 1] if bias is not None else 0.0
                    sc = scale if scale is not None else 1.0
                    return e.activation(out=out_ap, in_=ps_ap, func=act, bias=b, scale=sc)
                if bias is not None and scale is not None:
                    return e.tensor_scalar(out=out_ap, in0=ps_ap, scalar1=bias[:, fc:fc + 1],
                                           scalar2=scale, op0=ALU.add, op1=ALU.mult)
                if bias is not None:
                    return e.tensor_scalar(out=out_ap, in0=ps_ap, scalar1=bias[:, fc:fc + 1],
                                           scalar2=None, op0=ALU.add)
                if scale is not None:
                    return e.tensor_scalar(out=out_ap, in0=ps_ap, scalar1=float(scale),
                                           scalar2=None, op0=ALU.mult)
                if evac_eng == "scalar":
                    return e.copy(out=out_ap, in_=ps_ap)
                return e.tensor_copy(out=out_ap, in_=ps_ap)
            sem = ctx.sc if evac_eng == "scalar" else ctx.ve
            tok = p.emit(evac_eng, evfn, wait=[mm_tok], inc=[(sem, 1)], tag=f"ev_f{fc}_n{n0}")
            evac_toks[pi % len(psums)] = tok
            toks.append(tok)
            pi += 1
    return toks


def load_weight_bf16(p, ctx, dram_ap, K, F, name, wsb, stage, prev=None, wait=()):
    kch = K // 128
    tok = prev
    try:
        src_bf = dram_ap.tensor.dtype == BF16
    except Exception:
        src_bf = False
    if src_bf:
        w = list(wait) + ([tok] if tok else [])
        return p.dma("sync", wsb[:, 0:kch * F].rearrange("p (k f) -> p k f", k=kch),
                     dram_ap[0:K, 0:F].rearrange("(k p) f -> p k f", p=128),
                     ctx.ld, wait=w, tag=f"w_{name}_bf")
    for kc0 in range(0, kch, 8):
        kn = min(8, kch - kc0)
        for f0 in range(0, F, 512):
            fs = min(512, F - f0)
            w = list(wait) + ([tok] if tok else [])
            t = p.dma("sync", stage[:, 0:kn * fs].rearrange("p (k n) -> p k n", k=kn),
                      dram_ap[kc0 * 128:(kc0 + kn) * 128, f0:f0 + fs]
                      .rearrange("(k p) n -> p k n", p=128),
                      ctx.ld, wait=w, tag=f"w_{name}_{kc0}_{f0}")
            def cfn(e, kc0=kc0, kn=kn, f0=f0, fs=fs):
                return e.tensor_copy(
                    out=wsb[:, kc0 * F: (kc0 + kn) * F].rearrange(
                        "p (k f) -> p k f", k=kn)[:, :, f0:f0 + fs],
                    in_=stage[:, 0:kn * fs].rearrange("p (k n) -> p k n", k=kn))
            tok = p.emit("vector", cfn, wait=[t], inc=[(ctx.ve, 1)], tag=f"wc_{name}_{kc0}_{f0}")
    return tok


def ones_bcast_rows(p, ctx, ones1_bf, row_sbuf, psum, ncols, row_off=0, wait=(), nrows=128):
    def fn(e):
        return e.matmul(out=psum[0:nrows, 0:ncols], lhsT=ones1_bf[0:1, 0:nrows],
                        rhs=row_sbuf[0:1, row_off:row_off + ncols], start=True, stop=True)
    return p.emit("tensor", fn, wait=wait, inc=[(ctx.mm, 1)], tag="bcast")


def ln_stats(p, ctx, xsb, ntok, onesd, musb, rstdsb, ps_sum, ps_sq, sqscratch,
             nch=8, wait=(), eps_ap=None):
    toks = []
    prev_v = None
    prev_sq = None
    for n0 in range(0, ntok, 512):
        ns = min(512, ntok - n0)
        def mmsum(e, n0=n0, ns=ns):
            ins = None
            for kc in range(nch):
                ins = e.matmul(out=ps_sum[0:1, 0:ns], lhsT=onesd[:, 0:1],
                               rhs=xsb[:, kc * ntok + n0: kc * ntok + n0 + ns],
                               start=(kc == 0), stop=(kc == nch - 1))
            return ins
        w0 = list(wait) + ([prev_v] if prev_v else [])
        tsum = p.emit("tensor", mmsum, wait=w0, inc=[(ctx.mm, 1)], tag="ln_sum")
        sq_toks = []
        for kc in range(nch):
            def sqfn(e, kc=kc, n0=n0, ns=ns):
                return e.square(out=sqscratch[:, kc * 512: kc * 512 + ns],
                                in_=xsb[:, kc * ntok + n0: kc * ntok + n0 + ns])
            wsq = list(wait) + ([prev_sq] if prev_sq and kc == 0 else []) + ([prev_v] if prev_v and kc == 0 else [])
            sq_toks.append(p.emit("scalar", sqfn, wait=wsq, inc=[(ctx.sc, 1)], tag="ln_sq"))
        def mmsq(e, n0=n0, ns=ns):
            ins = None
            for kc in range(nch):
                ins = e.matmul(out=ps_sq[0:1, 0:ns], lhsT=onesd[:, 0:1],
                               rhs=sqscratch[:, kc * 512: kc * 512 + ns],
                               start=(kc == 0), stop=(kc == nch - 1))
            return ins
        tsq = p.emit("tensor", mmsq, wait=[sq_toks[-1]], inc=[(ctx.mm, 1)], tag="ln_sqmm")
        prev_sq = tsq
        Dd = nch * 128
        def vfn(e, n0=n0, ns=ns):
            e.tensor_scalar(out=musb[0:1, n0:n0 + ns], in0=ps_sum[0:1, 0:ns],
                            scalar1=1.0 / Dd, scalar2=None, op0=ALU.mult)
            return e.tensor_scalar(out=rstdsb[0:1, n0:n0 + ns], in0=ps_sq[0:1, 0:ns],
                                   scalar1=1.0 / Dd, scalar2=None, op0=ALU.mult)
        tv0 = p.emit("vector", vfn, wait=[tsum, tsq], inc=[(ctx.ve, 1)], tag="ln_var0")
        def vfn1(e, n0=n0, ns=ns):
            e.tensor_tensor(out=sqscratch[0:1, 0:ns], in0=musb[0:1, n0:n0 + ns],
                            in1=musb[0:1, n0:n0 + ns], op=ALU.mult)
            return e.tensor_tensor(out=rstdsb[0:1, n0:n0 + ns], in0=rstdsb[0:1, n0:n0 + ns],
                                   in1=sqscratch[0:1, 0:ns], op=ALU.subtract)
        tv = p.emit("vector", vfn1, wait=[tv0], inc=[(ctx.ve, 1)], tag="ln_var1")
        def sfn(e, n0=n0, ns=ns):
            return e.activation(out=rstdsb[0:1, n0:n0 + ns], in_=rstdsb[0:1, n0:n0 + ns],
                                func=AF.Sqrt, bias=eps_ap)
        ts = p.emit("scalar", sfn, wait=[tv], inc=[(ctx.sc, 1)], tag="ln_sqrt")
        def rfn(e, n0=n0, ns=ns):
            return e.reciprocal(out=rstdsb[0:1, n0:n0 + ns], in_=rstdsb[0:1, n0:n0 + ns])
        prev_v = p.emit("vector", rfn, wait=[ts], inc=[(ctx.ve, 1)], tag="ln_recip")
        toks.append(prev_v)
    return toks


def apply_rows_affine(p, ctx, *, xsb, osb, mrow, rrow, ones1, psA, psB, scratch,
                      ntok, nch=8, wait=(), out_off=0, x_off=0):
    toks = []
    prevA = prevB = None
    for n0 in range(0, ntok, 512):
        ns = min(512, ntok - n0)
        wA = list(wait) + ([prevA] if prevA else [])
        tA = ones_bcast_rows(p, ctx, ones1, mrow, psA, ns, row_off=n0, wait=wA)
        wB = [prevB] if prevB else []
        tB = ones_bcast_rows(p, ctx, ones1, rrow, psB, ns, row_off=n0, wait=wB)
        last = None
        for kc in range(nch):
            def vfn(e, kc=kc, n0=n0, ns=ns):
                e.tensor_tensor(out=scratch[:, 0:ns],
                                in0=xsb[:, x_off + kc * ntok + n0: x_off + kc * ntok + n0 + ns],
                                in1=psA[:, 0:ns], op=ALU.subtract)
                return e.tensor_tensor(out=osb[:, out_off + kc * ntok + n0: out_off + kc * ntok + n0 + ns],
                                       in0=scratch[:, 0:ns], in1=psB[:, 0:ns], op=ALU.mult)
            w = [tA, tB] if kc == 0 else []
            last = p.emit("vector", vfn, wait=w, inc=[(ctx.ve, 1)], tag=f"lnap{n0}_{kc}")
        prevA = last
        prevB = last
        toks.append(last)
    return toks


def fm_to_tm(p, ctx, xbf, tmsb, identbf, ptr, ntok, nch=8, wait=(), evac_eng="vector"):
    toks = []
    prev_ev = None
    for t in range(ntok // 128):
        def trfn(e, t=t):
            ins = None
            for kc in range(nch):
                ins = e.transpose(out=ptr[:, kc * 128:(kc + 1) * 128],
                                  in_=xbf[:, kc * ntok + t * 128: kc * ntok + t * 128 + 128],
                                  identity=identbf[:, :])
            return ins
        w = list(wait) + ([prev_ev] if prev_ev else [])
        tt = p.emit("tensor", trfn, wait=w, inc=[(ctx.mm, 1)], tag=f"tr{t}")
        def evfn(e, t=t):
            return e.tensor_copy(out=tmsb[:, t * nch * 128:(t + 1) * nch * 128],
                                 in_=ptr[:, 0:nch * 128])
        prev_ev = p.emit(evac_eng, evfn, wait=[tt], inc=[(ctx.ve, 1)], tag=f"trev{t}")
        toks.append(prev_ev)
    return toks



def to_fm(x):
    return np.ascontiguousarray(x.reshape(TOKG, x.shape[-1]).T)


def from_fm(xfm):
    return np.ascontiguousarray(xfm.T).reshape(B, SEQ, D)


def core_cols(c):
    return [np.arange(b * SEQ + c * SLOC, b * SEQ + (c + 1) * SLOC) for b in range(B)]


def fm_slice(xfm, c):
    i0, i1 = core_cols(c)
    return np.ascontiguousarray(np.concatenate([xfm[:, i0], xfm[:, i1]], axis=1))


def fm_unslice(slices):
    rows = slices[0].shape[0]
    out = np.zeros((rows, TOKG), slices[0].dtype)
    for c in range(NCORES):
        i0, i1 = core_cols(c)
        out[:, i0] = slices[c][:, :SLOC]
        out[:, i1] = slices[c][:, SLOC:]
    return out


def tm_unslice(slices):
    cols = slices[0].shape[1]
    out = np.zeros((TOKG, cols), slices[0].dtype)
    for c in range(NCORES):
        i0, i1 = core_cols(c)
        out[i0] = slices[c][:SLOC]
        out[i1] = slices[c][SLOC:]
    return out


IDENT_BF = np.eye(128, dtype=NBF)
ONES_COL = np.ones((128, 1), dtype=np.float32)
ONES_ROW = np.ones((1, 128), dtype=np.float32)



def launch_ln(xfm, want_tm=True, _fused=None):
    if _fused is None:
        p = Prog("ln")
        ctx = Ctx(p)
        xin = p.inp("x", [fm_slice(xfm, c) for c in range(NCORES)])
        o_fm = p.out("xn_fm", [D, TOKC], F32)
        o_tm = p.out("xn_tm", [TOKC, D], BF16) if want_tm else None
        io = {}
    else:
        p, ctx, io = _fused
        xin = io["x"]
        o_fm = io.get("xn_fm")
        o_tm = io.get("xn_tm")
        want_tm = o_tm is not None
    mk0 = p.sbuf_mark()
    ident = p.inp("ident", IDENT_BF)
    onesd = p.inp("onesd", ONES_COL)
    ones1 = p.inp("ones1", ONES_ROW)

    xs = p.sbuf("xs", [128, 8 * TOKC], F32)
    xn = xs
    xnb = p.sbuf("xnb", [128, 8 * TOKC], BF16)
    tm = p.sbuf("tm", [128, (TOKC // 128) * D], BF16)
    sid = p.sbuf("sid", [128, 128], BF16)
    so1 = p.sbuf("so1", [128, 1], F32)
    sor = p.sbuf("sor", [1, 128], F32)
    mrow = p.sbuf("mrow", [1, TOKC], F32)
    rrow = p.sbuf("rrow", [1, TOKC], F32)
    scratch = p.sbuf("scratch", [128, 8 * 512], F32)
    sc2 = p.sbuf("sc2", [128, 512], F32)
    ps_sum = p.psum("ps_sum", [1, 512], F32)
    ps_sq = p.psum("ps_sq", [1, 512], F32)
    psA = p.psum("psA", [128, 512], F32)
    psB = p.psum("psB", [128, 512], F32)
    ptr = p.psum("ptr", [128, 1024], BF16)

    ccol, tcc = load_consts(p, ctx)
    tx = fm_dma_in(p, ctx, xin, xs, TOKC)
    tid = p.dma("sync", sid[:, :], ident, ctx.ld)
    to1 = p.dma("sync", so1[:, :], onesd, ctx.ld)
    tor = p.dma("sync", sor[:, :], ones1, ctx.ld)
    st = ln_stats(p, ctx, xs, TOKC, so1, mrow, rrow, ps_sum, ps_sq, scratch,
                  wait=[tx, to1, tcc], eps_ap=ccol("eps_ln", 1))
    ap = apply_rows_affine(p, ctx, xsb=xs, osb=xn, mrow=mrow, rrow=rrow,
                           ones1=sor, psA=psA, psB=psB, scratch=sc2,
                           ntok=TOKC, wait=[st[-1], tor])
    tc = cast_tiles(p, ctx, xn, xnb, 8, TOKC, wait=[ap[-1]])
    toks = []
    if o_fm is not None:
        t1 = fm_dma_out(p, ctx, xn, o_fm, TOKC, wait=[ap[-1]])
        toks.append(t1)
    if io.get("xn_bf") is not None:
        tbf = fm_dma_out_bf(p, ctx, xnb, io["xn_bf"], TOKC, wait=[tc])
        toks.append(tbf)
    if io.get("affine") is not None:
        g_ap, b_ap = io["affine"]
        gsb = p.sbuf("ln_gs", [128, 8], F32)
        bsb = p.sbuf("ln_bs", [128, 8], F32)
        tg_ = p.dma("sync", gsb[:, :], g_ap, ctx.ld, tag="lng")
        tb_ = p.dma("sync", bsb[:, :], b_ap, ctx.ld, tag="lnb")
        absb = p.sbuf("ln_abf", [128, 8 * TOKC], BF16)
        asc = p.sbuf("ln_asc", [128, 512], F32)
        lastA = None
        for kc in range(8):
            for n0 in range(0, TOKC, 512):
                def af1(e, kc=kc, n0=n0):
                    return e.tensor_scalar(out=asc[:, :],
                                           in0=xn[:, kc * TOKC + n0: kc * TOKC + n0 + 512],
                                           scalar1=gsb[:, kc:kc + 1], scalar2=None,
                                           op0=ALU.mult)
                w = [ap[-1], tg_, tb_] + ([lastA] if lastA else [])
                tA = p.emit("vector", af1, wait=w, inc=[(ctx.ve, 1)], tag=f"af1{kc}_{n0}")
                def af2(e, kc=kc, n0=n0):
                    return e.tensor_scalar(out=absb[:, kc * TOKC + n0: kc * TOKC + n0 + 512],
                                           in0=asc[:, :], scalar1=bsb[:, kc:kc + 1],
                                           scalar2=None, op0=ALU.add)
                lastA = p.emit("vector", af2, wait=[tA], inc=[(ctx.ve, 1)], tag=f"af2{kc}_{n0}")
        tabf = fm_dma_out_bf(p, ctx, absb, io["abf"], TOKC, wait=[lastA])
        toks.append(tabf)
    if want_tm:
        ttm = fm_to_tm(p, ctx, xnb, tm, sid, ptr, TOKC, wait=[tc, tid])
        t2 = p.dma("gpsimd", o_tm[0:TOKC, 0:D].rearrange("(t p) d -> p t d", p=128),
                   tm[:, :].rearrange("p (t d) -> p t d", t=TOKC // 128),
                   ctx.st, wait=[ttm[-1]], tag="tm_out")
        toks.append(t2)
    if _fused is not None:
        p.barrier("ln_done")
        p.sbuf_release(mk0)
        return toks
    def fin(e):
        return e.nop()
    p.emit("gpsimd", fin, wait=toks, tag="fin")
    res = p.run()
    fm = fm_unslice([res.results[c]["xn_fm"] for c in range(NCORES)])
    tmf = tm_unslice([res.results[c]["xn_tm"] for c in range(NCORES)]) if want_tm else None
    return fm, tmf


def linear_tm(p, ctx, *, xsb, wsb, osb, kch, F, toktiles, psums,
              x_stride, obf_dtype=True, wait=(), tag="ltm", evac_eng="vector",
              x_tile_list=None):
    toks = []
    ev_toks = {}
    pi = 0
    nslice = 512
    for t in range(toktiles):
        xoff = x_tile_list[t] if x_tile_list is not None else t * 128
        for f0 in range(0, F, nslice):
            fs = min(nslice, F - f0)
            ps = psums[pi % len(psums)]
            def mmfn(e, xoff=xoff, f0=f0, fs=fs, ps=ps):
                ins = None
                for kc in range(kch):
                    ins = e.matmul(out=ps[:, 0:fs],
                                   lhsT=xsb[:, kc * x_stride + xoff: kc * x_stride + xoff + 128],
                                   rhs=wsb[:, kc * F + f0: kc * F + f0 + fs],
                                   start=(kc == 0), stop=(kc == kch - 1))
                return ins
            w = list(wait)
            if pi % len(psums) in ev_toks:
                w.append(ev_toks[pi % len(psums)])
            mt = p.emit("tensor", mmfn, wait=w, inc=[(ctx.mm, 1)], tag=f"{tag}_mm{t}_{f0}")
            def evfn(e, t=t, f0=f0, fs=fs, ps=ps):
                if evac_eng == "vector":
                    return e.tensor_copy(out=osb[:, t * F + f0: t * F + f0 + fs], in_=ps[:, 0:fs])
                return e.copy(out=osb[:, t * F + f0: t * F + f0 + fs], in_=ps[:, 0:fs])
            sem = ctx.ve if evac_eng == "vector" else ctx.sc
            et = p.emit(evac_eng, evfn, wait=[mt], inc=[(sem, 1)], tag=f"{tag}_ev{t}_{f0}")
            ev_toks[pi % len(psums)] = et
            toks.append(et)
            pi += 1
    return toks


def build_l0_masks(core):
    W = 256
    m = np.zeros((8, 128, 640), np.float32)
    for q in range(8):
        qs = core * SLOC + q * 128
        i = np.arange(128)[:, None]
        j = np.arange(640)[None, :]
        valid = (j - i >= 0) & (j - i <= 2 * W)
        kg = qs - W + j
        valid &= (kg >= 0) & (kg < SEQ)
        m[q] = np.where(valid, 0.0, -1e30)
    return m






L0_DEBUG = False


def launch_l0_attn(xnfm, W, _fused=None):
    HN, DH, WIN = 16, 64, 256
    NH = SLOC + 2 * WIN
    if _fused is None:
        p = Prog("l0attn")
        ctx = Ctx(p)

        def xn_halo(c):
            cols = []
            for b in range(B):
                base = b * SEQ
                idx = np.clip(np.arange(c * SLOC - WIN, (c + 1) * SLOC + WIN),
                              0, SEQ - 1) + base
                cols.append(xnfm[:, idx])
            return np.ascontiguousarray(np.stack(cols, 0))

        xin = p.inp("xnh", [xn_halo(c) for c in range(NCORES)])
        o_s = p.out("s_fm", [D, TOKC], F32)
    else:
        p, ctx, io = _fused
        xin = io["xnh"]
        o_s = io["s_fm"]
    mk0 = p.sbuf_mark()
    wq = p.inp("wq", np.ascontiguousarray(np.asarray(W["L0_attn_Wq"]).astype(NBF)))
    wk = p.inp("wk", np.ascontiguousarray(np.asarray(W["L0_attn_Wk"]).astype(NBF)))
    wv = p.inp("wv", np.ascontiguousarray(np.asarray(W["L0_attn_Wv"]).astype(NBF)))
    wd = p.inp("wd", np.ascontiguousarray(np.asarray(W["L0_attn_Wd"]).astype(NBF)))
    bq = p.inp("bq", np.ascontiguousarray(W["L0_attn_bq"].reshape(8, 128).T))
    bv = p.inp("bv", np.ascontiguousarray(W["L0_attn_bv"].reshape(8, 128).T))
    bd = p.inp("bd", np.ascontiguousarray(W["L0_attn_bd"].reshape(8, 128).T))
    masks = p.inp("masks", [build_l0_masks(c).astype(NBF) for c in range(NCORES)])
    ident = p.inp("ident", IDENT_BF)
    if L0_DEBUG:
        o_ao = p.out("d_ao", [128, 8 * TOKC], BF16)
        o_k = p.out("d_k", [128, 8 * (SLOC + 2 * 256)], BF16)
        o_v = p.out("d_v", [128, ((SLOC + 2 * 256) // 128) * D], BF16)
        o_qt = p.out("d_qt", [128, 8 * 128], BF16)
        o_dm = p.out("d_dm", [128, 512], F32)
        o_dr = p.out("d_dr", [128, 512], F32)

    wqb = p.sbuf("wqb", [128, 8 * D], BF16)
    wkb = p.sbuf("wkb", [128, 8 * D], BF16)
    wvb = p.sbuf("wvb", [128, 8 * D], BF16)
    wdb = p.sbuf("wdb", [128, 8 * D], BF16)
    ao = p.sbuf("ao", [128, 8 * TOKC], BF16)
    msk = p.sbuf("msk", [128, 8 * 640], BF16)
    sid = p.sbuf("sid", [128, 128], BF16)
    bqs = p.sbuf("bqs", [128, 8], F32)
    bvs = p.sbuf("bvs", [128, 8], F32)
    bds = p.sbuf("bds", [128, 8], F32)
    qt = p.sbuf("qt", [128, 8 * 128], BF16)
    expb = p.sbuf("expb", [128, 640], BF16)
    attb = p.sbuf("attb", [128, 640], BF16)
    mcol = p.sbuf("mcol", [128, 2], F32)
    if L0_DEBUG:
        dbgm = p.sbuf("dbgm", [128, 512], F32)
        dbgr = p.sbuf("dbgr", [128, 512], F32)
    xnb = p.sbuf("xnb", [128, 8 * NH], BF16)
    kbf = p.sbuf("kbf", [128, 8 * NH], BF16)
    vtm = p.sbuf("vtm", [128, (NH // 128) * D], BF16)
    stage = p.sbuf("stage", [128, 8 * 512], F32)
    sfm = stage
    ps0 = p.psum("ps0", [128, 640], F32)
    ptr = p.psum("ptr", [128, 640], BF16)
    ps1 = p.psum("ps1", [64, 128], F32)
    psA = p.psum("psA", [128, 512], F32)
    psB = p.psum("psB", [128, 512], F32)
    psC = p.psum("psC", [128, 512], F32)
    psD = p.psum("psD", [128, 512], F32)

    tid = p.dma("sync", sid[:, :], ident, ctx.ld)
    tbq = p.dma("sync", bqs[:, :], bq, ctx.ld)
    tbv = p.dma("sync", bvs[:, :], bv, ctx.ld)
    tbd = p.dma("sync", bds[:, :], bd, ctx.ld)
    tmsk = p.dma("sync", msk[:, :].rearrange("p (q n) -> p q n", q=8),
                 masks[0:8, 0:128, 0:640].rearrange("q p n -> p q n"), ctx.ld)

    wt = load_weight_bf16(p, ctx, wq, D, D, "wq", wqb, stage)
    wt = load_weight_bf16(p, ctx, wk, D, D, "wk", wkb, stage, prev=wt)
    wt = load_weight_bf16(p, ctx, wv, D, D, "wv", wvb, stage, prev=wt)
    wt = load_weight_bf16(p, ctx, wd, D, D, "wd", wdb, stage, prev=wt)

    for b in range(B):
        p.barrier(f"b{b}")
        tok = None
        for n0 in range(0, NH, 512):
            w = [tok] if tok else []
            t = p.dma("sync", stage[:, :].rearrange("p (k n) -> p k n", k=8),
                      xin[b, 0:D, n0:n0 + 512].rearrange("(k p) n -> p k n", p=128),
                      ctx.ld, wait=w, tag="xload")
            def cfn(e, n0=n0):
                return e.tensor_copy(
                    out=xnb[:, 0:8 * NH].rearrange("p (k n) -> p k n", k=8)[:, :, n0:n0 + 512],
                    in_=stage[:, :].rearrange("p (k n) -> p k n", k=8))
            tok = p.emit("vector", cfn, wait=[t], inc=[(ctx.ve, 1)], tag="xcast")
        tx = tok
        linear_fm(p, ctx, wsb=wkb, xsb=xnb, osb=None, obf=kbf, kch=8, fch=8,
                  ntok=NH, psums=[psA, psB], wait=[tx], evac_eng="vector")
        linear_tm(p, ctx, xsb=xnb, wsb=wvb, osb=vtm, kch=8, F=D,
                  toktiles=NH // 128, psums=[psC, psD], x_stride=NH,
                  wait=[tx], tag="vtm")
        p.barrier(f"proj{b}")
        prev_exp = None
        prev_mmv = None
        for q in range(8):
            def qevac(e, fc, n0, ns, ps_ap, out_ap):
                return e.tensor_scalar(out=out_ap, in0=ps_ap, scalar1=bqs[:, fc:fc + 1],
                                       scalar2=None, op0=ALU.add)
            qw = [prev_exp] if prev_exp else []
            qtoks = linear_fm(p, ctx, wsb=wqb, xsb=xnb, osb=None, obf=qt, kch=8, fch=8,
                              ntok=128, psums=[psA, psB], wait=qw,
                              x_col0=WIN + q * 128, x_stride=NH, out_stride=128,
                              extra_evac=qevac, evac_eng="vector")
            tq = qtoks[-1]
            for h in range(HN):
                kc0 = q * 128
                hrow = 64 * (h % 2)
                hch = h // 2
                def mmfn(e, q=q, kc0=kc0, hrow=hrow, hch=hch):
                    qap = qt[hrow:hrow + 64, hch * 128: hch * 128 + 128]
                    ins = None
                    for o, n in ((0, 512), (512, 128)):
                        ins = e.matmul(out=ps0[:, o:o + n], lhsT=qap,
                                       rhs=kbf[hrow:hrow + 64, hch * NH + kc0 + o:
                                               hch * NH + kc0 + o + n],
                                       start=True, stop=True)
                    return ins
                w = [tq] + ([prev_exp] if prev_exp else [])
                tm_ = p.emit("tensor", mmfn, wait=w, inc=[(ctx.mm, 1)], tag=f"sc{b}_{q}_{h}")
                def mkfn(e, q=q):
                    return e.tensor_tensor(out=ps0[:, 0:640], in0=ps0[:, 0:640],
                                           in1=msk[:, q * 640:(q + 1) * 640], op=ALU.add)
                tmk = p.emit("vector", mkfn, wait=[tm_, tmsk], inc=[(ctx.ve, 1)], tag="mask")
                def mxfn(e):
                    return e.tensor_reduce(out=mcol[:, 0:1], in_=ps0[:, 0:640],
                                           axis=mybir.AxisListType.X, op=ALU.max)
                tmx0 = p.emit("vector", mxfn, wait=[tmk], inc=[(ctx.ve, 1)], tag="max0")
                def mxfn2(e):
                    return e.tensor_scalar(out=mcol[:, 1:2], in0=mcol[:, 0:1],
                                           scalar1=-0.125, scalar2=None, op0=ALU.mult)
                tmx = p.emit("vector", mxfn2, wait=[tmx0], inc=[(ctx.ve, 1)], tag="max1")
                def exfn(e):
                    return e.activation(out=expb[:, 0:640], in_=ps0[:, 0:640],
                                        func=AF.Exp, bias=mcol[:, 1:2], scale=0.125,
                                        accum_out=mcol[:, 0:1])
                it = (b * 8 + q) * 16 + h
                if L0_DEBUG and it < 256:
                    def dmfn(e, it=it):
                        return e.tensor_copy(out=dbgm[:, 2 * it:2 * it + 2], in_=mcol[:, 0:2])
                    p.emit("vector", dmfn, wait=[tmx], inc=[(ctx.ve, 1)], tag="dbgm")
                tex = p.emit("scalar", exfn, wait=[tmx], inc=[(ctx.sc, 1)], tag="exp")
                prev_exp = tex
                if L0_DEBUG and it < 256:
                    def drfn(e, it=it):
                        return e.tensor_copy(out=dbgr[:, 2 * it:2 * it + 2], in_=mcol[:, 0:2])
                    p.emit("vector", drfn, wait=[tex], inc=[(ctx.ve, 1)], tag="dbgr")
                def nrfn(e):
                    return e.reciprocal(out=mcol[:, 1:2], in_=mcol[:, 0:1])
                tnr0 = p.emit("vector", nrfn, wait=[tex], inc=[(ctx.ve, 1)], tag="norm0")
                def nrfn2(e):
                    return e.tensor_scalar(out=expb[:, 0:640], in0=expb[:, 0:640],
                                           scalar1=mcol[:, 1:2], scalar2=None, op0=ALU.mult)
                tnr = p.emit("vector", nrfn2, wait=[tnr0], inc=[(ctx.ve, 1)], tag="norm1")
                def trfn(e):
                    ins = None
                    for j in range(5):
                        ins = e.transpose(out=ptr[:, j * 128:(j + 1) * 128],
                                          in_=expb[:, j * 128:(j + 1) * 128],
                                          identity=sid[:, :])
                    return ins
                w = [tnr, tid] + ([prev_mmv] if prev_mmv else [])
                ttr = p.emit("tensor", trfn, wait=w, inc=[(ctx.mm, 1)], tag="tr")
                def aefn(e):
                    return e.tensor_copy(out=attb[:, 0:640], in_=ptr[:, 0:640])
                tae = p.emit("vector", aefn, wait=[ttr], inc=[(ctx.ve, 1)], tag="attev")
                def mvfn(e, kc0=kc0, h=h):
                    ins = None
                    for j in range(5):
                        ktile = (kc0 + j * 128) // 128
                        ins = e.matmul(out=ps1[:, 0:128],
                                       lhsT=vtm[:, ktile * D + h * 64: ktile * D + h * 64 + 64],
                                       rhs=attb[:, j * 128:(j + 1) * 128],
                                       start=(j == 0), stop=(j == 4))
                    return ins
                tmv = p.emit("tensor", mvfn, wait=[tae], inc=[(ctx.mm, 1)], tag="attnv")
                prev_mmv = tmv
                def oefn(e, b=b, q=q, hrow=hrow, hch=hch):
                    return e.tensor_scalar(
                        out=ao[hrow:hrow + 64, hch * TOKC + b * SLOC + q * 128:
                               hch * TOKC + b * SLOC + q * 128 + 128],
                        in0=ps1[:, 0:128], scalar1=bvs[hrow:hrow + 64, hch:hch + 1],
                        scalar2=None, op0=ALU.add)
                p.emit("vector", oefn, wait=[tmv, tbv], inc=[(ctx.ve, 1)], tag="aoev")
    p.barrier("attn")
    if L0_DEBUG:
        p.dma("gpsimd", o_ao, ao[:, :], ctx.st, tag="dao")
        p.dma("gpsimd", o_k, kbf[:, :], ctx.st, tag="dk")
        p.dma("gpsimd", o_v, vtm[:, :], ctx.st, tag="dv")
        p.dma("gpsimd", o_qt, qt[:, :], ctx.st, tag="dqt")
        p.dma("gpsimd", o_dm, dbgm[:, :], ctx.st, tag="ddm")
        p.dma("gpsimd", o_dr, dbgr[:, :], ctx.st, tag="ddr")
        p.barrier("dbg")

    def devac(e, fc, n0, ns, ps_ap, out_ap):
        return e.tensor_scalar(out=out_ap, in0=ps_ap, scalar1=bds[:, fc:fc + 1],
                               scalar2=None, op0=ALU.add)
    st_toks = []
    prev_st = None
    for n0 in range(0, TOKC, 512):
        w = [tbd] + ([prev_st] if prev_st else [])
        linear_fm(p, ctx, wsb=wdb, xsb=ao, osb=sfm, kch=8, fch=8,
                  ntok=512, psums=[psA, psB], wait=w,
                  x_col0=n0, x_stride=TOKC, out_stride=512,
                  extra_evac=devac, evac_eng="vector")
        p.barrier(f"op{n0}")
        prev_st = fm_dma_out(p, ctx, sfm, o_s, 512, col0=n0)
        st_toks.append(prev_st)
    if _fused is not None:
        p.barrier("l0a_done")
        p.sbuf_release(mk0)
        return st_toks
    def fin(e):
        return e.nop()
    p.emit("gpsimd", fin, wait=st_toks, tag="fin")
    res = p.run()
    if L0_DEBUG:
        return res
    return fm_unslice([res.results[c]["s_fm"] for c in range(NCORES)])



def launch_l0_route(xnfm, W, _fused=None):
    if _fused is None:
        p = Prog("l0route")
        ctx = Ctx(p)
        xin = p.inp("xn", [fm_slice(xnfm, c) for c in range(NCORES)])
        o_idx = p.out("idx", [128, 16], F32)
        o_gate = p.out("gate", [128, 16], F32)
        o_pos = p.out("pos", [1, TOKC], F32)
        o_cnt = p.out("cnt", [16, 2], F32)
        o_idxrow = p.out("idxrow_o", [1, TOKC], F32)
        o_gaterow = p.out("gaterow_o", [1, TOKC], F32)
        io = {}
    else:
        p, ctx, io = _fused
        xin = io["xn"]
    posr = p.sbuf("posr", [1, TOKC], F32)
    idxrow = p.sbuf("idxrow", [1, TOKC], F32)
    gaterow = p.sbuf("gaterow", [1, TOKC], F32)
    cnts = p.sbuf("cnts", [16, 2], F32)
    mk0 = p.sbuf_mark()
    wg = p.inp("wg", W["L0_ffn_Wg"])
    bg = p.inp("bg", np.ascontiguousarray(W["L0_ffn_bg"].reshape(1, 16)))
    ident = p.inp("ident", IDENT_BF)
    identf = p.inp("identf", np.eye(128, dtype=np.float32))
    iota16 = p.inp("iota16", np.arange(16, dtype=np.float32).reshape(16, 1))
    zero16 = p.inp("zero16", np.zeros((16, 1), np.float32))

    stage = p.sbuf("stage", [128, 8 * 512], F32)
    wgs = p.sbuf("wgs", [128, 8 * 16], F32)
    bgs = p.sbuf("bgs", [1, 16], F32)
    logit = p.sbuf("logit", [128, 16 * 16], F32)
    mx8 = p.sbuf("mx8", [128, 8], F32)
    mxi = p.sbuf("mxi", [128, 8], U32)
    idxc = p.sbuf("idxc", [128, 16], F32)
    gatec = p.sbuf("gatec", [128, 16], F32)
    den = p.sbuf("den", [128, 1], F32)
    expsc = p.sbuf("expsc", [128, 16], F32)
    sidf = p.sbuf("sidf", [128, 128], F32)
    io16 = p.sbuf("io16", [16, 1], F32)
    z16 = p.sbuf("z16", [16, 1], F32)
    psg2 = p.psum("psg2", [1, 512], F32)
    ohT = p.sbuf("ohT", [16, TOKC], F32)
    incl = p.sbuf("incl", [16, TOKC], F32)
    tmp16 = p.sbuf("tmp16", [16, TOKC], F32)
    ps1 = p.psum("ps1", [128, 512], F32)
    ps2 = p.psum("ps2", [128, 512], F32)
    psr = p.psum("psr", [1, 512], F32)
    psb = p.psum("psb", [16, 512], F32)

    tw = p.dma("sync", wgs[:, :].rearrange("p (k f) -> p k f", k=8),
               wg[0:D, 0:16].rearrange("(k p) f -> p k f", p=128), ctx.ld)
    tb = p.dma("sync", bgs[:, :], bg, ctx.ld)
    tif = p.dma("sync", sidf[:, :], identf, ctx.ld)
    tio = p.dma("sync", io16[:, :], iota16, ctx.ld)
    tz = p.dma("sync", z16[:, :], zero16, ctx.ld)
    prev = None
    prev_ev = {0: None, 1: None}
    for n0 in range(0, TOKC, 512):
        t = p.dma("sync", stage[:, :].rearrange("p (k n) -> p k n", k=8),
                  xin[0:D, n0:n0 + 512].rearrange("(k p) n -> p k n", p=128),
                  ctx.ld, wait=[prev] if prev else (), tag="xload")
        last_ev = None
        for tt in range(4):
            ti = n0 // 128 + tt
            ps = [ps1, ps2][tt % 2]
            def lfn(e, tt=tt, ps=ps):
                ins = None
                for kc in range(8):
                    ins = e.matmul(out=ps[:, 0:16],
                                   lhsT=stage[:, kc * 512 + tt * 128: kc * 512 + tt * 128 + 128],
                                   rhs=wgs[:, kc * 16:(kc + 1) * 16],
                                   start=(kc == 0), stop=(kc == 7))
                return ins
            w = [t, tw] + ([prev_ev[tt % 2]] if prev_ev[tt % 2] else [])
            tmm = p.emit("tensor", lfn, wait=w, inc=[(ctx.mm, 1)], tag=f"lg{ti}")
            def lev(e, ti=ti, ps=ps):
                return e.tensor_copy(out=logit[:, ti * 16:(ti + 1) * 16], in_=ps[:, 0:16])
            last_ev = p.emit("vector", lev, wait=[tmm], inc=[(ctx.ve, 1)], tag=f"lev{ti}")
            prev_ev[tt % 2] = last_ev
        prev = last_ev
    p.barrier("logits")

    onesr = p.inp("onesr", ONES_ROW)
    so_r = p.sbuf("so_r", [1, 128], F32)
    tor = p.dma("sync", so_r[:, :], onesr, ctx.ld)
    bgb = p.sbuf("bgb", [128, 16], F32)
    psg = p.psum("psg", [128, 16], F32)
    def bgb_mm(e):
        return e.matmul(out=psg[:, 0:16], lhsT=so_r[0:1, 0:128], rhs=bgs[0:1, 0:16],
                        start=True, stop=True)
    tg = p.emit("tensor", bgb_mm, wait=[tb, tor], inc=[(ctx.mm, 1)], tag="bgb")
    def bgb_ev(e):
        return e.tensor_copy(out=bgb[:, :], in_=psg[:, 0:16])
    tgb = p.emit("vector", bgb_ev, wait=[tg], inc=[(ctx.ve, 1)], tag="bgbev")

    prev_tr = None
    prev_ex = None
    rowev = [None] * 16
    growev = [None] * 16
    for t in range(16):
        def addb(e, t=t):
            return e.tensor_tensor(out=logit[:, t * 16:(t + 1) * 16],
                                   in0=logit[:, t * 16:(t + 1) * 16],
                                   in1=bgb[:, :], op=ALU.add)
        ta = p.emit("vector", addb, wait=[tgb], inc=[(ctx.ve, 1)], tag=f"addb{t}")
        def mwfn(e, t=t):
            return e.max(out=mx8[:, 0:8], in_=logit[:, t * 16:(t + 1) * 16])
        tmw0 = p.emit("vector", mwfn, wait=[ta], inc=[(ctx.ve, 1)], tag=f"mw{t}")
        def mifn(e, t=t):
            return e.max_index(out=mxi[:, 0:8], in_max=mx8[:, 0:8],
                               in_values=logit[:, t * 16:(t + 1) * 16])
        tmw = p.emit("vector", mifn, wait=[tmw0], inc=[(ctx.ve, 1)], tag=f"mi{t}")
        def cvfn(e, t=t):
            return e.tensor_copy(out=idxc[:, t:t + 1], in_=mxi[:, 0:1])
        tcv = p.emit("vector", cvfn, wait=[tmw], inc=[(ctx.ve, 1)], tag=f"cv{t}")
        def subfn(e, t=t):
            return e.tensor_scalar(out=expsc[:, 0:16], in0=logit[:, t * 16:(t + 1) * 16],
                                   scalar1=mx8[:, 0:1], scalar2=None, op0=ALU.subtract)
        wsb_ = [tmw] + ([prev_ex] if prev_ex else [])
        tsb = p.emit("vector", subfn, wait=wsb_, inc=[(ctx.ve, 1)], tag=f"sub{t}")
        def exfn(e, t=t):
            return e.activation(out=expsc[:, 0:16], in_=expsc[:, 0:16], func=AF.Exp,
                                accum_out=den[:, 0:1])
        tex = p.emit("scalar", exfn, wait=[tsb], inc=[(ctx.sc, 1)], tag=f"ex{t}")
        prev_ex = tex
        def gfn(e, t=t):
            return e.reciprocal(out=gatec[:, t:t + 1], in_=den[:, 0:1])
        tgf = p.emit("vector", gfn, wait=[tex], inc=[(ctx.ve, 1)], tag=f"g{t}")
        def trfn(e, t=t):
            return e.transpose(out=psr[0:1, (t % 4) * 128:(t % 4) * 128 + 128],
                               in_=idxc[:, t:t + 1], identity=sidf[:, :])
        w = [tcv, tif] + ([prev_tr] if prev_tr else [])
        if t >= 4 and rowev[t - 4] is not None:
            w.append(rowev[t - 4])
        ttr_ = p.emit("tensor", trfn, wait=w, inc=[(ctx.mm, 1)], tag=f"tr{t}")
        prev_tr = ttr_
        def evr(e, t=t):
            return e.tensor_copy(out=idxrow[0:1, t * 128:(t + 1) * 128],
                                 in_=psr[0:1, (t % 4) * 128:(t % 4) * 128 + 128])
        rowev[t] = p.emit("vector", evr, wait=[ttr_], inc=[(ctx.ve, 1)], tag=f"evr{t}")
        def trg(e, t=t):
            return e.transpose(out=psg2[0:1, (t % 4) * 128:(t % 4) * 128 + 128],
                               in_=gatec[:, t:t + 1], identity=sidf[:, :])
        w2_ = [tgf, tif] + ([growev[t - 4]] if t >= 4 and growev[t - 4] else [])
        tg_ = p.emit("tensor", trg, wait=w2_, inc=[(ctx.mm, 1)], tag=f"trg{t}")
        def evg(e, t=t):
            return e.tensor_copy(out=gaterow[0:1, t * 128:(t + 1) * 128],
                                 in_=psg2[0:1, (t % 4) * 128:(t % 4) * 128 + 128])
        growev[t] = p.emit("vector", evg, wait=[tg_], inc=[(ctx.ve, 1)], tag=f"evg{t}")
    trow = rowev[15]

    toh = None
    for n0 in range(0, TOKC, 512):
        def ibc(e, n0=n0):
            return e.matmul(out=psb[0:16, 0:512], lhsT=so_r[0:1, 0:16],
                            rhs=idxrow[0:1, n0:n0 + 512], start=True, stop=True)
        w = [trow, tor] + ([toh] if toh else [])
        tib = p.emit("tensor", ibc, wait=w, inc=[(ctx.mm, 1)], tag=f"ibc{n0}")
        def ohfn(e, n0=n0):
            return e.tensor_scalar(out=ohT[:, n0:n0 + 512], in0=psb[0:16, 0:512],
                                   scalar1=io16[0:16, 0:1], scalar2=None, op0=ALU.is_equal)
        toh = p.emit("vector", ohfn, wait=[tib, tio], inc=[(ctx.ve, 1)], tag=f"oh{n0}")
    def scfn(e):
        return e.tensor_tensor_scan(out=incl[:, :], data0=ohT[:, :],
                                    data1=z16[0:16, 0:1].broadcast_to((16, TOKC)),
                                    initial=0.0, op0=ALU.add, op1=ALU.add)
    tscan = p.emit("vector", scfn, wait=[toh, tz], inc=[(ctx.ve, 1)], tag="scan")
    def pmul(e):
        return e.tensor_tensor(out=tmp16[:, :], in0=incl[:, :], in1=ohT[:, :], op=ALU.mult)
    tpm = p.emit("vector", pmul, wait=[tscan], inc=[(ctx.ve, 1)], tag="pmul")
    ones16 = p.sbuf("ones16", [16, 1], F32)
    onesc = p.inp("ones16c", np.ones((16, 1), np.float32))
    t16 = p.dma("sync", ones16[:, :], onesc, ctx.ld)
    tpe = None
    for n0 in range(0, TOKC, 512):
        def posmm(e, n0=n0):
            return e.matmul(out=psr[0:1, 0:512], lhsT=ones16[0:16, 0:1],
                            rhs=tmp16[:, n0:n0 + 512], start=True, stop=True)
        w = [tpm, t16] + ([tpe] if tpe else [])
        tpmm = p.emit("tensor", posmm, wait=w, inc=[(ctx.mm, 1)], tag=f"posmm{n0}")
        def posev(e, n0=n0):
            return e.tensor_copy(out=posr[0:1, n0:n0 + 512], in_=psr[0:1, 0:512])
        tpe = p.emit("vector", posev, wait=[tpmm], inc=[(ctx.ve, 1)], tag=f"posev{n0}")
    def cfn2(e):
        e.tensor_copy(out=cnts[:, 0:1], in_=incl[:, SLOC - 1:SLOC])
        return e.tensor_tensor(out=cnts[:, 1:2], in0=incl[:, TOKC - 1:TOKC],
                               in1=incl[:, SLOC - 1:SLOC], op=ALU.subtract)
    tcn0 = p.emit("vector", cfn2, wait=[tscan], inc=[(ctx.ve, 1)], tag="cnt")
    p.barrier("rt")
    if _fused is not None:
        t4 = p.dma("gpsimd", io["cnt_dram"], cnts[:, :], ctx.st, tag="ocnt")
        p.barrier("route_done")
        p.sbuf_release(mk0)
        return {"posr": posr, "idxrow": idxrow, "gaterow": gaterow, "cnts": cnts}
    t1 = p.dma("gpsimd", o_idx, idxc[:, :], ctx.st, tag="oidx")
    t2 = p.dma("gpsimd", o_gate, gatec[:, :], ctx.st, tag="ogate")
    t3 = p.dma("gpsimd", o_pos, posr[0:1, :], ctx.st, tag="opos")
    t4 = p.dma("gpsimd", o_cnt, cnts[:, :], ctx.st, tag="ocnt")
    t5 = p.dma("gpsimd", o_idxrow, idxrow[0:1, :], ctx.st, tag="oir")
    t6 = p.dma("gpsimd", o_gaterow, gaterow[0:1, :], ctx.st, tag="ogr")
    def fin(e):
        return e.nop()
    p.emit("gpsimd", fin, wait=[t1, t2, t3, t4, t5, t6], tag="fin")
    res = p.run()
    return [res.results[c] for c in range(NCORES)]



EXP_ONLY_WEIGHTS = 0
EXP_PHASE = 3


def launch_l0_expert(route, xn_tm, W):
    p = Prog("l0exp")
    ctx = Ctx(p)
    CAP = 1024
    counts_all = np.zeros((2, NCORES, 16), np.float32)
    for c in range(NCORES):
        counts_all[0, c] = route[c]["cnt"][:, 0]
        counts_all[1, c] = route[c]["cnt"][:, 1]
    a_counts = p.inp("i_counts", np.ascontiguousarray(counts_all))
    a_pos = p.inp("i_pos", [route[c]["pos"] for c in range(NCORES)])
    a_idxr = p.inp("i_idxr", [route[c]["idxrow_o"] for c in range(NCORES)])
    a_gater = p.inp("i_gater", [route[c]["gaterow_o"] for c in range(NCORES)])
    a_xn = p.inp("xn_tm", xn_tm)
    a_w1 = p.inp("w1", [W["L0_ffn_W1"][2 * c:2 * c + 2] for c in range(NCORES)])
    a_b1 = p.inp("b1", [np.ascontiguousarray(
        W["L0_ffn_b1"][2 * c:2 * c + 2].reshape(2, 32, 128).transpose(0, 2, 1))
        for c in range(NCORES)])
    a_w2 = p.inp("w2", [W["L0_ffn_W2"][2 * c:2 * c + 2] for c in range(NCORES)])
    a_b2 = p.inp("b2", [W["L0_ffn_b2"][2 * c:2 * c + 2] for c in range(NCORES)])
    a_sel = p.inp("i_sel", [np.ascontiguousarray(
        (np.arange(NCORES) < c).astype(np.float32).reshape(NCORES, 1)) for c in range(NCORES)])
    a_notme = p.inp("notme", [np.ascontiguousarray(
        (np.arange(NCORES) != c).astype(np.float32).reshape(NCORES, 1)) for c in range(NCORES)])
    a_sel2 = p.inp("sel2", [np.ascontiguousarray(
        np.eye(16, dtype=np.float32)[:, 2 * c:2 * c + 2]) for c in range(NCORES)])
    a_ones16 = p.inp("ones16", np.ones((16, 1), np.float32))
    a_onesr = p.inp("onesr", ONES_ROW)
    a_idf = p.inp("identf", np.eye(128, dtype=np.float32))
    a_idb = p.inp("identb", IDENT_BF)
    a_iota16 = p.inp("iota16", np.arange(16, dtype=np.float32).reshape(16, 1))
    a_gid1 = p.inp("gid1", [np.ascontiguousarray(
        (np.concatenate(core_cols(c)) + 1).astype(np.float32).reshape(16, 128).T)
        for c in range(NCORES)])
    o_y = p.out("y", [TOKC, D], BF16)
    o_slotc = p.out("slotc_o", [128, 16], F32)
    o_gk = p.out("gk_o", [1, TOKC], F32)
    o_tok = p.out("tok_o", [128, 16], F32)
    o_tf = p.out("tf_o", [16, 1024], F32)
    o_h = p.out("h_o", [128, 32 * 512], BF16)
    o_bf = p.out("bf_o", [128, 8 * 512], BF16)
    o_b2 = p.out("b2_o", [128, D], F32)
    o_w1 = p.out("w1_o", [128, 8 * 4096], BF16)

    cnts = p.sbuf("cnts", [NCORES, 32], F32)
    sel = p.sbuf("sel_s", [NCORES, 1], F32)
    notme = p.sbuf("notme_s", [NCORES, 1], F32)
    sel2 = p.sbuf("sel2_s", [16, 2], F32)
    on16 = p.sbuf("on16", [16, 1], F32)
    onr = p.sbuf("onr", [1, 128], F32)
    sidf = p.sbuf("sidf", [128, 128], F32)
    sidb = p.sbuf("sidb", [128, 128], BF16)
    io16 = p.sbuf("io16", [16, 1], F32)
    offs = p.sbuf("offs", [16, 2], F32)
    slotc = p.sbuf("slotc", [128, 16], F32)
    slotci = p.sbuf("slotci", [128, 16], I32)
    gidc = p.sbuf("gidc", [128, 16], F32)
    tokfull = p.sbuf("tokfull", [16, 1024], F32)
    tokid = p.sbuf("tokid", [128, 16], F32)
    tokidi = p.sbuf("tokidi", [128, 16], I32)
    posr = p.sbuf("posr", [1, TOKC], F32)
    idxr = p.sbuf("idxr", [1, TOKC], F32)
    gater = p.sbuf("gater", [1, TOKC], F32)
    ohT = p.sbuf("ohT", [16, TOKC], F32)
    tmp16 = p.sbuf("tmp16", [16, TOKC], F32)
    offtok = p.sbuf("offtok", [1, TOKC], F32)
    zrow = p.sbuf("zrow", [1, 4096], F32)
    psS = p.psum("psS", [16, 512], F32)
    psR = p.psum("psR", [1, 512], F32)
    psT = p.psum("psT", [128, 512], F32)
    tokmap = p.dram("tokmap_dram", [TOKG, 1], F32)
    tokmap_out = p.dram("tokmap_out", [16, 1024], F32)

    for ap, sb in ((a_counts[0:2, 0:NCORES, 0:16].rearrange("s c e -> c s e"),
                    cnts[:, :].rearrange("c (s e) -> c s e", s=2)),
                   (a_sel, sel[:, :]), (a_notme, notme[:, :]), (a_sel2, sel2[:, :]),
                   (a_ones16, on16[:, :]), (a_onesr, onr[:, :]),
                   (a_idf, sidf[:, :]), (a_idb, sidb[:, :]), (a_iota16, io16[:, :]),
                   (a_gid1, gidc[:, :])):
        p.dma("sync", sb, ap, ctx.ld)
    p.dma("sync", posr[0:1, :], a_pos, ctx.ld)
    p.dma("sync", idxr[0:1, :], a_idxr, ctx.ld)
    p.dma("sync", gater[0:1, :], a_gater, ctx.ld)
    p.barrier("ld")

    def mset(e):
        return e.memset(zrow[0:1, :], 0.0)
    t3 = p.emit("gpsimd", mset, inc=[(ctx.gp, 1)], tag="zrow")
    t4a = p.dma("gpsimd", tokmap[0:TOKG, 0:1].rearrange("(o t) x -> o (t x)", o=1)[0:1, 0:4096],
                zrow[0:1, :], ctx.st, wait=[t3], tag="pre0")
    t4b = p.dma("gpsimd", tokmap[0:TOKG, 0:1].rearrange("(o t) x -> o (t x)", o=1)[0:1, 4096:8192],
                zrow[0:1, :], ctx.st, wait=[t3], tag="pre1")
    t4c = p.dma("gpsimd", tokmap[0:TOKG, 0:1].rearrange("(o t) x -> o (t x)", o=1)[0:1, 8192:12288],
                zrow[0:1, :], ctx.st, wait=[t3], tag="pre2")
    t4d = p.dma("gpsimd", tokmap[0:TOKG, 0:1].rearrange("(o t) x -> o (t x)", o=1)[0:1, 12288:16384],
                zrow[0:1, :], ctx.st, wait=[t3], tag="pre3")

    def offmm(e):
        e.matmul(out=psS[0:16, 0:1], lhsT=cnts[:, 0:16], rhs=sel[:, 0:1],
                 start=True, stop=True)
        e.matmul(out=psS[0:16, 1:2], lhsT=cnts[:, 0:16], rhs=notme[:, 0:1],
                 start=True, stop=False)
        return e.matmul(out=psS[0:16, 1:2], lhsT=cnts[:, 16:32], rhs=sel[:, 0:1],
                        start=False, stop=True)
    t = p.emit("tensor", offmm, inc=[(ctx.mm, 1)], tag="offmm")
    def offev(e):
        return e.tensor_copy(out=offs[:, :], in_=psS[0:16, 0:2])
    toff = p.emit("vector", offev, wait=[t], inc=[(ctx.ve, 1)], tag="offev")
    toh = None
    for n0 in range(0, TOKC, 512):
        def ibc(e, n0=n0):
            return e.matmul(out=psS[0:16, 0:512], lhsT=onr[0:1, 0:16],
                            rhs=idxr[0:1, n0:n0 + 512], start=True, stop=True)
        w = ([toh] if toh else []) + [toff]
        tib = p.emit("tensor", ibc, wait=w, inc=[(ctx.mm, 1)], tag=f"ibc{n0}")
        def ohfn(e, n0=n0):
            return e.tensor_scalar(out=ohT[:, n0:n0 + 512], in0=psS[0:16, 0:512],
                                   scalar1=io16[0:16, 0:1], scalar2=None, op0=ALU.is_equal)
        toh = p.emit("vector", ohfn, wait=[tib], inc=[(ctx.ve, 1)], tag=f"oh{n0}")
    def otm(e):
        e.tensor_scalar(out=tmp16[:, 0:SLOC], in0=ohT[:, 0:SLOC],
                        scalar1=offs[:, 0:1], scalar2=None, op0=ALU.mult)
        return e.tensor_scalar(out=tmp16[:, SLOC:TOKC], in0=ohT[:, SLOC:TOKC],
                               scalar1=offs[:, 1:2], scalar2=None, op0=ALU.mult)
    tot = p.emit("vector", otm, wait=[toh], inc=[(ctx.ve, 1)], tag="otm")
    toe = None
    for n0 in range(0, TOKC, 512):
        def omm(e, n0=n0):
            return e.matmul(out=psR[0:1, 0:512], lhsT=on16[0:16, 0:1],
                            rhs=tmp16[:, n0:n0 + 512], start=True, stop=True)
        w = [tot] + ([toe] if toe else [])
        tom = p.emit("tensor", omm, wait=w, inc=[(ctx.mm, 1)], tag=f"omm{n0}")
        def oev(e, n0=n0):
            return e.tensor_copy(out=offtok[0:1, n0:n0 + 512], in_=psR[0:1, 0:512])
        toe = p.emit("vector", oev, wait=[tom], inc=[(ctx.ve, 1)], tag=f"oev{n0}")
    slotr = ohT
    gkr = tmp16
    def pg1(e):
        e.tensor_tensor(out=posr[0:1, :], in0=posr[0:1, :], in1=offtok[0:1, :], op=ALU.add)
        return e.tensor_scalar(out=posr[0:1, :], in0=posr[0:1, :], scalar1=-1.0,
                               scalar2=None, op0=ALU.add)
    tpg = p.emit("vector", pg1, wait=[toe], inc=[(ctx.ve, 1)], tag="pg")
    def kp1(e):
        return e.tensor_scalar(out=offtok[0:1, :], in0=posr[0:1, :], scalar1=float(CAP),
                               scalar2=None, op0=ALU.is_lt)
    tkp = p.emit("vector", kp1, wait=[tpg], inc=[(ctx.ve, 1)], tag="kp")
    def sl1(e):
        e.tensor_scalar(out=slotr[0:1, :], in0=idxr[0:1, :], scalar1=float(CAP),
                        scalar2=None, op0=ALU.mult)
        return e.tensor_tensor(out=slotr[0:1, :], in0=slotr[0:1, :], in1=posr[0:1, :],
                               op=ALU.add)
    tsl = p.emit("vector", sl1, wait=[tkp], inc=[(ctx.ve, 1)], tag="sl")
    def sl2(e):
        e.tensor_scalar(out=gkr[0:1, :], in0=offtok[0:1, :], scalar1=-1e9, scalar2=1e9,
                        op0=ALU.mult, op1=ALU.add)
        return e.tensor_tensor(out=slotr[0:1, :], in0=slotr[0:1, :], in1=gkr[0:1, :],
                               op=ALU.add)
    tsl2 = p.emit("vector", sl2, wait=[tsl], inc=[(ctx.ve, 1)], tag="sl2")
    def gk1(e):
        return e.tensor_tensor(out=gkr[0:1, :], in0=gater[0:1, :], in1=offtok[0:1, :],
                               op=ALU.mult)
    tgk = p.emit("vector", gk1, wait=[tsl2], inc=[(ctx.ve, 1)], tag="gk")
    tgko = p.dma("gpsimd", o_gk, gkr[0:1, :], ctx.st, wait=[tgk], tag="ogk")

    trs = [None] * 16
    for t_ in range(16):
        def trfn(e, t_=t_):
            return e.transpose(out=psT[:, (t_ % 4):(t_ % 4) + 1],
                               in_=slotr[0:1, t_ * 128:(t_ + 1) * 128],
                               identity=sidf[0:1, 0:1])
        w = [tsl2, tgk] + ([trs[t_ - 4]] if t_ >= 4 and trs[t_ - 4] else [])
        tt = p.emit("tensor", trfn, wait=w, inc=[(ctx.mm, 1)], tag=f"str{t_}")
        def evfn(e, t_=t_):
            return e.tensor_copy(out=slotc[:, t_:t_ + 1], in_=psT[:, (t_ % 4):(t_ % 4) + 1])
        trs[t_] = p.emit("vector", evfn, wait=[tt], inc=[(ctx.ve, 1)], tag=f"sev{t_}")
    tslc = trs[15]
    tslo = p.dma("gpsimd", o_slotc, slotc[:, :], ctx.st, wait=[tslc], tag="oslot")
    def sli(e):
        return e.tensor_copy(out=slotci[:, :], in_=slotc[:, :])
    t2 = p.emit("vector", sli, wait=[tslc], inc=[(ctx.ve, 1)], tag="sli")

    sc_last = None
    for t_ in range(16):
        def scat(e, t_=t_):
            return e.indirect_dma_start(
                out=tokmap[0:TOKG, 0:1],
                out_offset=IndirectOffsetOnAxis(slotci[:, t_:t_ + 1], 0),
                in_=gidc[:, t_:t_ + 1], in_offset=None,
                bounds_check=TOKG, oob_is_err=False)
        sc_last = p.emit("gpsimd", scat, wait=[t2, t4a, t4b, t4c, t4d],
                         inc=[(ctx.st, 16)], tag=f"scat{t_}", is_dma=True)
    p.barrier("scat")
    def cc(e):
        return e.collective_compute(
            "AllReduce", ALU.add, replica_groups=[list(range(NCORES))],
            ins=[tokmap.ap().opt()], outs=[tokmap_out.ap().opt()])
    tcc = p.emit("gpsimd", cc, inc=[(ctx.gp, 1)], tag="allreduce")
    t5 = p.dma("sync", tokfull[:, :], tokmap_out[0:16, 0:1024], ctx.ld, wait=[tcc], tag="tokrd")
    tks = [None] * 16
    for h in range(2):
        for s_ in range(8):
            t_ = 8 * h + s_
            def tmm(e, h=h, s_=s_):
                return e.matmul(out=psT[:, (s_ % 4):(s_ % 4) + 1]
                                if False else psT[:, 0:1],
                                lhsT=tokfull[0:16, s_ * 128:(s_ + 1) * 128],
                                rhs=sel2[:, h:h + 1], start=True, stop=True)
            w = [t5] + ([tks[t_ - 1]] if t_ >= 1 and tks[t_ - 1] else [])
            ttm = p.emit("tensor", tmm, wait=w, inc=[(ctx.mm, 1)], tag=f"tmm{t_}")
            def tev(e, t_=t_):
                return e.tensor_scalar(out=tokid[:, t_:t_ + 1], in0=psT[:, 0:1],
                                       scalar1=-1.0, scalar2=None, op0=ALU.add)
            tks[t_] = p.emit("vector", tev, wait=[ttm], inc=[(ctx.ve, 1)], tag=f"tev{t_}")
    def tki(e):
        return e.tensor_copy(out=tokidi[:, :], in_=tokid[:, :])
    t6 = p.emit("vector", tki, wait=[tks[15]], inc=[(ctx.ve, 1)], tag="tki")
    p.barrier("route_done")
    p.sbuf_free("zrow", "offtok", "tmp16", "ohT", "gater", "idxr", "posr")

    stage = p.sbuf("stage", [128, 8 * 512], F32)
    w1b = p.sbuf("w1b", [128, 8 * 4096], BF16)
    w2b = p.sbuf("w2b", [128, 32 * 1024], BF16)
    b1s = p.sbuf("b1s", [128, 32], F32)
    b2row = p.sbuf("b2row", [1, D], F32)
    b2b = p.sbuf("b2b", [128, D], F32)
    buf_tm = p.sbuf("buf_tm", [128, 4 * D], BF16)
    buf_fm = p.sbuf("buf_fm", [128, 8 * 512], BF16)
    h_bf = p.sbuf("h_bf", [128, 32 * 512], BF16)
    psE = p.psum("psE", [128, 512], F32)
    psF = p.psum("psF", [128, 512], F32)
    ptrb = p.psum("ptrb", [128, 1024], BF16)
    psY = p.psum("psY", [128, 512], F32)

    prev_dma = None
    prev_hts = None
    for e_i in range(EXP_ONLY_WEIGHTS if EXP_ONLY_WEIGHTS else 2):
        wt = load_weight_bf16(p, ctx, a_w1[e_i, 0:D, 0:4096], D, 4096, f"w1_{e_i}",
                              w1b, stage, prev=prev_dma)
        wt2 = load_weight_bf16(p, ctx, a_w2[e_i, 0:4096, 0:D], 4096, D, f"w2_{e_i}",
                               w2b, stage, prev=wt)
        wdep = [prev_dma] if prev_dma else []
        tb1 = p.dma("sync", b1s[:, :], a_b1[e_i, 0:128, 0:32], ctx.ld, wait=wdep, tag="b1")
        tb2 = p.dma("sync", b2row[0:1, :], a_b2[e_i:e_i + 1, 0:D], ctx.ld, wait=wdep, tag="b2")
        tbb = ones_bcast_rows(p, ctx, onr, b2row, psT, 512, wait=[tb2] + wdep, nrows=128)
        def bbf(e):
            return e.tensor_copy(out=b2b[:, 0:512], in_=psT[:, 0:512])
        tb3 = p.emit("vector", bbf, wait=[tbb], inc=[(ctx.ve, 1)], tag="b2ev0")
        tbb2 = ones_bcast_rows(p, ctx, onr, stage, psT, 512, row_off=512, wait=[tb3], nrows=128)
        def bbf2(e):
            return e.tensor_copy(out=b2b[:, 512:1024], in_=psT[:, 0:512])
        tb4 = p.emit("vector", bbf2, wait=[tbb2], inc=[(ctx.ve, 1)], tag="b2ev1")
        for g in ([] if EXP_ONLY_WEIGHTS else range(2)):
            g_toks = []
            def msetb(e):
                return e.memset(buf_tm[:, :], 0.0)
            wmset = ([prev_dma] if prev_dma else []) + [wt2, tb1, tb4]
            tm0 = p.emit("gpsimd", msetb, wait=wmset,
                         inc=[(ctx.gp, 1)], tag="msetb")
            for j in (range(4) if EXP_PHASE >= 0 else []):
                col = 8 * e_i + 4 * g + j
                def gat(e, j=j, col=col):
                    return e.indirect_dma_start(
                        out=buf_tm[:, j * D:(j + 1) * D], out_offset=None,
                        in_=a_xn, in_offset=IndirectOffsetOnAxis(tokidi[:, col:col + 1], 0),
                        bounds_check=TOKG, oob_is_err=False)
                g_toks.append(p.emit("gpsimd", gat, wait=[tm0, t6], inc=[(ctx.ld, 16)],
                                     tag=f"gat{col}", is_dma=True))
            tr_toks = []
            prev_tr2 = None
            for j in (range(4) if EXP_PHASE >= 1 else []):
                def trf(e, j=j):
                    ins = None
                    for kc in range(8):
                        ins = e.transpose(out=ptrb[:, kc * 128:(kc + 1) * 128],
                                          in_=buf_tm[:, j * D + kc * 128: j * D + kc * 128 + 128],
                                          identity=sidb[:, :])
                    return ins
                w = [g_toks[j]] + ([prev_tr2] if prev_tr2 else [])
                ttj = p.emit("tensor", trf, wait=w, inc=[(ctx.mm, 1)], tag=f"bt{j}")
                def tev(e, j=j):
                    return e.tensor_copy(
                        out=buf_fm[:, 0:8 * 512].rearrange("p (k n) -> p k n", k=8)[:, :, j * 128:(j + 1) * 128],
                        in_=ptrb[:, 0:1024].rearrange("p (k n) -> p k n", k=8))
                prev_tr2 = p.emit("vector", tev, wait=[ttj], inc=[(ctx.ve, 1)], tag=f"btev{j}")
                tr_toks.append(prev_tr2)
            if EXP_PHASE >= 2:
                w_h = [tr_toks[-1], wt, tb1] + ([prev_hts] if prev_hts else [])
                hts = linear_fm(p, ctx, wsb=w1b, xsb=buf_fm, osb=None, obf=h_bf,
                                kch=8, fch=32, ntok=512, psums=[psE, psF],
                                wait=w_h, bias=b1s, act=AF.Relu,
                                evac_eng="scalar")
                prev_hts = hts[-1]
            prev_yev = None
            yts = []
            for j in (range(4) if EXP_PHASE >= 3 else []):
                for n0 in (0, 512):
                    def ymm(e, j=j, n0=n0):
                        ins = None
                        for fc in range(32):
                            ins = e.matmul(out=psY[:, 0:512],
                                           lhsT=h_bf[:, fc * 512 + j * 128: fc * 512 + j * 128 + 128],
                                           rhs=w2b[:, fc * 1024 + n0: fc * 1024 + n0 + 512],
                                           start=(fc == 0), stop=(fc == 31))
                        return ins
                    w = [hts[-1], wt2] + ([prev_yev] if prev_yev else [])
                    tym = p.emit("tensor", ymm, wait=w, inc=[(ctx.mm, 1)], tag=f"ymm{j}_{n0}")
                    def yev(e, j=j, n0=n0):
                        return e.tensor_tensor(out=buf_tm[:, j * D + n0: j * D + n0 + 512],
                                               in0=psY[:, 0:512], in1=b2b[:, n0:n0 + 512],
                                               op=ALU.add)
                    prev_yev = p.emit("vector", yev, wait=[tym, tb4], inc=[(ctx.ve, 1)],
                                      tag=f"yev{j}_{n0}")
                    yts.append(prev_yev)
            if EXP_PHASE >= 3:
                prev_dma = p.dma("gpsimd",
                                 o_y[e_i * CAP + g * 512: e_i * CAP + (g + 1) * 512, 0:D]
                                 .rearrange("(t p) d -> p t d", p=128),
                                 buf_tm[:, :].rearrange("p (t d) -> p t d", t=4),
                                 ctx.st, wait=[yts[-1]], tag=f"ystore{e_i}_{g}")
            else:
                prev_dma = g_toks[-1] if g_toks else (tm0 if EXP_PHASE < 0 else prev_dma)
    p.barrier("dbg")
    ttk = p.dma("gpsimd", o_tok, tokid[:, :], ctx.st, tag="otok")
    p.dma("gpsimd", o_h, h_bf[:, :], ctx.st, tag="oh")
    p.dma("gpsimd", o_bf, buf_fm[:, :], ctx.st, tag="obf")
    p.dma("gpsimd", o_b2, b2b[:, :], ctx.st, tag="ob2")
    p.dma("gpsimd", o_w1, w1b[:, :], ctx.st, tag="ow1")
    ttf = p.dma("gpsimd", o_tf, tokfull[:, :], ctx.st, tag="otf")
    def fin(e):
        return e.nop()
    fw = [tgko, tslo, ttk, ttf] + ([prev_dma] if prev_dma else [wt2])
    p.emit("gpsimd", fin, wait=fw, tag="fin")
    res = p.run()
    launch_l0_expert.last_res = res
    y_slab = np.concatenate([res.results[c]["y"] for c in range(NCORES)], 0)
    return (y_slab, [res.results[c]["slotc_o"] for c in range(NCORES)],
            [res.results[c]["gk_o"] for c in range(NCORES)])



def launch_combine(x_fm, s_fm, ff_fm=None, moe=None, _fused=None):
    moe_sb = None
    if _fused is None:
        p = Prog("combine")
        ctx = Ctx(p)
        xin = p.inp("xi", [fm_slice(x_fm, c) for c in range(NCORES)])
        sin = p.inp("si", [fm_slice(s_fm, c) for c in range(NCORES)])
        is_moe = moe is not None
        if is_moe:
            y_slab, slotc, gk = moe
            a_y = p.inp("ysl", y_slab)
            a_slot = p.inp("slot", [np.ascontiguousarray(s) for s in slotc])
            a_gk = p.inp("gk", [np.ascontiguousarray(g) for g in gk])
        else:
            fin_ = p.inp("ffi", [fm_slice(ff_fm, c) for c in range(NCORES)])
        o_x = p.out("xn_o", [D, TOKC], F32)
    else:
        p, ctx, io = _fused
        xin = io["xi"]
        sin = io["si"]
        o_x = io["xn_o"]
        is_moe = "ysl" in io
        if is_moe:
            a_y = io["ysl"]
            moe_sb = (io["slot_sb"], io["gk_sb"])
        else:
            fin_ = io["ffi"]
    mk0 = p.sbuf_mark()
    if is_moe:
        a_idb = p.inp("identb", IDENT_BF)
        a_onesr = p.inp("onesr", ONES_ROW)

    stage = p.sbuf("stage", [128, 8 * 512], F32)
    xs = p.sbuf("xs", [128, 8 * TOKC], F32)
    sstage = p.sbuf("sstage", [128, 8 * 512], F32)
    gslice = p.sbuf("gslice", [128, 8 * 512], F32)
    psA = p.psum("psA", [128, 512], F32)
    if is_moe:
        ffb = p.sbuf("ffb", [128, 8 * TOKC], BF16)
        ytile = p.sbuf("ytile", [128, D], BF16)
        if moe_sb is None:
            slotci = p.sbuf("slotci", [128, 16], I32)
            slotf = p.sbuf("slotf", [128, 16], F32)
            gkr = p.sbuf("gkr", [1, TOKC], F32)
        else:
            slotci, gkr = moe_sb
        sidb = p.sbuf("sidb", [128, 128], BF16)
        onr = p.sbuf("onr", [1, 128], F32)
        ptrb = p.psum("ptrb", [128, 1024], BF16)

    tok = None
    for n0 in range(0, TOKC, 512):
        t = p.dma("sync", stage[:, :].rearrange("p (k n) -> p k n", k=8),
                  xin[0:D, n0:n0 + 512].rearrange("(k p) n -> p k n", p=128),
                  ctx.ld, wait=[tok] if tok else (), tag="xload")
        def cfn(e, n0=n0):
            return e.tensor_copy(
                out=xs[:, 0:8 * TOKC].rearrange("p (k n) -> p k n", k=8)[:, :, n0:n0 + 512],
                in_=stage[:, :].rearrange("p (k n) -> p k n", k=8))
        tok = p.emit("vector", cfn, wait=[t], inc=[(ctx.ve, 1)], tag="xcast")
    tx = tok

    if is_moe:
        tid = p.dma("sync", sidb[:, :], a_idb, ctx.ld, tag="id")
        tor = p.dma("sync", onr[:, :], a_onesr, ctx.ld, tag="or")
        if moe_sb is None:
            tgk = p.dma("sync", gkr[0:1, :], a_gk, ctx.ld, tag="gk")
            tsl = p.dma("sync", slotf[:, :], a_slot, ctx.ld, tag="slot")
            def sli(e):
                return e.tensor_copy(out=slotci[:, :], in_=slotf[:, :])
            tsli = p.emit("vector", sli, wait=[tsl], inc=[(ctx.ve, 1)], tag="sli")
        else:
            tgk = tor
            tsli = tor
        prev_g = None
        for t_ in range(16):
            def gat(e, t_=t_):
                return e.indirect_dma_start(
                    out=ytile[:, :], out_offset=None,
                    in_=a_y, in_offset=IndirectOffsetOnAxis(slotci[:, t_:t_ + 1], 0),
                    bounds_check=TOKG, oob_is_err=False)
            w = [tsli, tx, tgk, tid, tor] + ([prev_ev] if t_ > 0 else [])
            tg = p.emit("gpsimd", gat, wait=w, inc=[(ctx.ld, 16)], tag=f"gat{t_}",
                        is_dma=True)
            def trf(e, t_=t_):
                ins = None
                for kc in range(8):
                    ins = e.transpose(out=ptrb[:, kc * 128:(kc + 1) * 128],
                                      in_=ytile[:, kc * 128:kc * 128 + 128],
                                      identity=sidb[:, :])
                return ins
            ttr = p.emit("tensor", trf, wait=[tg], inc=[(ctx.mm, 1)], tag=f"tr{t_}")
            def tev(e, t_=t_):
                return e.tensor_copy(
                    out=ffb[:, 0:8 * TOKC].rearrange("p (k n) -> p k n", k=8)[:, :, t_ * 128:(t_ + 1) * 128],
                    in_=ptrb[:, 0:1024].rearrange("p (k n) -> p k n", k=8))
            prev_ev = p.emit("vector", tev, wait=[ttr], inc=[(ctx.ve, 1)], tag=f"tev{t_}")
        tff = prev_ev
        p.barrier("gather")

    prev_s = None
    prev_c = {}
    for n0 in range(0, TOKC, 512):
        ws = [tx] + ([prev_s] if prev_s else [])
        ts = p.dma("sync", sstage[:, :].rearrange("p (k n) -> p k n", k=8),
                   sin[0:D, n0:n0 + 512].rearrange("(k p) n -> p k n", p=128),
                   ctx.ld, wait=ws, tag="sload")
        if is_moe:
            def gbc(e, n0=n0):
                return e.matmul(out=psA[:, 0:512], lhsT=onr[0:1, 0:128],
                                rhs=gkr[0:1, n0:n0 + 512], start=True, stop=True)
            wb = [tgk] + ([prev_c.get("mm")] if prev_c.get("mm") else [])
            tb = p.emit("tensor", gbc, wait=wb, inc=[(ctx.mm, 1)], tag=f"gbc{n0}")
        else:
            wf = [prev_c.get("ff")] if prev_c.get("ff") else []
            tfl = p.dma("sync", gslice[:, :].rearrange("p (k n) -> p k n", k=8),
                        fin_[0:D, n0:n0 + 512].rearrange("(k p) n -> p k n", p=128),
                        ctx.ld, wait=wf, tag="ffload")
        last = None
        for kc in range(8):
            if is_moe:
                def m1(e, kc=kc, n0=n0):
                    return e.tensor_tensor(
                        out=gslice[:, kc * 512:(kc + 1) * 512],
                        in0=ffb[:, kc * TOKC + n0: kc * TOKC + n0 + 512],
                        in1=psA[:, 0:512], op=ALU.mult)
                w = [tb, tff] if kc == 0 else []
                t1 = p.emit("vector", m1, wait=w, inc=[(ctx.ve, 1)], tag=f"m1_{n0}_{kc}")
            else:
                def m1(e, kc=kc):
                    return e.tensor_copy(out=gslice[:, kc * 512:(kc + 1) * 512],
                                         in_=gslice[:, kc * 512:(kc + 1) * 512])
                t1 = [tfl] if kc == 0 else None
            def sl(e, kc=kc):
                return e.activation(out=gslice[:, kc * 512:(kc + 1) * 512],
                                    in_=gslice[:, kc * 512:(kc + 1) * 512], func=AF.Silu)
            wsl = ([t1] if is_moe else ([tfl] if kc == 0 else []))
            t2 = p.emit("scalar", sl, wait=wsl, inc=[(ctx.sc, 1)], tag=f"sil{n0}_{kc}")
            def cmb(e, kc=kc, n0=n0):
                e.tensor_tensor(out=gslice[:, kc * 512:(kc + 1) * 512],
                                in0=gslice[:, kc * 512:(kc + 1) * 512],
                                in1=sstage[:, kc * 512:(kc + 1) * 512], op=ALU.mult)
                return e.tensor_tensor(
                    out=xs[:, kc * TOKC + n0: kc * TOKC + n0 + 512],
                    in0=xs[:, kc * TOKC + n0: kc * TOKC + n0 + 512],
                    in1=gslice[:, kc * 512:(kc + 1) * 512], op=ALU.add)
            last = p.emit("vector", cmb, wait=[t2, ts], inc=[(ctx.ve, 1)], tag=f"cmb{n0}_{kc}")
        prev_s = last
        prev_c["mm"] = last
        prev_c["ff"] = last
    p.barrier("combine")
    st = []
    for n0 in range(0, TOKC, 512):
        st.append(p.dma("gpsimd",
                        o_x[0:D, n0:n0 + 512].rearrange("(k p) n -> p k n", p=128),
                        xs[:, 0:8 * TOKC].rearrange("p (k n) -> p k n", k=8)[:, :, n0:n0 + 512],
                        ctx.st, tag=f"xout{n0}"))
    if _fused is not None:
        p.barrier("cmb_done")
        p.sbuf_release(mk0)
        return st
    def fin2(e):
        return e.nop()
    p.emit("gpsimd", fin2, wait=st, tag="fin")
    res = p.run()
    return fm_unslice([res.results[c]["xn_o"] for c in range(NCORES)])



def launch_linear(x_fm, W, bias=None, act=None, out_name="lin"):
    K, F = W.shape
    kch, fch = K // 128, F // 128
    p = Prog(out_name)
    ctx = Ctx(p)
    in_bf = x_fm.dtype == NBF
    xin = p.inp("xi", [fm_slice(x_fm, c) for c in range(NCORES)])
    wi = p.inp("wi", np.ascontiguousarray(W))
    bi = p.inp("bi", np.ascontiguousarray(
        bias.reshape(fch, 128).T)) if bias is not None else None
    o = p.out("o", [F, TOKC], F32)

    stage = p.sbuf("stage", [128, kch * 512], F32)
    xb = p.sbuf("xb", [128, kch * TOKC], BF16)
    wb = p.sbuf("wb", [128, kch * F], BF16)
    bs = p.sbuf("bs", [128, fch], F32) if bias is not None else None
    osl = p.sbuf("osl", [128, fch * 512], F32)
    psA = p.psum("psA", [128, 512], F32)
    psB = p.psum("psB", [128, 512], F32)

    tok = None
    if in_bf:
        tok = p.dma("sync", xb[:, :].rearrange("p (k n) -> p k n", k=kch),
                    xin[0:K, 0:TOKC].rearrange("(k p) n -> p k n", p=128),
                    ctx.ld, tag="xload")
    else:
        for n0 in range(0, TOKC, 512):
            t = p.dma("sync", stage[:, :].rearrange("p (k n) -> p k n", k=kch),
                      xin[0:K, n0:n0 + 512].rearrange("(k p) n -> p k n", p=128),
                      ctx.ld, wait=[tok] if tok else (), tag="xload")
            def cfn(e, n0=n0):
                return e.tensor_copy(
                    out=xb[:, :].rearrange("p (k n) -> p k n", k=kch)[:, :, n0:n0 + 512],
                    in_=stage[:, :].rearrange("p (k n) -> p k n", k=kch))
            tok = p.emit("vector", cfn, wait=[t], inc=[(ctx.ve, 1)], tag="xcast")
    wt = load_weight_bf16(p, ctx, wi, K, F, "w", wb, stage, prev=tok)
    tb = p.dma("sync", bs[:, :], bi, ctx.ld, tag="b") if bias is not None else None
    p.barrier("ld")

    prev_st = None
    st = []
    for n0 in range(0, TOKC, 512):
        def evac(e, fc, nn0, ns, ps_ap, out_ap):
            if act is not None:
                b = bs[:, fc:fc + 1] if bias is not None else 0.0
                return e.activation(out=out_ap, in_=ps_ap, func=act, bias=b)
            if bias is not None:
                return e.tensor_scalar(out=out_ap, in0=ps_ap, scalar1=bs[:, fc:fc + 1],
                                       scalar2=None, op0=ALU.add)
            return e.copy(out=out_ap, in_=ps_ap)
        w = ([prev_st] if prev_st else [])
        linear_fm(p, ctx, wsb=wb, xsb=xb, osb=osl, kch=kch, fch=fch,
                  ntok=512, psums=[psA, psB], wait=w,
                  x_col0=n0, x_stride=TOKC, out_stride=512,
                  extra_evac=evac, evac_eng="scalar")
        p.barrier(f"sl{n0}")
        prev_st = p.dma("gpsimd",
                        o[0:F, n0:n0 + 512].rearrange("(k p) n -> p k n", p=128),
                        osl[:, :].rearrange("p (k n) -> p k n", k=fch),
                        ctx.st, tag=f"out{n0}")
        st.append(prev_st)
    def fin(e):
        return e.nop()
    p.emit("gpsimd", fin, wait=st, tag="fin")
    res = p.run()
    return fm_unslice([res.results[c]["o"] for c in range(NCORES)])



def launch_l1_proj(xnfm, W):
    p = Prog("l1proj")
    ctx = Ctx(p)
    xin = p.inp("xi", [fm_slice(xnfm, c) for c in range(NCORES)])
    wq = p.inp("wq", np.ascontiguousarray(W["L1_attn_Wq"].T))
    wk = p.inp("wk", np.ascontiguousarray(W["L1_attn_Wk"].T))
    wv = p.inp("wv", np.ascontiguousarray(W["L1_attn_Wv"].T))
    o_q = p.out("oq", [TOKC, D], BF16)
    o_k = p.out("ok", [TOKC, D], BF16)
    o_v = p.out("ov", [TOKC, D], BF16)
    o_kb = p.out("okb", [TOKC, 16], F32)

    stage = p.sbuf("stage", [128, 8 * 512], F32)
    xb = p.sbuf("xb", [128, 8 * TOKC], BF16)
    wqb = p.sbuf("wqb", [128, 8 * D], BF16)
    wkb = p.sbuf("wkb", [128, 8 * D], BF16)
    wvb = p.sbuf("wvb", [128, 8 * D], BF16)
    otile = p.sbuf("otile", [128, D], BF16)
    ktile = p.sbuf("ktile", [128, D], F32)
    sq = p.sbuf("sq", [128, D], F32)
    nrm = p.sbuf("nrm", [128, 16], F32)
    kbt = p.sbuf("kbt", [128, 16], F32)
    psA = p.psum("psA", [128, 512], F32)
    psB = p.psum("psB", [128, 512], F32)

    tok = None
    for n0 in range(0, TOKC, 512):
        t = p.dma("sync", stage[:, :].rearrange("p (k n) -> p k n", k=8),
                  xin[0:D, n0:n0 + 512].rearrange("(k p) n -> p k n", p=128),
                  ctx.ld, wait=[tok] if tok else (), tag="xload")
        def cfn(e, n0=n0):
            return e.tensor_copy(
                out=xb[:, :].rearrange("p (k n) -> p k n", k=8)[:, :, n0:n0 + 512],
                in_=stage[:, :].rearrange("p (k n) -> p k n", k=8))
        tok = p.emit("vector", cfn, wait=[t], inc=[(ctx.ve, 1)], tag="xcast")
    wt = load_weight_bf16(p, ctx, wq, D, D, "wq", wqb, stage, prev=tok)
    wt = load_weight_bf16(p, ctx, wk, D, D, "wk", wkb, stage, prev=wt)
    wt = load_weight_bf16(p, ctx, wv, D, D, "wv", wvb, stage, prev=wt)
    p.barrier("ld")

    prev = {}
    st = []
    for t_ in range(16):
        for which, wsb, out_ap in (("q", wqb, o_q), ("v", wvb, o_v), ("k", wkb, o_k)):
            for half in range(2):
                ps = [psA, psB][half]
                def mmfn(e, t_=t_, wsb=wsb, half=half, ps=ps):
                    ins = None
                    for kc in range(8):
                        ins = e.matmul(out=ps[:, 0:512],
                                       lhsT=xb[:, kc * TOKC + t_ * 128: kc * TOKC + t_ * 128 + 128],
                                       rhs=wsb[:, kc * D + half * 512: kc * D + half * 512 + 512],
                                       start=(kc == 0), stop=(kc == 7))
                    return ins
                w = ([prev.get(half)] if prev.get(half) else [])
                tm_ = p.emit("tensor", mmfn, wait=w, inc=[(ctx.mm, 1)], tag=f"mm{which}{t_}_{half}")
                if which == "k":
                    def kev(e, half=half, ps=ps):
                        e.tensor_copy(out=ktile[:, half * 512:(half + 1) * 512], in_=ps[:, 0:512])
                        return e.tensor_copy(out=otile[:, half * 512:(half + 1) * 512], in_=ps[:, 0:512])
                    prev[half] = p.emit("vector", kev, wait=[tm_], inc=[(ctx.ve, 1)], tag=f"kev{t_}_{half}")
                else:
                    def oev(e, half=half, ps=ps):
                        return e.tensor_copy(out=otile[:, half * 512:(half + 1) * 512], in_=ps[:, 0:512])
                    prev[half] = p.emit("vector", oev, wait=[tm_], inc=[(ctx.ve, 1)], tag=f"oev{which}{t_}_{half}")
            wst = [prev[0], prev[1]]
            tst = p.dma("gpsimd", out_ap[t_ * 128:(t_ + 1) * 128, 0:D], otile[:, :],
                        ctx.st, wait=wst, tag=f"st{which}{t_}")
            st.append(tst)
            prev[0] = tst
            prev[1] = tst
            if which == "k":
                def sqf(e):
                    return e.square(out=sq[:, :], in_=ktile[:, :])
                t1 = p.emit("scalar", sqf, wait=[prev[0]], inc=[(ctx.sc, 1)], tag=f"sq{t_}")
                def red(e):
                    ins = None
                    for h in range(16):
                        e.tensor_reduce(out=nrm[:, h:h + 1], in_=sq[:, h * 64:(h + 1) * 64],
                                        axis=mybir.AxisListType.X, op=ALU.add)
                        ins = e.tensor_reduce(out=kbt[:, h:h + 1], in_=ktile[:, h * 64:(h + 1) * 64],
                                              axis=mybir.AxisListType.X, op=ALU.add)
                    return ins
                t2 = p.emit("vector", red, wait=[t1], inc=[(ctx.ve, 1)], tag=f"red{t_}")
                def srt(e):
                    return e.sqrt(out=nrm[:, :], in_=nrm[:, :])
                t3 = p.emit("scalar", srt, wait=[t2], inc=[(ctx.sc, 1)], tag=f"srt{t_}")
                def fin1(e):
                    return e.tensor_scalar(out=nrm[:, :], in0=nrm[:, :], scalar1=1e-12,
                                           scalar2=None, op0=ALU.max)
                t4a = p.emit("vector", fin1, wait=[t3], inc=[(ctx.ve, 1)], tag=f"fa{t_}")
                def fin1b(e):
                    return e.reciprocal(out=nrm[:, :], in_=nrm[:, :])
                t4 = p.emit("vector", fin1b, wait=[t4a], inc=[(ctx.ve, 1)], tag=f"fin{t_}")
                def fin2(e):
                    return e.tensor_tensor(out=kbt[:, :], in0=kbt[:, :], in1=nrm[:, :], op=ALU.mult)
                t5a = p.emit("vector", fin2, wait=[t4], inc=[(ctx.ve, 1)], tag=f"f2a{t_}")
                def fin2b(e):
                    return e.tensor_scalar(out=kbt[:, :], in0=kbt[:, :], scalar1=1.0 / 64,
                                           scalar2=None, op0=ALU.mult)
                t5 = p.emit("vector", fin2b, wait=[t5a], inc=[(ctx.ve, 1)], tag=f"fin2{t_}")
                tkb = p.dma("gpsimd", o_kb[t_ * 128:(t_ + 1) * 128, 0:16], kbt[:, :],
                            ctx.st, wait=[t5], tag=f"stkb{t_}")
                st.append(tkb)
                prev["kb"] = t5
    def fin(e):
        return e.nop()
    p.emit("gpsimd", fin, wait=st[-3:], tag="fin")
    res = p.run()
    q = tm_unslice([res.results[c]["oq"] for c in range(NCORES)])
    k = tm_unslice([res.results[c]["ok"] for c in range(NCORES)])
    v = tm_unslice([res.results[c]["ov"] for c in range(NCORES)])
    kb = tm_unslice([res.results[c]["okb"] for c in range(NCORES)])
    return q, k, v, kb



def launch_l1_scan(q_tm, k_tm, v_tm, kbar, W, _fused=None):
    C = 128
    NCH = SEQ // C
    if _fused is None:
        p = Prog("l1scan")
        ctx = Ctx(p)
        io = None
        hcols = lambda c: slice(128 * c, 128 * c + 128)
        a_q = p.inp("q", [np.ascontiguousarray(q_tm[:, hcols(c)]) for c in range(NCORES)])
        a_k = p.inp("k", [np.ascontiguousarray(k_tm[:, hcols(c)]) for c in range(NCORES)])
        a_v = p.inp("v", [np.ascontiguousarray(v_tm[:, hcols(c)]) for c in range(NCORES)])
        def kb_layout(c):
            kb = kbar[:, 2 * c:2 * c + 2]
            kb = kb.reshape(2, 64, 128, 2)
            return np.ascontiguousarray(kb.transpose(3, 0, 2, 1))
        a_kb = p.inp("kb", [kb_layout(c) for c in range(NCORES)])
    else:
        p, ctx, io = _fused
        mkS = p.sbuf_mark()
        a_q, a_k, a_v = io["q"], io["k"], io["v"]
        a_kb3 = io["kb3"]
        o_slab = io["s1slab"]
        a_wos = p.inp("l1_wos", [np.ascontiguousarray(
            W["L1_attn_Wo"][:, 128 * c:128 * (c + 1)].T.astype(NBF))
            for c in range(NCORES)])
    WG = W["L1_attn_W_gamma"]; BG = W["L1_attn_b_gamma"]
    WL = W["L1_attn_W_lambda"]; BL = W["L1_attn_b_lambda"]
    a_gates = p.inp("gates", [np.ascontiguousarray(np.tile(
        np.array([[WG[2 * c], WG[2 * c + 1], BG[2 * c], BG[2 * c + 1],
                   WL[2 * c], WL[2 * c + 1], BL[2 * c], BL[2 * c + 1]]], np.float32),
        (128, 1))) for c in range(NCORES)])
    Lt = np.tril(np.ones((128, 128), np.float32)).T
    a_lt = p.inp("ltri", np.ascontiguousarray(Lt))
    msub = np.where(np.arange(128)[None, :] > np.arange(128)[:, None], -1e30, 0.0).astype(np.float32)
    a_ms = p.inp("msub", msub)
    a_idf = p.inp("identf", np.eye(128, dtype=np.float32))
    a_idb = p.inp("identb", IDENT_BF)
    a_onesr = p.inp("onesr", ONES_ROW)
    if io is None:
        o_y = p.out("oy", [TOKG, 128], BF16)
    else:
        kbi = p.sbuf("kbi", [64, 128], F32)
        yfm = p.sbuf("yfm", [128, SEQ], BF16)
        s1sl = p.sbuf("s1sl", [128, 1024], BF16)
        wos = p.sbuf("wos", [128, 1024], BF16)

    qs = p.sbuf("qs", [128, NCH * 64], BF16)
    ks = p.sbuf("ks", [128, NCH * 64], BF16)
    vs = p.sbuf("vs", [128, NCH * 64], BF16)
    kbb = p.sbuf("kbb", [128, NCH], F32)
    gat = p.sbuf("gat", [128, 8], F32)
    lt_s = p.sbuf("lt_s", [128, 128], F32)
    ms_s = p.sbuf("ms_s", [128, 128], F32)
    idf = p.sbuf("idf", [128, 128], F32)
    idb = p.sbuf("idb", [128, 128], BF16)
    onr = p.sbuf("onr", [1, 128], F32)
    S = p.sbuf("S", [64, 64], F32)
    Sbf = p.sbuf("Sbf", [64, 64], BF16)
    gcol = p.sbuf("gcol", [128, 1], F32)
    lcol = p.sbuf("lcol", [128, 1], F32)
    rho = p.sbuf("rho", [128, 1], F32)
    omr = p.sbuf("omr", [128, 1], F32)
    lam_c = p.sbuf("lam_c", [128, 1], F32)
    lam_r = p.sbuf("lam_r", [1, 128], F32)
    lend = p.sbuf("lend", [128, 1], F32)
    pcol = p.sbuf("pcol", [128, 1], F32)
    cc = p.sbuf("cc", [128, 1], F32)
    dec = p.sbuf("dec", [64, 1], F32)
    sq_s = p.sbuf("sq_s", [128, 64], F32)
    n2 = p.sbuf("n2", [128, 1], F32)
    kn = p.sbuf("kn", [128, 64], BF16)
    kk = p.sbuf("kk", [128, 64], BF16)
    vv = p.sbuf("vv", [128, 64], BF16)
    qf = p.sbuf("qf", [64, 128], BF16)
    vf = p.sbuf("vf", [64, 128], BF16)
    wt_s = p.sbuf("wt_s", [128, 128], F32)
    wbf = p.sbuf("wbf", [128, 128], BF16)
    gw = p.sbuf("gw", [128, 128], BF16)
    gwt = p.sbuf("gwt", [128, 128], BF16)
    y2s = p.sbuf("y2s", [128, 64], F32)
    ybf = p.sbuf("ybf", [128, 64], BF16)
    psL = p.psum("psL", [128, 1], F32)
    psR = p.psum("psR", [1, 128], F32)
    psW = p.psum("psW", [128, 128], F32)
    psG = p.psum("psG", [128, 128], F32)
    psT = p.psum("psT", [64, 128], BF16)
    psY1 = p.psum("psY1", [128, 64], F32)
    psY2 = p.psum("psY2", [128, 64], F32)
    psS2 = p.psum("psS2", [64, 64], F32)

    for ap, sb in ((a_gates, gat), (a_lt, lt_s), (a_ms, ms_s), (a_idf, idf),
                   (a_idb, idb), (a_onesr, onr)):
        p.dma("sync", sb[:, :], ap, ctx.ld)
    if io is not None:
        p.dma("sync", wos[:, :], a_wos, ctx.ld, tag="wos")
    p.barrier("consts")

    last = None
    st_last = None
    st_s1 = None
    prev_s1e = None
    for b in range(B):
        for hl in range(2):
            tq = p.dma("sync", qs[:, :].rearrange("p (i d) -> p i d", i=NCH),
                       a_q[b * SEQ:(b + 1) * SEQ, hl * 64:hl * 64 + 64]
                       .rearrange("(i p) d -> p i d", p=128), ctx.ld,
                       wait=[last] if last else (), tag=f"lq{b}{hl}")
            tk = p.dma("sync", ks[:, :].rearrange("p (i d) -> p i d", i=NCH),
                       a_k[b * SEQ:(b + 1) * SEQ, hl * 64:hl * 64 + 64]
                       .rearrange("(i p) d -> p i d", p=128), ctx.ld, tag=f"lk{b}{hl}")
            tv = p.dma("sync", vs[:, :].rearrange("p (i d) -> p i d", i=NCH),
                       a_v[b * SEQ:(b + 1) * SEQ, hl * 64:hl * 64 + 64]
                       .rearrange("(i p) d -> p i d", p=128), ctx.ld, tag=f"lv{b}{hl}")
            if io is None:
                tkb = p.dma("sync", kbb[:, :], a_kb[hl, b, 0:128, 0:64],
                            ctx.ld, tag=f"lkb{b}{hl}")
            else:
                tk0 = p.dma("sync", kbi[0:64, 0:128],
                            a_kb3[hl:hl + 1, (b * 64) * 128:(b * 64 + 64) * 128]
                            .rearrange("x (i p) -> (x i) p", p=128),
                            ctx.ld, tag=f"lkb{b}{hl}")
                def kbtr(e):
                    return e.transpose(out=psW[:, 0:64], in_=kbi[0:64, 0:128],
                                       identity=idf[0:64, 0:64])
                tk1 = p.emit("tensor", kbtr, wait=[tk0] + ([last] if last else []),
                             inc=[(ctx.mm, 1)], tag=f"kbt{b}{hl}")
                def kbev(e):
                    return e.tensor_copy(out=kbb[:, 0:64], in_=psW[:, 0:64])
                tkb = p.emit("vector", kbev, wait=[tk1], inc=[(ctx.ve, 1)],
                             tag=f"kbe{b}{hl}")
            def srst(e):
                return e.memset(S[:, :], 0.0)
            t0 = p.emit("vector", srst, wait=[tq, tk, tv, tkb], inc=[(ctx.ve, 1)], tag="srst")
            def srst2(e):
                return e.tensor_copy(out=Sbf[:, :], in_=S[:, :])
            last = p.emit("vector", srst2, wait=[t0], inc=[(ctx.ve, 1)], tag="srst2")
            for i in range(NCH):
                def gfn(e, i=i, hl=hl):
                    return e.activation(out=gcol[:, :], in_=kbb[:, i:i + 1], func=AF.Sigmoid,
                                        scale=gat[:, hl:hl + 1], bias=gat[:, 2 + hl:3 + hl])
                t1 = p.emit("scalar", gfn, wait=[last], inc=[(ctx.sc, 1)], tag=f"g{i}")
                def lfn(e, i=i, hl=hl):
                    return e.activation(out=lcol[:, :], in_=kbb[:, i:i + 1], func=AF.Sigmoid,
                                        scale=gat[:, 4 + hl:5 + hl], bias=gat[:, 6 + hl:7 + hl])
                t2 = p.emit("scalar", lfn, wait=[t1], inc=[(ctx.sc, 1)], tag=f"l{i}")
                def rfn(e):
                    return e.tensor_scalar(out=rho[:, :], in0=lcol[:, :], scalar1=-1.0,
                                           scalar2=1.0, op0=ALU.mult, op1=ALU.add)
                t3a = p.emit("vector", rfn, wait=[t2], inc=[(ctx.ve, 1)], tag=f"ra{i}")
                def rfn2(e):
                    return e.tensor_tensor(out=rho[:, :], in0=rho[:, :], in1=gcol[:, :],
                                           op=ALU.mult)
                t3 = p.emit("vector", rfn2, wait=[t3a], inc=[(ctx.ve, 1)], tag=f"r{i}")
                def omfn(e):
                    return e.tensor_scalar(out=omr[:, :], in0=rho[:, :], scalar1=-1.0,
                                           scalar2=1.0, op0=ALU.mult, op1=ALU.add)
                t4 = p.emit("vector", omfn, wait=[t3], inc=[(ctx.ve, 1)], tag=f"om{i}")
                def lnf(e):
                    return e.activation(out=rho[:, :], in_=rho[:, :], func=AF.Ln)
                t5 = p.emit("scalar", lnf, wait=[t4], inc=[(ctx.sc, 1)], tag=f"ln{i}")
                def clf(e):
                    return e.tensor_scalar(out=rho[:, :], in0=rho[:, :], scalar1=-60.0,
                                           scalar2=None, op0=ALU.max)
                t6 = p.emit("vector", clf, wait=[t5], inc=[(ctx.ve, 1)], tag=f"cl{i}")
                def lmm(e):
                    return e.matmul(out=psL[:, 0:1], lhsT=lt_s[:, :], rhs=rho[:, :],
                                    start=True, stop=True)
                t7 = p.emit("tensor", lmm, wait=[t6], inc=[(ctx.mm, 1)], tag=f"lmm{i}")
                def lev(e):
                    return e.tensor_copy(out=lam_c[:, :], in_=psL[:, 0:1])
                t8 = p.emit("vector", lev, wait=[t7], inc=[(ctx.ve, 1)], tag=f"lev{i}")
                def ltr(e):
                    return e.transpose(out=psR[0:1, 0:128], in_=lam_c[:, :], identity=idf[:, :])
                t9 = p.emit("tensor", ltr, wait=[t8], inc=[(ctx.mm, 1)], tag=f"ltr{i}")
                def lrev(e):
                    return e.tensor_copy(out=lam_r[0:1, :], in_=psR[0:1, 0:128])
                t10 = p.emit("vector", lrev, wait=[t9], inc=[(ctx.ve, 1)], tag=f"lrev{i}")
                def msf(e):
                    return e.matmul(out=psW[:, 0:128], lhsT=onr[0:1, :], rhs=lam_r[0:1, :],
                                    start=True, stop=True)
                t11 = p.emit("tensor", msf, wait=[t10], inc=[(ctx.mm, 1)], tag=f"ms{i}")
                def wtf(e):
                    return e.scalar_tensor_tensor(out=wt_s[:, :], in0=psW[:, 0:128],
                                                  scalar=lam_c[:, 0:1], in1=ms_s[:, :],
                                                  op0=ALU.subtract, op1=ALU.subtract)
                t12 = p.emit("vector", wtf, wait=[t11], inc=[(ctx.ve, 1)], tag=f"wt{i}")
                def wef(e):
                    return e.activation(out=wbf[:, :], in_=wt_s[:, :], func=AF.Exp, scale=-1.0)
                t13 = p.emit("scalar", wef, wait=[t12], inc=[(ctx.sc, 1)], tag=f"we{i}")
                def pf(e):
                    return e.activation(out=pcol[:, :], in_=lam_c[:, :], func=AF.Exp)
                t14 = p.emit("scalar", pf, wait=[t8], inc=[(ctx.sc, 1)], tag=f"p{i}")
                def lef(e):
                    return e.matmul(out=psL[:, 0:1], lhsT=onr[0:1, 0:128],
                                    rhs=lam_r[0:1, 127:128], start=True, stop=True)
                t15 = p.emit("tensor", lef, wait=[t10, t8], inc=[(ctx.mm, 1)], tag=f"le{i}")
                def leev(e):
                    return e.tensor_copy(out=lend[:, :], in_=psL[:, 0:1])
                t16 = p.emit("vector", leev, wait=[t15], inc=[(ctx.ve, 1)], tag=f"leev{i}")
                def ccf(e):
                    return e.activation(out=cc[:, :], in_=lam_c[:, :], func=AF.Exp,
                                        scale=-1.0, bias=lend[:, 0:1])
                t17 = p.emit("scalar", ccf, wait=[t16], inc=[(ctx.sc, 1)], tag=f"cc{i}")
                def decf(e):
                    return e.activation(out=dec[:, :], in_=lend[0:64, 0:1], func=AF.Exp)
                t18 = p.emit("scalar", decf, wait=[t16], inc=[(ctx.sc, 1)], tag=f"dec{i}")
                def sqf(e, i=i):
                    return e.square(out=sq_s[:, :], in_=ks[:, i * 64:(i + 1) * 64])
                t19 = p.emit("scalar", sqf, wait=[last], inc=[(ctx.sc, 1)], tag=f"sq{i}")
                def redf(e):
                    return e.tensor_reduce(out=n2[:, :], in_=sq_s[:, :],
                                           axis=mybir.AxisListType.X, op=ALU.add)
                t20 = p.emit("vector", redf, wait=[t19], inc=[(ctx.ve, 1)], tag=f"red{i}")
                def sqrf(e):
                    return e.sqrt(out=n2[:, :], in_=n2[:, :])
                t21 = p.emit("scalar", sqrf, wait=[t20], inc=[(ctx.sc, 1)], tag=f"sqr{i}")
                def rcf(e):
                    return e.tensor_scalar(out=n2[:, :], in0=n2[:, :], scalar1=1e-12,
                                           scalar2=None, op0=ALU.max)
                t22a = p.emit("vector", rcf, wait=[t21], inc=[(ctx.ve, 1)], tag=f"rca{i}")
                def rcf2(e):
                    return e.reciprocal(out=n2[:, :], in_=n2[:, :])
                t22 = p.emit("vector", rcf2, wait=[t22a], inc=[(ctx.ve, 1)], tag=f"rc{i}")
                def knf(e, i=i):
                    return e.tensor_scalar(out=kn[:, :], in0=ks[:, i * 64:(i + 1) * 64],
                                           scalar1=n2[:, 0:1], scalar2=None, op0=ALU.mult)
                t23a = p.emit("vector", knf, wait=[t22, t4], inc=[(ctx.ve, 1)], tag=f"kna{i}")
                def knf2(e):
                    return e.tensor_scalar(out=kk[:, :], in0=kn[:, :],
                                           scalar1=omr[:, 0:1], scalar2=None, op0=ALU.mult)
                t23 = p.emit("vector", knf2, wait=[t23a], inc=[(ctx.ve, 1)], tag=f"kn{i}")
                def vvf(e, i=i):
                    return e.tensor_scalar(out=vv[:, :], in0=vs[:, i * 64:(i + 1) * 64],
                                           scalar1=cc[:, 0:1], scalar2=None, op0=ALU.mult)
                t24 = p.emit("vector", vvf, wait=[t17], inc=[(ctx.ve, 1)], tag=f"vv{i}")
                def tqf(e, i=i):
                    return e.transpose(out=psT[0:64, 0:128], in_=qs[:, i * 64:(i + 1) * 64],
                                       identity=idb[:, :])
                t25 = p.emit("tensor", tqf, wait=[last, t15], inc=[(ctx.mm, 1)], tag=f"tq{i}")
                def tqe(e):
                    return e.tensor_copy(out=qf[:, :], in_=psT[0:64, 0:128])
                t26 = p.emit("vector", tqe, wait=[t25], inc=[(ctx.ve, 1)], tag=f"tqe{i}")
                def tvf(e, i=i):
                    return e.transpose(out=psT[0:64, 0:128], in_=vs[:, i * 64:(i + 1) * 64],
                                       identity=idb[:, :])
                t27 = p.emit("tensor", tvf, wait=[t26], inc=[(ctx.mm, 1)], tag=f"tv{i}")
                def tve(e):
                    return e.tensor_copy(out=vf[:, :], in_=psT[0:64, 0:128])
                t28 = p.emit("vector", tve, wait=[t27], inc=[(ctx.ve, 1)], tag=f"tve{i}")
                def gmf(e):
                    return e.matmul(out=psG[:, 0:128], lhsT=qf[:, :], rhs=vf[:, :],
                                    start=True, stop=True)
                t29 = p.emit("tensor", gmf, wait=[t28], inc=[(ctx.mm, 1)], tag=f"gm{i}")
                def gwf(e):
                    return e.tensor_tensor(out=gw[:, :], in0=psG[:, 0:128], in1=wbf[:, :],
                                           op=ALU.mult)
                t30 = p.emit("vector", gwf, wait=[t29, t13], inc=[(ctx.ve, 1)], tag=f"gw{i}")
                def gtf(e):
                    return e.transpose(out=psG[:, 0:256].bitcast(BF16)[:, 0:128],
                                       in_=gw[:, :], identity=idb[:, :])
                t31 = p.emit("tensor", gtf, wait=[t30], inc=[(ctx.mm, 1)], tag=f"gt{i}")
                def gte(e):
                    return e.tensor_copy(out=gwt[:, :], in_=psG[:, 0:256].bitcast(BF16)[:, 0:128])
                t32 = p.emit("vector", gte, wait=[t31], inc=[(ctx.ve, 1)], tag=f"gte{i}")
                def y1f(e):
                    return e.matmul(out=psY1[:, 0:64], lhsT=gwt[:, :], rhs=kk[:, :],
                                    start=True, stop=True)
                t33 = p.emit("tensor", y1f, wait=[t32, t23], inc=[(ctx.mm, 1)], tag=f"y1{i}")
                def y2f(e):
                    return e.matmul(out=psY2[:, 0:64], lhsT=qf[:, :], rhs=Sbf[:, :],
                                    start=True, stop=True)
                t34 = p.emit("tensor", y2f, wait=[t26], inc=[(ctx.mm, 1)], tag=f"y2{i}")
                def y2e(e):
                    return e.tensor_scalar(out=y2s[:, :], in0=psY2[:, 0:64],
                                           scalar1=pcol[:, 0:1], scalar2=None, op0=ALU.mult)
                t35 = p.emit("vector", y2e, wait=[t34, t14], inc=[(ctx.ve, 1)], tag=f"y2e{i}")
                def yf(e):
                    return e.tensor_tensor(out=ybf[:, :], in0=psY1[:, 0:64], in1=y2s[:, :],
                                           op=ALU.add)
                wyf = [t33, t35] + ([st_last] if st_last else [])
                t36 = p.emit("vector", yf, wait=wyf, inc=[(ctx.ve, 1)], tag=f"y{i}")
                def smf(e):
                    return e.matmul(out=psS2[0:64, 0:64], lhsT=vv[:, :], rhs=kk[:, :],
                                    start=True, stop=True)
                t37 = p.emit("tensor", smf, wait=[t24, t23, t34], inc=[(ctx.mm, 1)], tag=f"sm{i}")
                def suf(e):
                    return e.scalar_tensor_tensor(out=S[:, :], in0=S[:, :],
                                                  scalar=dec[:, 0:1], in1=psS2[0:64, 0:64],
                                                  op0=ALU.mult, op1=ALU.add)
                t38 = p.emit("vector", suf, wait=[t37, t18], inc=[(ctx.ve, 1)], tag=f"su{i}")
                def sbff(e):
                    return e.tensor_copy(out=Sbf[:, :], in_=S[:, :])
                t39 = p.emit("vector", sbff, wait=[t38], inc=[(ctx.ve, 1)], tag=f"sbf{i}")
                if io is None:
                    wst = [t36] + ([st_last] if st_last else [])
                    st_last = p.dma("gpsimd",
                                    o_y[b * SEQ + i * 128: b * SEQ + (i + 1) * 128,
                                        hl * 64:hl * 64 + 64],
                                    ybf[:, :], ctx.st, wait=wst, tag=f"sty{b}{hl}{i}")
                    last = t39
                else:
                    def ytr(e):
                        return e.transpose(out=psT[0:64, 0:128], in_=ybf[:, :],
                                           identity=idb[:, :])
                    tyt = p.emit("tensor", ytr, wait=[t36, t28], inc=[(ctx.mm, 1)],
                                 tag=f"ytr{b}{hl}{i}")
                    def yfme(e, hl=hl, i=i):
                        return e.tensor_copy(out=yfm[hl * 64:(hl + 1) * 64,
                                                     i * 128:(i + 1) * 128],
                                             in_=psT[0:64, 0:128])
                    prev_yev = p.emit("vector", yfme, wait=[tyt], inc=[(ctx.ve, 1)],
                                      tag=f"yfm{b}{hl}{i}")
                    last = prev_yev
                    st_last = prev_yev
                    if hl == 1:
                        for hf in range(2):
                            def s1m(e, i=i, hf=hf):
                                return e.matmul(out=psW[:, 0:512],
                                                lhsT=yfm[:, i * 128:(i + 1) * 128],
                                                rhs=wos[:, hf * 512:(hf + 1) * 512],
                                                start=True, stop=True)
                            w_s1 = ([prev_yev, t12] + ([prev_s1e] if prev_s1e else [])
                                    + ([st_s1] if st_s1 else []))
                            tsm = p.emit("tensor", s1m, wait=w_s1, inc=[(ctx.mm, 1)],
                                         tag=f"s1m{b}{i}{hf}")
                            def s1e(e, hf=hf):
                                return e.tensor_copy(out=s1sl[:, hf * 512:(hf + 1) * 512],
                                                     in_=psW[:, 0:512])
                            prev_s1e = p.emit("vector", s1e, wait=[tsm], inc=[(ctx.ve, 1)],
                                              tag=f"s1e{b}{i}{hf}")
                        rowb = (i // 8) * 2048 + b * 1024 + (i % 8) * 128
                        st_s1 = p.dma("gpsimd", o_slab[rowb:rowb + 128, 0:D],
                                      s1sl[:, :], ctx.st,
                                      wait=[prev_s1e] + ([st_s1] if st_s1 else []),
                                      tag=f"s1s{b}{i}")
                        last = prev_s1e
    if io is not None:
        p.barrier("l1scan_done")
        p.sbuf_release(mkS)
        return
    def fin(e):
        return e.nop()
    p.emit("gpsimd", fin, wait=[st_last], tag="fin")
    res = p.run()
    y = np.zeros((TOKG, D), NBF)
    for c in range(NCORES):
        y[:, 128 * c:128 * c + 128] = res.results[c]["oy"]
    return y



def launch_l1_ffn(xnfm, W, _fused=None):
    if _fused is None:
        p = Prog("l1ffn")
        ctx = Ctx(p)
        io = None
        xin = p.inp("xi", [fm_slice(xnfm, c) for c in range(NCORES)])
    else:
        p, ctx, io = _fused
        mkF = p.sbuf_mark()
        xin = io["xn"]
    a_rw = p.inp("rw", np.ascontiguousarray(W["L1_ffn_router_w"].T))
    a_rb = p.inp("rb", np.ascontiguousarray(W["L1_ffn_router_b"].reshape(1, 16)))
    a_w1 = p.inp("w1", np.ascontiguousarray(W["L1_ffn_w1"].T.astype(NBF)))
    a_b1 = p.inp("b1", np.ascontiguousarray(W["L1_ffn_b1"].reshape(32, 128).T))
    a_w2 = p.inp("w2", np.ascontiguousarray(W["L1_ffn_w2"].T.astype(NBF)))
    a_b2 = p.inp("b2", np.ascontiguousarray(W["L1_ffn_b2"].reshape(8, 128).T))
    a_idf = p.inp("identf", np.eye(128, dtype=np.float32))
    a_onesr = p.inp("onesr", ONES_ROW)
    sel16_np = np.zeros((16, 16 * 128), np.float32)
    for j_ in range(16):
        sel16_np[j_, j_ * 128:(j_ + 1) * 128] = 1.0
    a_sel16 = p.inp("sel16", sel16_np.astype(NBF))
    o = io["o"] if io is not None else p.out("o", [D, TOKC], F32)

    stage = p.sbuf("stage", [128, 8 * 512], F32)
    xnb = p.sbuf("xnb", [128, 8 * TOKC], BF16)
    sel16 = p.sbuf("sel16_s", [16, 16 * 128], BF16)
    rws = p.sbuf("rws", [128, 8 * 16], F32)
    rbb = p.sbuf("rbb", [128, 16], F32)
    idf = p.sbuf("idf", [128, 128], F32)
    onr = p.sbuf("onr", [1, 128], F32)
    lgt = p.sbuf("lgt", [128, 16], F32)
    mx8 = p.sbuf("mx8", [128, 8], F32)
    msk = p.sbuf("msk", [128, 16], F32)
    mskT = p.sbuf("mskT", [16, TOKC], BF16)
    w1b = p.sbuf("w1b", [128, 8 * 4096], BF16)
    w2b = p.sbuf("w2b", [128, 32 * 1024], BF16)
    b1s = p.sbuf("b1s", [128, 32], F32)
    b2s = p.sbuf("b2s", [128, 8], F32)
    h_bf = p.sbuf("h_bf", [128, 32 * 256], BF16)
    psA = p.psum("psA", [128, 512], F32)
    psB = p.psum("psB", [128, 512], F32)
    psC = p.psum("psC", [128, 512], F32)

    t_rw = p.dma("sync", rws[:, :].rearrange("p (k f) -> p k f", k=8),
                 a_rw[0:D, 0:16].rearrange("(k p) f -> p k f", p=128), ctx.ld)
    t_rb = p.dma("sync", stage[0:1, 0:16], a_rb, ctx.ld)
    t_id = p.dma("sync", idf[:, :], a_idf, ctx.ld)
    t_or = p.dma("sync", onr[:, :], a_onesr, ctx.ld)
    t_b1 = p.dma("sync", b1s[:, :], a_b1, ctx.ld)
    t_b2 = p.dma("sync", b2s[:, :], a_b2, ctx.ld)
    t_s16 = p.dma("sync", sel16[:, :], a_sel16, ctx.ld)
    def rbbmm(e):
        return e.matmul(out=psC[:, 0:16], lhsT=onr[0:1, :], rhs=stage[0:1, 0:16],
                        start=True, stop=True)
    t1 = p.emit("tensor", rbbmm, wait=[t_rb, t_or], inc=[(ctx.mm, 1)], tag="rbb")
    def rbbev(e):
        return e.tensor_copy(out=rbb[:, :], in_=psC[:, 0:16])
    t_rbb = p.emit("vector", rbbev, wait=[t1], inc=[(ctx.ve, 1)], tag="rbbev")

    prev = t_rbb
    prev_tr = None
    for n0 in range(0, TOKC, 512):
        t = p.dma("sync", stage[:, :].rearrange("p (k n) -> p k n", k=8),
                  xin[0:D, n0:n0 + 512].rearrange("(k p) n -> p k n", p=128),
                  ctx.ld, wait=[prev], tag="xload")
        def cfn(e, n0=n0):
            return e.tensor_copy(
                out=xnb[:, :].rearrange("p (k n) -> p k n", k=8)[:, :, n0:n0 + 512],
                in_=stage[:, :].rearrange("p (k n) -> p k n", k=8))
        tc = p.emit("vector", cfn, wait=[t], inc=[(ctx.ve, 1)], tag="xcast")
        last = tc
        for tt in range(4):
            ti = n0 // 128 + tt
            def lmm(e, tt=tt):
                ins = None
                for kc in range(8):
                    ins = e.matmul(out=psA[:, 0:16],
                                   lhsT=stage[:, kc * 512 + tt * 128: kc * 512 + tt * 128 + 128],
                                   rhs=rws[:, kc * 16:(kc + 1) * 16],
                                   start=(kc == 0), stop=(kc == 7))
                return ins
            w = [t, t_rw] + ([last] if last else [])
            tm_ = p.emit("tensor", lmm, wait=w, inc=[(ctx.mm, 1)], tag=f"lg{ti}")
            def lev(e):
                return e.tensor_tensor(out=lgt[:, :], in0=psA[:, 0:16], in1=rbb[:, :],
                                       op=ALU.add)
            t2 = p.emit("vector", lev, wait=[tm_], inc=[(ctx.ve, 1)], tag=f"lev{ti}")
            def mxf(e):
                return e.max(out=mx8[:, :], in_=lgt[:, :])
            t3 = p.emit("vector", mxf, wait=[t2], inc=[(ctx.ve, 1)], tag=f"mx{ti}")
            def msf(e):
                return e.tensor_scalar(out=msk[:, :], in0=lgt[:, :],
                                       scalar1=mx8[:, 3:4], scalar2=None, op0=ALU.is_ge)
            t4 = p.emit("vector", msf, wait=[t3], inc=[(ctx.ve, 1)], tag=f"msf{ti}")
            def mtr(e):
                return e.transpose(out=psB[0:16, 0:128], in_=msk[:, :], identity=idf[:, :])
            w = [t4, t_id] + ([prev_tr] if prev_tr else [])
            t5 = p.emit("tensor", mtr, wait=w, inc=[(ctx.mm, 1)], tag=f"mtr{ti}")
            def mte(e, ti=ti):
                return e.tensor_copy(out=mskT[:, ti * 128:(ti + 1) * 128],
                                     in_=psB[0:16, 0:128])
            prev_tr = p.emit("vector", mte, wait=[t5], inc=[(ctx.ve, 1)], tag=f"mte{ti}")
            last = prev_tr
        prev = last
    wt = load_weight_bf16(p, ctx, a_w1, D, 4096, "w1", w1b, stage, prev=prev)
    wt2 = load_weight_bf16(p, ctx, a_w2, 4096, D, "w2", w2b, stage, prev=wt)
    p.barrier("prep")

    osl = stage
    prev_st = None
    prev_h = None
    NS = 256
    for n0 in range(0, TOKC, NS):
        w_h = [wt, t_b1] + ([prev_h] if prev_h else [])
        hts = linear_fm(p, ctx, wsb=w1b, xsb=xnb, osb=None, obf=h_bf, kch=8, fch=32,
                        ntok=NS, psums=[psA, psB], wait=w_h,
                        x_col0=n0, x_stride=TOKC, out_stride=NS,
                        bias=b1s, act=AF.Relu, evac_eng="scalar")
        prev_h = hts[-1]
        prev_mb = None
        lastm = None
        for j in range(16):
            def mbc(e, j=j, n0=n0):
                return e.matmul(out=psC[:, 0:NS], lhsT=sel16[:, j * 128:(j + 1) * 128],
                                rhs=mskT[0:16, n0:n0 + NS], start=True, stop=True)
            w = [hts[-1]] + ([prev_mb] if prev_mb else [])
            t6 = p.emit("tensor", mbc, wait=w, inc=[(ctx.mm, 1)], tag=f"mbc{n0}_{j}")
            def map1(e, j=j):
                return e.tensor_tensor(out=h_bf[:, (2 * j) * NS:(2 * j + 1) * NS],
                                       in0=h_bf[:, (2 * j) * NS:(2 * j + 1) * NS],
                                       in1=psC[:, 0:NS], op=ALU.mult)
            t7 = p.emit("vector", map1, wait=[t6], inc=[(ctx.ve, 1)], tag=f"ma{n0}_{j}a")
            def map2(e, j=j):
                return e.tensor_tensor(out=h_bf[:, (2 * j + 1) * NS:(2 * j + 2) * NS],
                                       in0=h_bf[:, (2 * j + 1) * NS:(2 * j + 2) * NS],
                                       in1=psC[:, 0:NS], op=ALU.mult)
            lastm = p.emit("vector", map2, wait=[t7], inc=[(ctx.ve, 1)], tag=f"ma{n0}_{j}b")
            prev_mb = lastm
        def evac(e, fc, nn0, ns, ps_ap, out_ap):
            return e.tensor_scalar(out=out_ap, in0=ps_ap, scalar1=b2s[:, fc:fc + 1],
                                   scalar2=None, op0=ALU.add)
        w_y = [lastm, wt2, t_b2] + ([prev_st] if prev_st else [])
        linear_fm(p, ctx, wsb=w2b, xsb=h_bf, osb=osl, kch=32, fch=8,
                  ntok=NS, psums=[psA, psB], wait=w_y,
                  x_col0=0, x_stride=NS, out_stride=NS, nslice=NS,
                  extra_evac=evac, evac_eng="vector")
        p.barrier(f"y{n0}")
        prev_st = p.dma("gpsimd", o[0:D, n0:n0 + NS].rearrange("(k p) n -> p k n", p=128),
                        osl[:, 0:8 * NS].rearrange("p (k n) -> p k n", k=8),
                        ctx.st, tag=f"yo{n0}")
    if io is not None:
        p.barrier("l1ffn_done")
        p.sbuf_release(mkF)
        return
    def fin(e):
        return e.nop()
    p.emit("gpsimd", fin, wait=[prev_st], tag="fin")
    res = p.run()
    return fm_unslice([res.results[c]["o"] for c in range(NCORES)])



L2_DEBUG = False


def launch_l2_attn(xnfm, W, _fused=None):
    HN, DH = 16, 64
    SCALE = 1.0 / 8.0
    if _fused is None:
        p = Prog("l2attn")
        ctx = Ctx(p)
        io = None
        xin = p.inp("xi", [fm_slice(xnfm, c) for c in range(NCORES)])
    else:
        p, ctx, io = _fused
        mkA = p.sbuf_mark()
        xin = io["xn"]
    a_wq = p.inp("l2_wq", np.ascontiguousarray(np.asarray(W["L2_attn_Wq"]).astype(NBF)))
    a_wkv = p.inp("l2_wkv", np.ascontiguousarray(np.asarray(W["L2_attn_Wkv"]).astype(NBF)))
    a_wo = p.inp("l2_wo", np.ascontiguousarray(np.asarray(W["L2_attn_Wo"]).astype(NBF)))
    a_g = p.inp("i_g", np.ascontiguousarray(W["L2_attn_ln_gamma"].reshape(8, 128).T))
    a_b = p.inp("i_b", np.ascontiguousarray(W["L2_attn_ln_beta"].reshape(8, 128).T))
    a_idb = p.inp("i_identb", IDENT_BF)
    a_onesd = p.inp("i_onesd", ONES_COL)
    a_onesr = p.inp("i_onesr", ONES_ROW)
    o_s = io["o"] if io is not None else p.out("o", [D, TOKC], F32)
    if L2_DEBUG:
        o_av = p.out("av_o", [128, 8 * TOKC], BF16)
        o_sc = p.out("sc_o", [128, 4 * 16], F32)
        o_tn = p.out("tn_o", [D, TOKC], F32)
        o_mr = p.out("mr_o", [2, TOKC], F32)
    ccol, tcc = load_consts(p, ctx)

    stage = p.sbuf("stage", [128, 8 * 512], F32)
    wqb = p.sbuf("wqb", [128, 8 * D], BF16)
    wkvb = p.sbuf("wkvb", [128, 8 * 2048], BF16)
    wob = p.sbuf("wob", [128, 8 * D], BF16)
    gs = p.sbuf("gs", [128, 8], F32)
    bs = p.sbuf("bs", [128, 8], F32)
    idb = p.sbuf("idb", [128, 128], BF16)
    onesd = p.sbuf("onesd", [128, 1], F32)
    onr = p.sbuf("onr", [1, 128], F32)
    sc4 = p.sbuf("sc4", [128, 4 * 16], F32)
    den = p.sbuf("den", [128, 2 * 16], F32)
    avt = p.sbuf("avt", [128, D], BF16)
    av = p.sbuf("av", [128, 8 * TOKC], BF16)
    mrow = p.sbuf("mrow", [1, TOKC], F32)
    rrow = p.sbuf("rrow", [1, TOKC], F32)
    qt = p.sbuf("qt", [128, D], BF16)
    qt1 = p.sbuf("qt1", [128, D], BF16)
    kt = p.sbuf("kt", [128, D], BF16)
    kt1 = p.sbuf("kt1", [128, D], BF16)
    vt = p.sbuf("vt", [128, D], BF16)
    vt1 = p.sbuf("vt1", [128, D], BF16)
    x2q = p.sbuf("x2q", [128, 2048], BF16)
    x2q1 = p.sbuf("x2q1", [128, 2048], BF16)
    x2k = p.sbuf("x2k", [128, 2048], BF16)
    x2k1 = p.sbuf("x2k1", [128, 2048], BF16)
    zz = p.sbuf("zz", [128, 16 * 131], F32)
    zb = p.sbuf("zb", [128, 2048], F32)
    zu = p.sbuf("zu", [128, 2048], F32)
    ub = zu
    xnbt = p.sbuf("xnbt", [128, 8 * 256], BF16)
    xstg = p.sbuf("xstg", [128, 8 * 256], F32)
    psA = p.psum("psA", [128, 512], F32)
    psB = p.psum("psB", [128, 512], F32)
    psT = p.psum("psT", [128, 1024], BF16)
    ps1 = p.psum("ps1", [1, 512], F32)
    ps2 = p.psum("ps2", [1, 512], F32)
    tn_dram = p.dram("tn_dram", [D, TOKC], F32)

    tld = []
    for ap, sb in ((a_g, gs), (a_b, bs), (a_idb, idb), (a_onesd, onesd), (a_onesr, onr)):
        tld.append(p.dma("sync", sb[:, :], ap, ctx.ld))
    wt = load_weight_bf16(p, ctx, a_wq, D, D, "wq", wqb, stage, prev=None)
    wt = load_weight_bf16(p, ctx, a_wkv, D, 2048, "wkv", wkvb, stage, prev=wt)
    wt = load_weight_bf16(p, ctx, a_wo, D, D, "wo", wob, stage, prev=wt)
    p.barrier("ld")

    prev_blk = None
    for blk in range(8):
        wld = [prev_blk] if prev_blk else []
        tx1 = p.dma("sync", xstg[:, :].rearrange("p (k n) -> p k n", k=8)[:, :, 0:128],
                    xin[0:D, blk * 128:(blk + 1) * 128].rearrange("(k p) n -> p k n", p=128),
                    ctx.ld, wait=wld, tag=f"bx{blk}")
        tx2_ = p.dma("sync", xstg[:, :].rearrange("p (k n) -> p k n", k=8)[:, :, 128:256],
                     xin[0:D, (8 + blk) * 128:(9 + blk) * 128].rearrange("(k p) n -> p k n", p=128),
                     ctx.ld, tag=f"bx2{blk}")
        def bcst(e):
            return e.tensor_copy(out=xnbt[:, :], in_=xstg[:, :])
        tbc = p.emit("vector", bcst, wait=[tx1, tx2_], inc=[(ctx.ve, 1)], tag=f"bc{blk}")
        toks_p = []
        for bi, (qtl, ktl, vtl) in enumerate(((qt, kt, vt), (qt1, kt1, vt1))):
            ti = blk + 8 * bi
            for which, wsb, otl, fo in (("q", wqb, qtl, 0), ("k", wkvb, ktl, 0),
                                        ("v", wkvb, vtl, 1024)):
                for half in range(2):
                    ps = [psA, psB][half]
                    FW = 2048 if which != "q" else 1024
                    def mmfn(e, bi=bi, wsb=wsb, half=half, ps=ps, fo=fo, FW=FW):
                        ins = None
                        for kc in range(8):
                            ins = e.matmul(out=ps[:, 0:512],
                                           lhsT=xnbt[:, kc * 256 + bi * 128: kc * 256 + bi * 128 + 128],
                                           rhs=wsb[:, kc * FW + fo + half * 512: kc * FW + fo + half * 512 + 512],
                                           start=(kc == 0), stop=(kc == 7))
                        return ins
                    w = [tbc] + ([toks_p[-1]] if toks_p else [])
                    tm_ = p.emit("tensor", mmfn, wait=w, inc=[(ctx.mm, 1)], tag=f"p{blk}{bi}{which}{half}")
                    def oev(e, otl=otl, half=half, ps=ps):
                        return e.tensor_copy(out=otl[:, half * 512:(half + 1) * 512],
                                             in_=ps[:, 0:512])
                    toks_p.append(p.emit("vector", oev, wait=[tm_], inc=[(ctx.ve, 1)],
                                         tag=f"pe{blk}{bi}{which}{half}"))
        tp = toks_p[-1]
        tx2 = []
        for src, dst in ((qt, x2q), (qt1, x2q1), (kt, x2k), (kt1, x2k1)):
            def r1(e, src=src, dst=dst):
                return e.activation(
                    out=dst[:, :].rearrange("p (h c) -> p h c", h=16)[:, :, 0:64],
                    in_=src[:, :].rearrange("p (h c) -> p h c", h=16), func=AF.Relu)
            ta = p.emit("scalar", r1, wait=[tp], inc=[(ctx.sc, 1)], tag="r1")
            def r2(e, src=src, dst=dst):
                return e.activation(
                    out=dst[:, :].rearrange("p (h c) -> p h c", h=16)[:, :, 64:128],
                    in_=src[:, :].rearrange("p (h c) -> p h c", h=16), func=AF.Relu,
                    scale=-1.0)
            tx2.append(p.emit("scalar", r2, wait=[ta], inc=[(ctx.sc, 1)], tag="r2"))
        tdp = tx2[-1]
        combos = [(0, x2q, x2k), (1, x2q, x2k1), (2, x2q1, x2k), (3, x2q1, x2k1)]
        tsc = None
        for ci, xa, xb_ in combos:
            def zf(e, xa=xa, xb_=xb_):
                return e.tensor_tensor(out=zb[:, :], in0=xa[:, :], in1=xb_[:, :], op=ALU.mult)
            w = [tdp] + ([tsc] if tsc else [])
            t1 = p.emit("vector", zf, wait=w, inc=[(ctx.ve, 1)], tag=f"z{ci}")
            def zzf(e):
                return e.tensor_copy(
                    out=zz[:, :].rearrange("p (h c) -> p h c", h=16)[:, :, 3:131],
                    in_=zb[:, :].rearrange("p (h c) -> p h c", h=16))
            t2 = p.emit("vector", zzf, wait=[t1], inc=[(ctx.ve, 1)], tag=f"zz{ci}")
            def zzw(e):
                return e.tensor_copy(
                    out=zz[:, :].rearrange("p (h c) -> p h c", h=16)[:, :, 0:3],
                    in_=zb[:, :].rearrange("p (h c) -> p h c", h=16)[:, :, 125:128])
            t3 = p.emit("vector", zzw, wait=[t2], inc=[(ctx.ve, 1)], tag=f"zzw{ci}")
            def uf(e):
                return e.tensor_tensor(
                    out=ub[:, :].rearrange("p (h c) -> p h c", h=16),
                    in0=zz[:, :].rearrange("p (h c) -> p h c", c=131)[:, :, 1:129],
                    in1=zz[:, :].rearrange("p (h c) -> p h c", c=131)[:, :, 2:130],
                    op=ALU.add)
            t4 = p.emit("vector", uf, wait=[t3], inc=[(ctx.ve, 1)], tag=f"u{ci}")
            def uf2(e):
                return e.tensor_tensor(
                    out=ub[:, :].rearrange("p (h c) -> p h c", h=16),
                    in0=ub[:, :].rearrange("p (h c) -> p h c", h=16),
                    in1=zz[:, :].rearrange("p (h c) -> p h c", c=131)[:, :, 0:128],
                    op=ALU.add)
            t5 = p.emit("vector", uf2, wait=[t4], inc=[(ctx.ve, 1)], tag=f"u2{ci}")
            def zuf(e):
                return e.tensor_tensor(out=zb[:, :], in0=zb[:, :], in1=ub[:, :], op=ALU.mult)
            t6 = p.emit("vector", zuf, wait=[t5], inc=[(ctx.ve, 1)], tag=f"zu{ci}")
            def rdf(e, ci=ci):
                return e.tensor_reduce(out=sc4[:, ci * 16:(ci + 1) * 16],
                                       in_=zb[:, :].rearrange("p (h c) -> p h c", h=16),
                                       axis=mybir.AxisListType.X, op=ALU.add)
            tsc = p.emit("vector", rdf, wait=[t6], inc=[(ctx.ve, 1)], tag=f"rd{ci}")
        def dnf(e):
            e.tensor_tensor(out=den[:, 0:16], in0=sc4[:, 0:16], in1=sc4[:, 16:32], op=ALU.add)
            return e.tensor_tensor(out=den[:, 16:32], in0=sc4[:, 32:48], in1=sc4[:, 48:64], op=ALU.add)
        t7 = p.emit("vector", dnf, wait=[tsc], inc=[(ctx.ve, 1)], tag="dn")
        def dnf2(e):
            return e.tensor_scalar(out=den[:, :], in0=den[:, :], scalar1=SCALE,
                                   scalar2=1e-5, op0=ALU.mult, op1=ALU.add)
        t8 = p.emit("vector", dnf2, wait=[t7], inc=[(ctx.ve, 1)], tag="dn2")
        def dnf3(e):
            return e.reciprocal(out=zu[:, 0:32], in_=den[:, :])
        t9a = p.emit("vector", dnf3, wait=[t8], inc=[(ctx.ve, 1)], tag="dn3")
        def dnf4(e):
            return e.tensor_tensor(out=zu[:, 32:64], in0=den[:, :], in1=zu[:, 0:32],
                                   op=ALU.mult)
        t9b = p.emit("vector", dnf4, wait=[t9a], inc=[(ctx.ve, 1)], tag="dn4")
        def dnf5(e):
            return e.tensor_scalar(out=zu[:, 32:64], in0=zu[:, 32:64], scalar1=-1.0,
                                   scalar2=2.0, op0=ALU.mult, op1=ALU.add)
        t9c = p.emit("vector", dnf5, wait=[t9b], inc=[(ctx.ve, 1)], tag="dn5")
        def dnf6(e):
            return e.tensor_tensor(out=den[:, :], in0=zu[:, 0:32], in1=zu[:, 32:64],
                                   op=ALU.mult)
        t9 = p.emit("vector", dnf6, wait=[t9c], inc=[(ctx.ve, 1)], tag="dn6")
        tav_all = []
        for i_, (vts, vts1, sco) in enumerate(((vt, vt1, 0), (vt, vt1, 2))):
            base = sco * 16
            tavh = None
            for h in range(16):
                def av1(e, h=h, base=base, vts=vts):
                    return e.tensor_scalar(out=zu[:, h * 64:(h + 1) * 64],
                                           in0=vts[:, h * 64:(h + 1) * 64],
                                           scalar1=sc4[:, base + h:base + h + 1],
                                           scalar2=None, op0=ALU.mult)
                ta1 = p.emit("vector", av1, wait=[t9] if h == 0 else [], inc=[(ctx.ve, 1)], tag=f"av1{i_}{h}")
                def av2(e, h=h, base=base, vts1=vts1):
                    return e.scalar_tensor_tensor(out=zu[:, h * 64:(h + 1) * 64],
                                                  in0=vts1[:, h * 64:(h + 1) * 64],
                                                  scalar=sc4[:, base + 16 + h:base + 17 + h],
                                                  in1=zu[:, h * 64:(h + 1) * 64],
                                                  op0=ALU.mult, op1=ALU.add)
                ta2 = p.emit("vector", av2, wait=[ta1], inc=[(ctx.ve, 1)], tag=f"av2{i_}{h}")
                def av3(e, h=h, i_=i_):
                    return e.tensor_scalar(out=avt[:, h * 64:(h + 1) * 64],
                                           in0=zu[:, h * 64:(h + 1) * 64],
                                           scalar1=den[:, i_ * 16 + h:i_ * 16 + h + 1],
                                           scalar2=SCALE, op0=ALU.mult, op1=ALU.mult)
                tavh = p.emit("vector", av3, wait=[ta2], inc=[(ctx.ve, 1)], tag=f"av3{i_}{h}")
            ti = blk + 8 * i_
            def trf(e):
                ins = None
                for kc in range(8):
                    ins = e.transpose(out=psT[:, kc * 128:(kc + 1) * 128],
                                      in_=avt[:, kc * 128:kc * 128 + 128],
                                      identity=idb[:, :])
                return ins
            w = [tavh] + ([tav_all[-1]] if tav_all else [])
            ttr = p.emit("tensor", trf, wait=w, inc=[(ctx.mm, 1)], tag=f"avtr{i_}")
            def tev(e, ti=ti):
                return e.tensor_copy(
                    out=av[:, 0:8 * TOKC].rearrange("p (k n) -> p k n", k=8)[:, :, ti * 128:(ti + 1) * 128],
                    in_=psT[:, 0:1024].rearrange("p (k n) -> p k n", k=8))
            tav_all.append(p.emit("vector", tev, wait=[ttr], inc=[(ctx.ve, 1)], tag=f"avev{i_}"))
        prev_blk = tav_all[-1]
    p.barrier("attn")
    if L2_DEBUG:
        p.dma("gpsimd", o_av, av[:, :], ctx.st, tag="dav")
        p.dma("gpsimd", o_sc, sc4[:, :], ctx.st, tag="dsc")
        p.barrier("dbg")
    p.sbuf_free("xstg", "xnbt", "zu", "zb", "zz", "x2k1", "x2k", "x2q1", "x2q",
                "vt1", "vt", "kt1", "kt", "qt1", "qt")
    tn_sl = p.sbuf("tn_sl", [128, 8 * 512], F32)

    prev_st = None
    prev_sq = None
    prev_toks = []
    stats_toks = []
    for n0 in range(0, TOKC, 512):
        w = ([prev_st] if prev_st else []) + ([prev_sq] if prev_sq else [])
        txl = p.dma("sync", stage[:, :].rearrange("p (k n) -> p k n", k=8),
                    xin[0:D, n0:n0 + 512].rearrange("(k p) n -> p k n", p=128),
                    ctx.ld, wait=w, tag=f"xr{n0}")
        toks = []
        for fc in range(8):
            ps = [psA, psB][fc % 2]
            def mmfn(e, fc=fc, n0=n0, ps=ps):
                ins = None
                for kc in range(8):
                    ins = e.matmul(out=ps[:, 0:512],
                                   lhsT=wob[:, kc * D + fc * 128: kc * D + fc * 128 + 128],
                                   rhs=av[:, kc * TOKC + n0: kc * TOKC + n0 + 512],
                                   start=(kc == 0), stop=(kc == 7))
                return ins
            if len(toks) >= 2:
                w = [toks[-2]]
            elif prev_toks:
                w = [prev_toks[-2 + len(toks)]]
            else:
                w = []
            tm_ = p.emit("tensor", mmfn, wait=w, inc=[(ctx.mm, 1)], tag=f"wo{n0}_{fc}")
            def oev(e, fc=fc, ps=ps):
                return e.tensor_tensor(out=tn_sl[:, fc * 512:(fc + 1) * 512],
                                       in0=ps[:, 0:512], in1=stage[:, fc * 512:(fc + 1) * 512],
                                       op=ALU.add)
            toks.append(p.emit("vector", oev, wait=[tm_, txl], inc=[(ctx.ve, 1)], tag=f"woe{n0}_{fc}"))
        def mmsum(e):
            ins = None
            for kc in range(8):
                ins = e.matmul(out=ps1[0:1, 0:512], lhsT=onesd[:, 0:1],
                               rhs=tn_sl[:, kc * 512:(kc + 1) * 512],
                               start=(kc == 0), stop=(kc == 7))
            return ins
        t1 = p.emit("tensor", mmsum, wait=[toks[-1]], inc=[(ctx.mm, 1)], tag=f"lns{n0}")
        sq_toks = []
        for kc in range(8):
            def sqf(e, kc=kc):
                return e.square(out=stage[:, kc * 512:(kc + 1) * 512],
                                in_=tn_sl[:, kc * 512:(kc + 1) * 512])
            sq_toks.append(p.emit("scalar", sqf, wait=[toks[-1]], inc=[(ctx.sc, 1)], tag=f"sq{n0}_{kc}"))
        def mmsq(e):
            ins = None
            for kc in range(8):
                ins = e.matmul(out=ps2[0:1, 0:512], lhsT=onesd[:, 0:1],
                               rhs=stage[:, kc * 512:(kc + 1) * 512],
                               start=(kc == 0), stop=(kc == 7))
            return ins
        t2 = p.emit("tensor", mmsq, wait=[sq_toks[-1]], inc=[(ctx.mm, 1)], tag=f"lnq{n0}")
        prev_sq = t2
        prev_toks = toks
        def vfn(e, n0=n0):
            e.tensor_scalar(out=mrow[0:1, n0:n0 + 512], in0=ps1[0:1, 0:512],
                            scalar1=1.0 / D, scalar2=None, op0=ALU.mult)
            return e.tensor_scalar(out=rrow[0:1, n0:n0 + 512], in0=ps2[0:1, 0:512],
                                   scalar1=1.0 / D, scalar2=None, op0=ALU.mult)
        t3 = p.emit("vector", vfn, wait=[t1, t2], inc=[(ctx.ve, 1)], tag=f"v{n0}")
        p.barrier(f"tn{n0}")
        prev_st = p.dma("gpsimd", tn_dram.ap()[0:D, n0:n0 + 512].rearrange("(k p) n -> p k n", p=128),
                        tn_sl[:, :].rearrange("p (k n) -> p k n", k=8),
                        ctx.st, tag=f"tno{n0}")
        stats_toks.append(t3)
    def varf(e):
        return e.tensor_tensor(out=stage[0:1, 0:TOKC], in0=mrow[0:1, :], in1=mrow[0:1, :],
                               op=ALU.mult)
    t4a = p.emit("vector", varf, wait=[stats_toks[-1]], inc=[(ctx.ve, 1)], tag="vara")
    def varf2(e):
        return e.tensor_tensor(out=rrow[0:1, :], in0=rrow[0:1, :], in1=stage[0:1, 0:TOKC],
                               op=ALU.subtract)
    t4 = p.emit("vector", varf2, wait=[t4a], inc=[(ctx.ve, 1)], tag="var")
    def sqf2(e):
        return e.activation(out=rrow[0:1, :], in_=rrow[0:1, :], func=AF.Sqrt,
                            bias=ccol("eps_ln", 1))
    t5 = p.emit("scalar", sqf2, wait=[t4, tcc], inc=[(ctx.sc, 1)], tag="sqr")
    def rcf(e):
        return e.reciprocal(out=rrow[0:1, :], in_=rrow[0:1, :])
    t6 = p.emit("vector", rcf, wait=[t5], inc=[(ctx.ve, 1)], tag="rc")
    p.barrier("stats")
    if L2_DEBUG:
        p.dma("gpsimd", o_tn, tn_dram.ap()[0:D, 0:TOKC], ctx.st, tag="dtn")
        p.dma("gpsimd", o_mr[0:1, :], mrow[0:1, :], ctx.st, tag="dmr")
        p.dma("gpsimd", o_mr[1:2, :], rrow[0:1, :], ctx.st, tag="drr")
        p.barrier("dbg2")

    prev_o = None
    st = []
    for n0 in range(0, TOKC, 512):
        w = [prev_o] if prev_o else []
        trd = p.dma("sync", tn_sl[:, :].rearrange("p (k n) -> p k n", k=8),
                    tn_dram.ap()[0:D, n0:n0 + 512].rearrange("(k p) n -> p k n", p=128),
                    ctx.ld, wait=w, tag=f"tnr{n0}")
        tmb = ones_bcast_rows(p, ctx, onr, mrow, psA, 512, row_off=n0, wait=[trd])
        trb = ones_bcast_rows(p, ctx, onr, rrow, psB, 512, row_off=n0, wait=[tmb])
        lastc = None
        for kc in range(8):
            def c1(e, kc=kc):
                return e.tensor_tensor(out=stage[:, kc * 512:(kc + 1) * 512],
                                       in0=tn_sl[:, kc * 512:(kc + 1) * 512],
                                       in1=psA[:, 0:512], op=ALU.subtract)
            w2 = [trb] if kc == 0 else []
            tc1 = p.emit("vector", c1, wait=w2, inc=[(ctx.ve, 1)], tag=f"c1{n0}_{kc}")
            def c2(e, kc=kc):
                return e.tensor_tensor(out=stage[:, kc * 512:(kc + 1) * 512],
                                       in0=stage[:, kc * 512:(kc + 1) * 512],
                                       in1=psB[:, 0:512], op=ALU.mult)
            tc2 = p.emit("vector", c2, wait=[tc1], inc=[(ctx.ve, 1)], tag=f"c2{n0}_{kc}")
            def c3(e, kc=kc):
                return e.tensor_scalar(out=stage[:, kc * 512:(kc + 1) * 512],
                                       in0=stage[:, kc * 512:(kc + 1) * 512],
                                       scalar1=gs[:, kc:kc + 1], scalar2=None, op0=ALU.mult)
            tc3 = p.emit("vector", c3, wait=[tc2], inc=[(ctx.ve, 1)], tag=f"c3{n0}_{kc}")
            def c4(e, kc=kc):
                return e.tensor_scalar(out=stage[:, kc * 512:(kc + 1) * 512],
                                       in0=stage[:, kc * 512:(kc + 1) * 512],
                                       scalar1=bs[:, kc:kc + 1], scalar2=None, op0=ALU.add)
            lastc = p.emit("vector", c4, wait=[tc3], inc=[(ctx.ve, 1)], tag=f"c4{n0}_{kc}")
        p.barrier(f"o{n0}")
        prev_o = p.dma("gpsimd", o_s[0:D, n0:n0 + 512].rearrange("(k p) n -> p k n", p=128),
                       stage[:, :].rearrange("p (k n) -> p k n", k=8),
                       ctx.st, tag=f"so{n0}")
        st.append(prev_o)
    if io is not None:
        p.barrier("l2attn_done")
        p.sbuf_release(mkA)
        return
    def fin(e):
        return e.nop()
    p.emit("gpsimd", fin, wait=[st[-1]], tag="fin")
    res = p.run()
    launch_l2_attn.last_res = res
    return fm_unslice([res.results[c]["o"] for c in range(NCORES)])



def launch_l2_parts(xnfm, _fused=None):
    if _fused is None:
        p = Prog("l2parts")
        ctx = Ctx(p)
        io = None
        xin = p.inp("xi", [fm_slice(xnfm, c) for c in range(NCORES)])
        o = p.out("o", [128, 16], F32)
    else:
        p, ctx, io = _fused
        mkP = p.sbuf_mark()
        xin = io["xn"]
        o = io["o"]
    stage = p.sbuf("stage", [128, 8 * 1024], F32)
    acc = p.sbuf("acc", [128, 16], F32)
    prev = None
    toks = []
    for b in range(2):
        t = p.dma("sync", stage[:, :].rearrange("p (k n) -> p k n", k=8),
                  xin[0:D, b * SLOC:(b + 1) * SLOC].rearrange("(k p) n -> p k n", p=128),
                  ctx.ld, wait=[prev] if prev else (), tag=f"x{b}")
        def rfn(e, b=b):
            ins = None
            for kc in range(8):
                ins = e.tensor_reduce(out=acc[:, kc * 2 + b: kc * 2 + b + 1],
                                      in_=stage[:, kc * SLOC:(kc + 1) * SLOC],
                                      axis=mybir.AxisListType.X, op=ALU.add)
            return ins
        prev = p.emit("vector", rfn, wait=[t], inc=[(ctx.ve, 1)], tag=f"r{b}")
        toks.append(prev)
    tst = p.dma("gpsimd", o, acc[:, :], ctx.st, wait=[toks[-1]], tag="st")
    if io is not None:
        p.barrier("l2parts_done")
        p.sbuf_release(mkP)
        return
    def fin(e):
        return e.nop()
    p.emit("gpsimd", fin, wait=[tst], tag="fin")
    res = p.run()
    return np.stack([res.results[c]["o"] for c in range(NCORES)], 0)


def launch_l2_ffn(xnfm, parts, W, _fused=None):
    if _fused is None:
        p = Prog("l2ffn")
        ctx = Ctx(p)
        io = None
        xin = p.inp("xi", [fm_slice(xnfm, c) for c in range(NCORES)])
        a_parts = p.inp("i_parts", np.ascontiguousarray(parts))
        ap_parts = a_parts[0:NCORES, 0:128, 0:16].rearrange("c p n -> p c n")
    else:
        p, ctx, io = _fused
        mkF2 = p.sbuf_mark()
        xin = io["xn"]
        ap_parts = io["partsag"][0:NCORES * 128, 0:16].rearrange("(c p) n -> p c n", p=128)
    ws = {}
    for nm in ("cn_w1", "cn_w2", "cn_w3", "gc_w1", "gc_w2"):
        ws[nm] = p.inp("i_" + nm, np.ascontiguousarray(
            np.asarray(W["L2_ffn_" + nm]).astype(NBF)))
    a_w4 = p.inp("i_cn_w4", np.ascontiguousarray(W["L2_ffn_cn_w4"].reshape(2, 128).T))
    a_g3 = p.inp("i_gc_w3", np.ascontiguousarray(W["L2_ffn_gc_w3"].reshape(2, 128).T))
    bs_in = {}
    for nm, sz in (("cn_b1", 1024), ("cn_b2", 512), ("cn_b3", 256),
                   ("gc_b1", 512), ("gc_b2", 256)):
        bs_in[nm] = p.inp("i_" + nm, np.ascontiguousarray(
            W["L2_ffn_" + nm].reshape(sz // 128, 128).T))
    a_b4 = p.inp("i_cn_b4", np.ascontiguousarray(W["L2_ffn_cn_b4"].reshape(1, 1)))
    a_gb3 = p.inp("i_gc_b3", np.ascontiguousarray(W["L2_ffn_gc_b3"].reshape(1, 1)))
    a_ar = p.inp("i_ar", np.ascontiguousarray(W["L2_ffn_adaptive_range"].reshape(1, 1)))
    a_onesr = p.inp("i_onesr", ONES_ROW)
    o = io["o"] if io is not None else p.out("o", [D, TOKC], F32)

    stage = p.sbuf("stage", [128, 8 * 512], F32)
    outt = p.sbuf("outt", [128, 8 * TOKC], F32)
    outb = p.sbuf("outb", [128, 8 * TOKC], BF16)
    w1b = p.sbuf("w1b", [128, 8 * 1024], BF16)
    w2b = p.sbuf("w2b", [128, 8 * 512], BF16)
    w3b = p.sbuf("w3b", [128, 4 * 256], BF16)
    w4b = p.sbuf("w4b", [128, 2 * 1], BF16)
    g1b = p.sbuf("g1b", [128, 8 * 512], BF16)
    g2b = p.sbuf("g2b", [128, 4 * 256], BF16)
    g3b = p.sbuf("g3b", [128, 2 * 1], BF16)
    b1c = p.sbuf("b1c", [128, 8], F32)
    b2c = p.sbuf("b2c", [128, 4], F32)
    b3c = p.sbuf("b3c", [128, 2], F32)
    gb1c = p.sbuf("gb1c", [128, 4], F32)
    gb2c = p.sbuf("gb2c", [128, 2], F32)
    b4s = p.sbuf("b4s", [1, 1], F32)
    gb3s = p.sbuf("gb3s", [1, 1], F32)
    ars = p.sbuf("ars", [1, 1], F32)
    onr = p.sbuf("onr", [1, 128], F32)
    prt = p.sbuf("prt", [128, 16], F32)
    gcv = p.sbuf("gcv", [128, 16], BF16)
    h1g = p.sbuf("h1g", [128, 4 * 2], BF16)
    h2g = p.sbuf("h2g", [128, 2 * 2], BF16)
    gfrow = p.sbuf("gfrow", [1, 2], F32)
    h1b = p.sbuf("h1b", [128, 8 * 512], BF16)
    h2b = p.sbuf("h2b", [128, 4 * 512], BF16)
    h3b = p.sbuf("h3b", [128, 2 * 512], BF16)
    tfrow = p.sbuf("tfrow", [1, TOKC], F32)
    crow = p.sbuf("crow", [1, TOKC], F32)
    tmp4 = p.sbuf("tmp4", [128, 4 * 512], F32)
    scr = p.sbuf("scr", [128, 512], F32)
    psA = p.psum("psA", [128, 512], F32)
    psB = p.psum("psB", [128, 512], F32)
    psC = p.psum("psC", [128, 512], F32)
    psR = p.psum("psR", [1, 512], F32)

    tld = []
    for ap, sb in ((bs_in["cn_b1"], b1c), (bs_in["cn_b2"], b2c), (bs_in["cn_b3"], b3c),
                   (bs_in["gc_b1"], gb1c), (bs_in["gc_b2"], gb2c),
                   (a_b4, b4s), (a_gb3, gb3s), (a_ar, ars), (a_onesr, onr)):
        if sb is None:
            continue
        tld.append(p.dma("sync", sb[:, :], ap, ctx.ld))
    tpr = p.dma("sync", stage[:, 0:NCORES * 16].rearrange("p (c n) -> p c n", c=NCORES),
                ap_parts, ctx.ld)
    def sump(e):
        ins = None
        for c in range(1, NCORES):
            ins = e.tensor_tensor(
                out=stage[:, 0:16] if c == 1 else stage[:, 0:16],
                in0=stage[:, 0:16] if c > 1 else stage[:, 0:16],
                in1=stage[:, c * 16:(c + 1) * 16], op=ALU.add)
        return ins
    t1 = p.emit("vector", sump, wait=[tpr], inc=[(ctx.ve, 1)], tag="sump")
    def gcf(e):
        return e.tensor_scalar(out=prt[:, :], in0=stage[:, 0:16], scalar1=1.0 / SEQ,
                               scalar2=None, op0=ALU.mult)
    t2 = p.emit("vector", gcf, wait=[t1], inc=[(ctx.ve, 1)], tag="gc")
    def gcb(e):
        return e.tensor_copy(out=gcv[:, :], in_=prt[:, :])
    t3 = p.emit("vector", gcb, wait=[t2], inc=[(ctx.ve, 1)], tag="gcb")

    tok = t3
    for n0 in range(0, TOKC, 512):
        t = p.dma("sync", stage[:, :].rearrange("p (k n) -> p k n", k=8),
                  xin[0:D, n0:n0 + 512].rearrange("(k p) n -> p k n", p=128),
                  ctx.ld, wait=[tok], tag="xload")
        def cfn(e, n0=n0):
            return e.tensor_copy(
                out=outt[:, :].rearrange("p (k n) -> p k n", k=8)[:, :, n0:n0 + 512],
                in_=stage[:, :].rearrange("p (k n) -> p k n", k=8))
        tok = p.emit("vector", cfn, wait=[t], inc=[(ctx.ve, 1)], tag="xcast")
    wt = load_weight_bf16(p, ctx, ws["cn_w1"], D, D, "w1", w1b, stage, prev=tok)
    wt = load_weight_bf16(p, ctx, ws["cn_w2"], D, 512, "w2", w2b, stage, prev=wt)
    wt = load_weight_bf16(p, ctx, ws["cn_w3"], 512, 256, "w3", w3b, stage, prev=wt)
    wt = load_weight_bf16(p, ctx, ws["gc_w1"], D, 512, "g1", g1b, stage, prev=wt)
    wt = load_weight_bf16(p, ctx, ws["gc_w2"], 512, 256, "g2", g2b, stage, prev=wt)
    t_w4 = p.dma("sync", stage[:, 0:2], a_w4, ctx.ld, wait=[wt], tag="w4l")
    def w4c(e):
        e.tensor_copy(out=w4b[:, :], in_=stage[:, 0:2])
        return e.tensor_copy(out=g3b[:, :], in_=stage[:, 2:4])
    t_g3 = p.dma("sync", stage[:, 2:4], a_g3, ctx.ld, wait=[t_w4], tag="g3l")
    wt = p.emit("vector", w4c, wait=[t_g3], inc=[(ctx.ve, 1)], tag="w4c")
    p.barrier("ld")

    def gmm1(e):
        ins = None
        for fc in range(4):
            for kc in range(8):
                ins = e.matmul(out=psA[:, fc * 2:fc * 2 + 2],
                               lhsT=g1b[:, kc * 512 + fc * 128: kc * 512 + fc * 128 + 128],
                               rhs=gcv[:, kc * 2:kc * 2 + 2],
                               start=(kc == 0), stop=(kc == 7))
        return ins
    t4 = p.emit("tensor", gmm1, inc=[(ctx.mm, 1)], tag="gmm1")
    def gev1(e):
        ins = None
        for fc in range(4):
            ins = e.activation(out=h1g[:, fc * 2:fc * 2 + 2], in_=psA[:, fc * 2:fc * 2 + 2],
                               func=AF.Gelu, bias=gb1c[:, fc:fc + 1])
        return ins
    t5 = p.emit("scalar", gev1, wait=[t4], inc=[(ctx.sc, 1)], tag="gev1")
    def gmm2(e):
        ins = None
        for fc in range(2):
            for kc in range(4):
                ins = e.matmul(out=psB[:, fc * 2:fc * 2 + 2],
                               lhsT=g2b[:, kc * 256 + fc * 128: kc * 256 + fc * 128 + 128],
                               rhs=h1g[:, kc * 2:kc * 2 + 2],
                               start=(kc == 0), stop=(kc == 3))
        return ins
    t6 = p.emit("tensor", gmm2, wait=[t5], inc=[(ctx.mm, 1)], tag="gmm2")
    def gev2(e):
        ins = None
        for fc in range(2):
            ins = e.activation(out=h2g[:, fc * 2:fc * 2 + 2], in_=psB[:, fc * 2:fc * 2 + 2],
                               func=AF.Gelu, bias=gb2c[:, fc:fc + 1])
        return ins
    t7 = p.emit("scalar", gev2, wait=[t6], inc=[(ctx.sc, 1)], tag="gev2")
    def gmm3(e):
        ins = None
        for kc in range(2):
            ins = e.matmul(out=psR[0:1, 0:2],
                           lhsT=g3b[:, kc:kc + 1],
                           rhs=h2g[:, kc * 2:kc * 2 + 2],
                           start=(kc == 0), stop=(kc == 1))
        return ins
    t8 = p.emit("tensor", gmm3, wait=[t7], inc=[(ctx.mm, 1)], tag="gmm3")
    def gev3(e):
        return e.activation(out=gfrow[0:1, :], in_=psR[0:1, 0:2], func=AF.Sigmoid,
                            bias=gb3s[0:1, 0:1])
    t9 = p.emit("scalar", gev3, wait=[t8], inc=[(ctx.sc, 1)], tag="gev3")
    def gf7(e):
        return e.tensor_scalar(out=gfrow[0:1, :], in0=gfrow[0:1, :], scalar1=0.7,
                               scalar2=None, op0=ALU.mult)
    t10 = p.emit("vector", gf7, wait=[t9], inc=[(ctx.ve, 1)], tag="gf7")

    prev_cy = t10
    for cyc in range(3):
        cst = None
        for n0 in range(0, TOKC, 2048):
            def cf2(e, n0=n0):
                return e.tensor_copy(
                    out=outb[:, :].rearrange("p (k n) -> p k n", k=8)[:, :, n0:n0 + 2048],
                    in_=outt[:, :].rearrange("p (k n) -> p k n", k=8)[:, :, n0:n0 + 2048])
            cst = p.emit("vector", cf2, wait=[prev_cy] if n0 == 0 else [], inc=[(ctx.ve, 1)],
                         tag=f"cb{cyc}_{n0}")
        prev_h = cst
        for n0 in range(0, TOKC, 512):
            h1t = linear_fm(p, ctx, wsb=w1b, xsb=outb, osb=None, obf=h1b, kch=8, fch=8,
                            ntok=512, psums=[psA, psB], wait=[prev_h],
                            x_col0=n0, x_stride=TOKC, out_stride=512,
                            bias=b1c, act=AF.Gelu, evac_eng="scalar")
            h2t = linear_fm(p, ctx, wsb=w2b, xsb=h1b, osb=None, obf=h2b, kch=8, fch=4,
                            ntok=512, psums=[psA, psB], wait=[h1t[-1]],
                            x_col0=0, x_stride=512, out_stride=512,
                            bias=b2c, act=AF.Gelu, evac_eng="scalar")
            h3t = linear_fm(p, ctx, wsb=w3b, xsb=h2b, osb=None, obf=h3b, kch=4, fch=2,
                            ntok=512, psums=[psA, psB], wait=[h2t[-1]],
                            x_col0=0, x_stride=512, out_stride=512,
                            bias=b3c, act=AF.Gelu, evac_eng="scalar")
            def tfm(e):
                ins = None
                for kc in range(2):
                    ins = e.matmul(out=psR[0:1, 0:512],
                                   lhsT=w4b[:, kc:kc + 1], rhs=h3b[:, kc * 512:(kc + 1) * 512],
                                   start=(kc == 0), stop=(kc == 1))
                return ins
            tf1 = p.emit("tensor", tfm, wait=[h3t[-1]], inc=[(ctx.mm, 1)], tag=f"tf{cyc}_{n0}")
            def tfe(e, n0=n0):
                return e.activation(out=tfrow[0:1, n0:n0 + 512], in_=psR[0:1, 0:512],
                                    func=AF.Sigmoid, bias=b4s[0:1, 0:1])
            prev_h = p.emit("scalar", tfe, wait=[tf1], inc=[(ctx.sc, 1)], tag=f"tfe{cyc}_{n0}")
        def cr1(e):
            return e.tensor_scalar(out=crow[0:1, :], in0=tfrow[0:1, :], scalar1=0.3,
                                   scalar2=None, op0=ALU.mult)
        c1 = p.emit("vector", cr1, wait=[prev_h], inc=[(ctx.ve, 1)], tag=f"cr1{cyc}")
        def cr2(e):
            e.tensor_scalar(out=crow[0:1, 0:SLOC], in0=crow[0:1, 0:SLOC],
                            scalar1=gfrow[0:1, 0:1], scalar2=None, op0=ALU.add)
            return e.tensor_scalar(out=crow[0:1, SLOC:TOKC], in0=crow[0:1, SLOC:TOKC],
                                   scalar1=gfrow[0:1, 1:2], scalar2=None, op0=ALU.add)
        c2 = p.emit("vector", cr2, wait=[c1], inc=[(ctx.ve, 1)], tag=f"cr2{cyc}")
        def cr3(e):
            return e.tensor_scalar(out=crow[0:1, :], in0=crow[0:1, :], scalar1=-0.5,
                                   scalar2=None, op0=ALU.add)
        c3 = p.emit("vector", cr3, wait=[c2], inc=[(ctx.ve, 1)], tag=f"cr3{cyc}")
        def cr4(e):
            return e.tensor_scalar(out=crow[0:1, :], in0=crow[0:1, :],
                                   scalar1=ars[0:1, 0:1], scalar2=2.0,
                                   op0=ALU.mult, op1=ALU.mult)
        c4 = p.emit("vector", cr4, wait=[c3], inc=[(ctx.ve, 1)], tag=f"cr4{cyc}")
        def cr5(e):
            return e.tensor_scalar(out=crow[0:1, :], in0=crow[0:1, :], scalar1=0.1,
                                   scalar2=None, op0=ALU.add)
        c5 = p.emit("vector", cr5, wait=[c4], inc=[(ctx.ve, 1)], tag=f"cr5{cyc}")
        prev_u = c5
        for n0 in range(0, TOKC, 512):
            def cbm(e, n0=n0):
                return e.matmul(out=psC[:, 0:512], lhsT=onr[0:1, :],
                                rhs=crow[0:1, n0:n0 + 512], start=True, stop=True)
            tb = p.emit("tensor", cbm, wait=[prev_u], inc=[(ctx.mm, 1)], tag=f"cb{cyc}_{n0}")
            def sv(e, n0=n0):
                ins = None
                for kc in range(4):
                    ins = e.tensor_copy(out=tmp4[:, kc * 512:(kc + 1) * 512],
                                        in_=outt[:, kc * TOKC + n0: kc * TOKC + n0 + 512])
                return ins
            ts_ = p.emit("vector", sv, wait=[tb], inc=[(ctx.ve, 1)], tag=f"sv{cyc}_{n0}")
            upds = [(0, 4, 1.0, "out"), (1, 5, 1.0, "out"), (2, 6, -1.0, "out"),
                    (3, 7, -1.0, "out"), (4, 0, -1.0, "tmp"), (5, 1, -1.0, "tmp"),
                    (6, 2, 1.0, "tmp"), (7, 3, 1.0, "tmp")]
            lastu = ts_
            for dst, src, sgn, where in upds:
                def u1(e, dst=dst, src=src, where=where, n0=n0):
                    srcap = (outt[:, src * TOKC + n0: src * TOKC + n0 + 512]
                             if where == "out" else
                             tmp4[:, (src % 4) * 512:((src % 4) + 1) * 512])
                    return e.tensor_tensor(out=scr[:, :], in0=srcap, in1=psC[:, 0:512],
                                           op=ALU.mult)
                tu1 = p.emit("vector", u1, wait=[lastu], inc=[(ctx.ve, 1)], tag=f"u1{cyc}_{n0}_{dst}")
                def u2(e, dst=dst, sgn=sgn, n0=n0):
                    return e.tensor_tensor(out=outt[:, dst * TOKC + n0: dst * TOKC + n0 + 512],
                                           in0=outt[:, dst * TOKC + n0: dst * TOKC + n0 + 512],
                                           in1=scr[:, :],
                                           op=ALU.add if sgn > 0 else ALU.subtract)
                lastu = p.emit("vector", u2, wait=[tu1], inc=[(ctx.ve, 1)], tag=f"u2{cyc}_{n0}_{dst}")
            prev_u = lastu
        prev_cy = prev_u
    p.barrier("cycles")
    st = []
    prev_o = None
    for n0 in range(0, TOKC, 512):
        prev_o = p.dma("gpsimd", o[0:D, n0:n0 + 512].rearrange("(k p) n -> p k n", p=128),
                       outt[:, :].rearrange("p (k n) -> p k n", k=8)[:, :, n0:n0 + 512],
                       ctx.st, wait=[prev_o] if prev_o else (), tag=f"oo{n0}")
        st.append(prev_o)
    if io is not None:
        p.barrier("l2ffn_done")
        p.sbuf_release(mkF2)
        return
    def fin(e):
        return e.nop()
    p.emit("gpsimd", fin, wait=[st[-1]], tag="fin")
    res = p.run()
    return fm_unslice([res.results[c]["o"] for c in range(NCORES)])



L3_DEBUG = False


def launch_l3_pre(xnfm, W):
    p = Prog("l3pre")
    ctx = Ctx(p)
    xin = p.inp("xi", [fm_slice(xnfm, c) for c in range(NCORES)])
    a_w = p.inp("i_win", W["L3_attn_W_in"])
    a_g = p.inp("i_g", np.ascontiguousarray(W["L3_attn_ln_gamma"].reshape(8, 128).T))
    a_b = p.inp("i_b", np.ascontiguousarray(W["L3_attn_ln_beta"].reshape(8, 128).T))
    a_onesd = p.inp("i_onesd", ONES_COL)
    a_onesr = p.inp("i_onesr", ONES_ROW)
    o_u = p.out("ou", [2048, TOKC], BF16)
    o_z = p.out("oz", [2048, TOKC], BF16)
    if L3_DEBUG:
        o_xnb = p.out("xnb_o", [128, 8 * TOKC], BF16)
        o_mr = p.out("mr_o", [2, TOKC], F32)
    ccol, tcc = load_consts(p, ctx)

    stage = p.sbuf("stage", [128, 8 * 512], F32)
    gs = p.sbuf("gs", [128, 8], F32)
    bs = p.sbuf("bs", [128, 8], F32)
    onesd = p.sbuf("onesd", [128, 1], F32)
    onr = p.sbuf("onr", [1, 128], F32)
    mrow = p.sbuf("mrow", [1, TOKC], F32)
    rrow = p.sbuf("rrow", [1, TOKC], F32)
    sq = stage
    sc2 = p.sbuf("sc2", [128, 512], F32)
    osl = p.sbuf("osl", [128, 16 * 512], BF16)
    ps_sum = p.psum("ps_sum", [1, 512], F32)
    ps_sq = p.psum("ps_sq", [1, 512], F32)
    psA = p.psum("psA", [128, 512], F32)
    psB = p.psum("psB", [128, 512], F32)
    xnb = p.sbuf("xnb", [128, 8 * TOKC], BF16)
    xs = p.sbuf("xs", [128, 8 * TOKC], F32)

    for ap, sb in ((a_g, gs), (a_b, bs), (a_onesd, onesd), (a_onesr, onr)):
        p.dma("sync", sb[:, :], ap, ctx.ld)
    tok = None
    for n0 in range(0, TOKC, 512):
        t = p.dma("sync", stage[:, :].rearrange("p (k n) -> p k n", k=8),
                  xin[0:D, n0:n0 + 512].rearrange("(k p) n -> p k n", p=128),
                  ctx.ld, wait=[tok] if tok else (), tag="xload")
        def cfn(e, n0=n0):
            return e.tensor_copy(
                out=xs[:, :].rearrange("p (k n) -> p k n", k=8)[:, :, n0:n0 + 512],
                in_=stage[:, :].rearrange("p (k n) -> p k n", k=8))
        tok = p.emit("vector", cfn, wait=[t], inc=[(ctx.ve, 1)], tag="xcast")
    tx = tok
    st = ln_stats(p, ctx, xs, TOKC, onesd, mrow, rrow, ps_sum, ps_sq, sq,
                  wait=[tx, tcc], eps_ap=ccol("eps_ln", 1))
    prevA = None
    for n0 in range(0, TOKC, 512):
        wA = [st[-1]] + ([prevA] if prevA else [])
        tA = ones_bcast_rows(p, ctx, onr, mrow, psA, 512, row_off=n0, wait=wA)
        tB = ones_bcast_rows(p, ctx, onr, rrow, psB, 512, row_off=n0, wait=[tA])
        lastk = None
        for kc in range(8):
            def c1(e, kc=kc, n0=n0):
                return e.tensor_tensor(out=sc2[:, :],
                                       in0=xs[:, kc * TOKC + n0: kc * TOKC + n0 + 512],
                                       in1=psA[:, 0:512], op=ALU.subtract)
            w = [tB] if kc == 0 else []
            t1 = p.emit("vector", c1, wait=w, inc=[(ctx.ve, 1)], tag=f"a{n0}_{kc}")
            def c2(e, kc=kc):
                return e.tensor_tensor(out=sc2[:, :], in0=sc2[:, :], in1=psB[:, 0:512],
                                       op=ALU.mult)
            t2 = p.emit("vector", c2, wait=[t1], inc=[(ctx.ve, 1)], tag=f"b{n0}_{kc}")
            def c3(e, kc=kc, n0=n0):
                return e.tensor_scalar(out=xnb[:, kc * TOKC + n0: kc * TOKC + n0 + 512],
                                       in0=sc2[:, :], scalar1=gs[:, kc:kc + 1],
                                       scalar2=None, op0=ALU.mult)
            t3 = p.emit("vector", c3, wait=[t2], inc=[(ctx.ve, 1)], tag=f"c{n0}_{kc}")
            def c4(e, kc=kc, n0=n0):
                return e.tensor_scalar(out=xnb[:, kc * TOKC + n0: kc * TOKC + n0 + 512],
                                       in0=xnb[:, kc * TOKC + n0: kc * TOKC + n0 + 512],
                                       scalar1=bs[:, kc:kc + 1], scalar2=None, op0=ALU.add)
            lastk = p.emit("vector", c4, wait=[t3], inc=[(ctx.ve, 1)], tag=f"d{n0}_{kc}")
        prevA = lastk
    p.barrier("applydone")
    if L3_DEBUG:
        p.dma("gpsimd", o_xnb, xnb[:, :], ctx.st, tag="dxnb")
        p.dma("gpsimd", o_mr[0:1, :], mrow[0:1, :], ctx.st, tag="dmr")
        p.dma("gpsimd", o_mr[1:2, :], rrow[0:1, :], ctx.st, tag="drr")
        p.barrier("dbg")
    p.sbuf_free("xs")
    wb = p.sbuf("wb", [128, 8 * 4096], BF16)
    wt = load_weight_bf16(p, ctx, a_w, D, 4096, "win", wb, stage, prev=prevA)
    p.barrier("prep")

    prev_st = None
    for n0 in range(0, TOKC, 512):
        def evac_u(e, fc, nn0, ns, ps_ap, out_ap):
            return e.copy(out=out_ap, in_=ps_ap)
        w = [wt] + ([prev_st] if prev_st else [])
        linear_fm(p, ctx, wsb=wb, xsb=xnb, osb=None, obf=osl, kch=8, fch=16,
                  ntok=512, psums=[psA, psB], wait=w,
                  x_col0=n0, x_stride=TOKC, out_stride=512,
                  w_stride=4096, w_col0=0,
                  extra_evac=evac_u, evac_eng="scalar")
        p.barrier(f"u{n0}")
        tu = p.dma("gpsimd", o_u[0:2048, n0:n0 + 512].rearrange("(k p) n -> p k n", p=128),
                   osl[:, :].rearrange("p (k n) -> p k n", k=16),
                   ctx.st, tag=f"uo{n0}")
        def evac_z(e, fc, nn0, ns, ps_ap, out_ap):
            return e.activation(out=out_ap, in_=ps_ap, func=AF.Silu)
        linear_fm(p, ctx, wsb=wb, xsb=xnb, osb=None, obf=osl, kch=8, fch=16,
                  ntok=512, psums=[psA, psB], wait=[tu],
                  x_col0=n0, x_stride=TOKC, out_stride=512, out_col0=0,
                  w_stride=4096, w_col0=2048,
                  extra_evac=evac_z, evac_eng="scalar")
        p.barrier(f"z{n0}")
        prev_st = p.dma("gpsimd", o_z[0:2048, n0:n0 + 512].rearrange("(k p) n -> p k n", p=128),
                        osl[:, :].rearrange("p (k n) -> p k n", k=16),
                        ctx.st, tag=f"zo{n0}")
    def fin(e):
        return e.nop()
    p.emit("gpsimd", fin, wait=[prev_st], tag="fin")
    res = p.run()
    launch_l3_pre.last_res = res
    u = fm_unslice([res.results[c]["ou"] for c in range(NCORES)])
    zz_ = fm_unslice([res.results[c]["oz"] for c in range(NCORES)])
    return u, zz_


def launch_l3_mid(u_raw, W):
    p = Prog("l3mid")
    ctx = Ctx(p)
    def u_halo(c):
        cols = []
        for b in range(B):
            base = b * SEQ
            lo = c * SLOC - 3
            idx = np.arange(lo, (c + 1) * SLOC)
            seg = np.zeros((2048, SLOC + 3), NBF)
            valid = idx >= 0
            seg[:, valid] = u_raw[:, base + idx[valid]]
            cols.append(seg)
        return np.ascontiguousarray(np.concatenate(cols, 1))
    a_u = p.inp("i_u", [u_halo(c) for c in range(NCORES)])
    a_cw = p.inp("i_cw", np.ascontiguousarray(
        W["L3_attn_conv_w"].reshape(16, 128, 4).transpose(1, 0, 2).reshape(128, 64)))
    a_cb = p.inp("i_cb", np.ascontiguousarray(W["L3_attn_conv_b"].reshape(16, 128).T))
    a_wxp = p.inp("i_wxp", W["L3_attn_W_xproj"])
    a_wdt = p.inp("i_wdt", W["L3_attn_W_dt"])
    a_bdt = p.inp("i_bdt", np.ascontiguousarray(W["L3_attn_b_dt"].reshape(16, 128).T))
    o_uc = p.out("ouc", [2048, TOKC], BF16)
    o_dl = p.out("odl", [2048, TOKC], BF16)
    o_bc = p.out("obc", [32, TOKC], F32)
    ccol, tcc = load_consts(p, ctx)
    NH3 = SLOC + 3

    us = p.sbuf("us", [128, 16 * 2 * NH3], BF16)
    cw = p.sbuf("cw", [128, 64], F32)
    cb = p.sbuf("cb", [128, 16], F32)
    wxp = p.sbuf("wxp", [128, 16 * 96], BF16)
    wdt = p.sbuf("wdt", [64, 2048], BF16)
    bdt = p.sbuf("bdt", [128, 16], F32)
    stage = p.sbuf("stage", [128, 8 * 512], F32)
    acc = p.sbuf("acc", [128, 1024], F32)
    ucb = p.sbuf("ucb", [128, 16 * TOKC], BF16)
    xdbl = p.sbuf("xdbl", [96, TOKC], F32)
    dtb = p.sbuf("dtb", [64, TOKC], BF16)
    raws = p.sbuf("raws", [128, 512], F32)
    expt = p.sbuf("expt", [128, 512], F32)
    lnt = p.sbuf("lnt", [128, 512], F32)
    msk = p.sbuf("msk", [128, 512], U8)
    dsl = p.sbuf("dsl", [128, 512], BF16)
    psA = p.psum("psA", [128, 512], F32)
    psB = p.psum("psB", [128, 512], F32)
    psX = p.psum("psX", [96, 512], F32)

    t_u = p.dma("sync", us[:, :].rearrange("p (k n) -> p k n", k=16),
                a_u[0:2048, 0:2 * NH3].rearrange("(k p) n -> p k n", p=128), ctx.ld)
    for ap, sb in ((a_cw, cw), (a_cb, cb), (a_bdt, bdt)):
        p.dma("sync", sb[:, :], ap, ctx.ld)
    wt = load_weight_bf16(p, ctx, a_wxp, 2048, 96, "wxp", wxp, stage, prev=None)
    t_wd = p.dma("sync", stage[0:64, 0:2048], a_wdt, ctx.ld, wait=[wt], tag="wdt")
    def wdc(e):
        return e.tensor_copy(out=wdt[:, :], in_=stage[0:64, 0:2048])
    wt2 = p.emit("vector", wdc, wait=[t_wd], inc=[(ctx.ve, 1)], tag="wdtc")
    p.barrier("ld")

    prev = None
    for ch in range(16):
        for b in range(B):
            base = ch * 2 * NH3 + b * NH3
            def c0(e, base=base, ch=ch):
                return e.tensor_scalar(out=acc[:, 0:SLOC], in0=us[:, base: base + SLOC],
                                       scalar1=cw[:, ch * 4:ch * 4 + 1], scalar2=None,
                                       op0=ALU.mult)
            w = [t_u] + ([prev] if prev else [])
            t1 = p.emit("vector", c0, wait=w, inc=[(ctx.ve, 1)], tag=f"c0{ch}{b}")
            for k in range(1, 4):
                def ck(e, base=base, ch=ch, k=k):
                    return e.scalar_tensor_tensor(
                        out=acc[:, 0:SLOC], in0=us[:, base + k: base + k + SLOC],
                        scalar=cw[:, ch * 4 + k:ch * 4 + k + 1], in1=acc[:, 0:SLOC],
                        op0=ALU.mult, op1=ALU.add)
                t1 = p.emit("vector", ck, wait=[t1], inc=[(ctx.ve, 1)], tag=f"c{k}{ch}{b}")
            def sil(e, ch=ch, b=b):
                return e.activation(out=ucb[:, ch * TOKC + b * SLOC: ch * TOKC + (b + 1) * SLOC],
                                    in_=acc[:, 0:SLOC], func=AF.Silu,
                                    bias=cb[:, ch:ch + 1])
            prev = p.emit("scalar", sil, wait=[t1], inc=[(ctx.sc, 1)], tag=f"s{ch}{b}")
    tuc = prev
    p.barrier("conv")
    st1 = p.dma("gpsimd", o_uc[0:2048, 0:TOKC].rearrange("(k p) n -> p k n", p=128),
                ucb[:, :].rearrange("p (k n) -> p k n", k=16), ctx.st, tag="ucout")

    prev_e = None
    for n0 in range(0, TOKC, 512):
        def xmf(e, n0=n0):
            ins = None
            for kc in range(16):
                ins = e.matmul(out=psX[:, 0:512],
                               lhsT=wxp[:, kc * 96:(kc + 1) * 96],
                               rhs=ucb[:, kc * TOKC + n0: kc * TOKC + n0 + 512],
                               start=(kc == 0), stop=(kc == 15))
            return ins
        w = [tuc] + ([prev_e] if prev_e else [])
        t1 = p.emit("tensor", xmf, wait=w, inc=[(ctx.mm, 1)], tag=f"xd{n0}")
        def xev(e, n0=n0):
            return e.tensor_copy(out=xdbl[:, n0:n0 + 512], in_=psX[0:96, 0:512])
        prev_e = p.emit("vector", xev, wait=[t1], inc=[(ctx.ve, 1)], tag=f"xde{n0}")
    def dtc(e):
        return e.tensor_copy(out=dtb[:, :], in_=xdbl[0:64, 0:TOKC])
    tdt = p.emit("vector", dtc, wait=[prev_e], inc=[(ctx.ve, 1)], tag="dtc")
    st2 = p.dma("gpsimd", o_bc, xdbl[0:96, :][64:96, :] if False else xdbl[64:96, 0:TOKC],
                ctx.st, wait=[prev_e], tag="bcout")

    prev_d = None
    for ch in range(16):
        for n0 in range(0, TOKC, 512):
            def dmf(e, ch=ch, n0=n0):
                return e.matmul(out=psA[:, 0:512],
                                lhsT=wdt[:, ch * 128:(ch + 1) * 128],
                                rhs=dtb[:, n0:n0 + 512], start=True, stop=True)
            w = [tdt, wt2] + ([prev_d] if prev_d else [])
            t1 = p.emit("tensor", dmf, wait=w, inc=[(ctx.mm, 1)], tag=f"dm{ch}_{n0}")
            def rev(e, ch=ch):
                return e.tensor_scalar(out=raws[:, :], in0=psA[:, 0:512],
                                       scalar1=bdt[:, ch:ch + 1], scalar2=None, op0=ALU.add)
            t2 = p.emit("vector", rev, wait=[t1], inc=[(ctx.ve, 1)], tag=f"re{ch}_{n0}")
            def exf(e):
                return e.activation(out=expt[:, :], in_=raws[:, :], func=AF.Exp)
            t3 = p.emit("scalar", exf, wait=[t2], inc=[(ctx.sc, 1)], tag=f"ex{ch}_{n0}")
            def lnf(e):
                return e.activation(out=lnt[:, :], in_=expt[:, :], func=AF.Ln,
                                    bias=ccol("one", 128)[0:128, 0:1])
            t4 = p.emit("scalar", lnf, wait=[t3], inc=[(ctx.sc, 1)], tag=f"ln{ch}_{n0}")
            def cmf(e):
                return e.tensor_scalar(out=msk[:, :], in0=raws[:, :], scalar1=20.0,
                                       scalar2=None, op0=ALU.is_gt)
            t5 = p.emit("vector", cmf, wait=[t3], inc=[(ctx.ve, 1)], tag=f"cm{ch}_{n0}")
            def slf(e):
                return e.select(out=dsl[:, :], mask=msk[:, :], on_true=raws[:, :],
                                on_false=lnt[:, :])
            t6 = p.emit("vector", slf, wait=[t4, t5], inc=[(ctx.ve, 1)], tag=f"sl{ch}_{n0}")
            prev_d = p.dma("gpsimd", o_dl[ch * 128:(ch + 1) * 128, n0:n0 + 512],
                           dsl[:, :], ctx.st, wait=[t6], tag=f"dl{ch}_{n0}")
    def fin(e):
        return e.nop()
    p.emit("gpsimd", fin, wait=[prev_d, st1, st2], tag="fin")
    res = p.run()
    uc = fm_unslice([res.results[c]["ouc"] for c in range(NCORES)])
    dl = fm_unslice([res.results[c]["odl"] for c in range(NCORES)])
    bc = fm_unslice([res.results[c]["obc"] for c in range(NCORES)])
    return uc, dl, bc


def launch_l3_scan(dl, uc, bc, W, _fused=None):
    if _fused is None:
        p = Prog("l3scan")
        ctx = Ctx(p)
        io = None
        a_dl = p.inp("i_dl", [np.ascontiguousarray(dl[256 * c:256 * (c + 1), :]) for c in range(NCORES)])
        a_uc = p.inp("i_uc", [np.ascontiguousarray(uc[256 * c:256 * (c + 1), :]) for c in range(NCORES)])
        a_bc = p.inp("i_bc", np.ascontiguousarray(bc))
        bco = 0
    else:
        p, ctx, io = _fused
        mkS3 = p.sbuf_mark()
        a_dl, a_uc = io["dl"], io["uc"]
        a_bc = io["xdblar"]
        bco = 64
    AL = W["L3_attn_A_log"]
    def alar(c):
        m = np.zeros((128, 32), np.float32)
        for g2 in range(32):
            for d in range(8):
                for n in range(16):
                    m[d * 16 + n, g2] = AL[256 * c + g2 * 8 + d, n]
        return m
    a_al = p.inp("i_al", [alar(c) for c in range(NCORES)])
    indD = np.zeros((128, 16 * 128), NBF)
    for j in range(16):
        for d in range(8):
            for n in range(16):
                indD[j * 8 + d, j * 128 + d * 16 + n] = 1
    a_id = p.inp("i_id", indD)
    indN = np.zeros((32, 256), NBF)
    for d in range(8):
        for n in range(16):
            indN[n, d * 16 + n] = 1
            indN[16 + n, 128 + d * 16 + n] = 1
    a_in = p.inp("i_in", indN)
    ind2 = np.zeros((128, 8), NBF)
    for pp in range(128):
        ind2[pp, pp // 16] = 1
    a_i2 = p.inp("i_i2", ind2)
    o_y = io["y"] if io is not None else p.out("oy", [256, TOKG], F32)

    dsb = p.sbuf("dsb", [128, SEQ], BF16)
    usb = p.sbuf("usb", [128, SEQ], BF16)
    dub = p.sbuf("dub", [128, SEQ], BF16)
    Bbc = p.sbuf("Bbc", [128, SEQ], BF16)
    Cbc = p.sbuf("Cbc", [128, SEQ], BF16)
    bcbf = p.sbuf("bcbf", [32, SEQ], BF16)
    stage = p.sbuf("stage", [32, 4096], F32)
    idb = p.sbuf("idb", [128, 16 * 128], BF16)
    inb = p.sbuf("inb", [32, 256], BF16)
    i2b = p.sbuf("i2b", [128, 8], BF16)
    alb = p.sbuf("alb", [128, 32], F32)
    acol = p.sbuf("acol", [128, 32], F32)
    asb = p.sbuf("asb", [128, 4096], F32)
    xsb = p.sbuf("xsb", [128, 4096], F32)
    hsb = p.sbuf("hsb", [128, 4096], F32)
    hC = p.sbuf("hC", [128, 4096], BF16)
    hlast = p.sbuf("hlast", [128, 1], F32)
    yev = p.sbuf("yev", [8, 4096], F32)
    psA = p.psum("psA", [128, 512], F32)
    psB = p.psum("psB", [128, 512], F32)
    psC = p.psum("psC", [128, 512], F32)
    psD = p.psum("psD", [128, 512], F32)
    psY0 = p.psum("psY0", [8, 512], F32)
    psY1 = p.psum("psY1", [8, 512], F32)

    tld = []
    for ap, sb in ((a_id, idb), (a_in, inb), (a_i2, i2b), (a_al, alb)):
        tld.append(p.dma("sync", sb[:, :], ap, ctx.ld))
    def aef(e):
        return e.activation(out=acol[:, :], in_=alb[:, :], func=AF.Exp)
    t1 = p.emit("scalar", aef, wait=[tld[-1]], inc=[(ctx.sc, 1)], tag="aexp")
    def anf(e):
        return e.tensor_scalar(out=acol[:, :], in0=acol[:, :], scalar1=-1.0,
                               scalar2=None, op0=ALU.mult)
    tac = p.emit("vector", anf, wait=[t1], inc=[(ctx.ve, 1)], tag="aneg")
    p.barrier("consts")

    prev_store = None
    prev_scan = None
    for b in range(B):
        tcast = None
        for half in range(2):
            tst = p.dma("sync", stage[:, :],
                        a_bc[bco:bco + 32, b * SEQ + half * 4096: b * SEQ + (half + 1) * 4096],
                        ctx.ld, wait=[tcast] if tcast else [], tag=f"bcl{b}{half}")
            def cf(e, half=half):
                return e.tensor_copy(out=bcbf[:, half * 4096:(half + 1) * 4096], in_=stage[:, :])
            tcast = p.emit("vector", cf, wait=[tst], inc=[(ctx.ve, 1)], tag=f"bcc{b}{half}")
        prev_ev = None
        for s0 in range(0, SEQ, 512):
            def bmf(e, s0=s0):
                return e.matmul(out=psA[:, 0:512], lhsT=inb[0:32, 0:128],
                               rhs=bcbf[0:32, s0:s0 + 512], start=True, stop=True)
            w = [tcast] + ([prev_ev] if prev_ev else [])
            if s0 == 0 and prev_scan is not None:
                w.append(prev_scan)
            tm1 = p.emit("tensor", bmf, wait=w, inc=[(ctx.mm, 1)], tag=f"bm{b}_{s0}")
            def cmf2(e, s0=s0):
                return e.matmul(out=psB[:, 0:512], lhsT=inb[0:32, 128:256],
                               rhs=bcbf[0:32, s0:s0 + 512], start=True, stop=True)
            tm2 = p.emit("tensor", cmf2, wait=[tm1], inc=[(ctx.mm, 1)], tag=f"cm{b}_{s0}")
            def bev(e, s0=s0):
                return e.tensor_copy(out=Bbc[:, s0:s0 + 512], in_=psA[:, 0:512])
            te1 = p.emit("vector", bev, wait=[tm1], inc=[(ctx.ve, 1)], tag=f"be{b}_{s0}")
            def cev(e, s0=s0):
                return e.tensor_copy(out=Cbc[:, s0:s0 + 512], in_=psB[:, 0:512])
            prev_ev = p.emit("vector", cev, wait=[tm2], inc=[(ctx.ve, 1)], tag=f"ce{b}_{s0}")
        tbc = prev_ev
        for ch in range(2):
            tdl = p.dma("sync", dsb[:, :], a_dl[ch * 128:(ch + 1) * 128, b * SEQ:(b + 1) * SEQ],
                        ctx.ld, wait=[prev_scan] if prev_scan else [], tag=f"dl{b}{ch}")
            tul = p.dma("sync", usb[:, :], a_uc[ch * 128:(ch + 1) * 128, b * SEQ:(b + 1) * SEQ],
                        ctx.ld, tag=f"ul{b}{ch}")
            def duf(e):
                return e.tensor_tensor(out=dub[:, :], in0=dsb[:, :], in1=usb[:, :], op=ALU.mult)
            tdu = p.emit("vector", duf, wait=[tdl, tul, tbc], inc=[(ctx.ve, 1)], tag=f"du{b}{ch}")
            for j in range(16):
                g2 = ch * 16 + j
                for half in range(2):
                    toff = half * 4096
                    pe_prev = None
                    for i, s0 in enumerate(range(0, 4096, 512)):
                        pd = psA if i % 2 == 0 else psC
                        pd2 = psB if i % 2 == 0 else psD
                        def m1(e, j=j, s0=s0, toff=toff, pd=pd):
                            return e.matmul(out=pd[:, 0:512], lhsT=idb[:, j * 128:(j + 1) * 128],
                                           rhs=dsb[:, toff + s0: toff + s0 + 512],
                                           start=True, stop=True)
                        w = [tdu] + (list(pe_prev) if pe_prev else []) + ([prev_scan] if prev_scan and i == 0 else [])
                        ta = p.emit("tensor", m1, wait=w, inc=[(ctx.mm, 1)], tag=f"da{b}{g2}{half}{i}")
                        def ef(e, s0=s0, g2=g2, pd=pd):
                            return e.activation(out=asb[:, s0:s0 + 512], in_=pd[:, 0:512],
                                               func=AF.Exp, scale=acol[:, g2:g2 + 1])
                        te = p.emit("scalar", ef, wait=[ta, tac], inc=[(ctx.sc, 1)], tag=f"ae{b}{g2}{half}{i}")
                        def m2(e, j=j, s0=s0, toff=toff, pd2=pd2):
                            return e.matmul(out=pd2[:, 0:512], lhsT=idb[:, j * 128:(j + 1) * 128],
                                           rhs=dub[:, toff + s0: toff + s0 + 512],
                                           start=True, stop=True)
                        tb = p.emit("tensor", m2, wait=[ta], inc=[(ctx.mm, 1)], tag=f"db{b}{g2}{half}{i}")
                        def xf(e, s0=s0, toff=toff, pd2=pd2):
                            return e.tensor_tensor(out=xsb[:, s0:s0 + 512], in0=pd2[:, 0:512],
                                                  in1=Bbc[:, toff + s0: toff + s0 + 512], op=ALU.mult)
                        tx = p.emit("vector", xf, wait=[tb], inc=[(ctx.ve, 1)], tag=f"x{b}{g2}{half}{i}")
                        pe_prev = (te, tx)
                        pe_last = (te, tx)
                    def scf(e, half=half):
                        ini = 0.0 if half == 0 else hlast[:, 0:1]
                        return e.tensor_tensor_scan(out=hsb[:, :], data0=asb[:, :],
                                                   data1=xsb[:, :], initial=ini,
                                                   op0=ALU.mult, op1=ALU.add)
                    w = [pe_last[0], pe_last[1]]
                    if half == 1:
                        w.append(thl)
                    tsc = p.emit("vector", scf, wait=w, inc=[(ctx.ve, 1)], tag=f"sc{b}{g2}{half}")
                    def hlf(e):
                        return e.tensor_copy(out=hlast[:, :], in_=hsb[:, 4095:4096])
                    thl = p.emit("vector", hlf, wait=[tsc], inc=[(ctx.ve, 1)], tag=f"hl{b}{g2}{half}")
                    def hcf(e, toff=toff):
                        return e.tensor_tensor(out=hC[:, :], in0=hsb[:, :],
                                              in1=Cbc[:, toff:toff + 4096], op=ALU.mult)
                    thc = p.emit("vector", hcf, wait=[tsc], inc=[(ctx.ve, 1)], tag=f"hc{b}{g2}{half}")
                    ev_prev = None
                    for i, s0 in enumerate(range(0, 4096, 512)):
                        py = psY0 if i % 2 == 0 else psY1
                        def ym(e, s0=s0, py=py):
                            return e.matmul(out=py[:, 0:512], lhsT=i2b[:, 0:8],
                                           rhs=hC[:, s0:s0 + 512], start=True, stop=True)
                        w = [thc] + ([ev_prev] if ev_prev else []) + ([prev_store] if prev_store and i == 0 else [])
                        tym = p.emit("tensor", ym, wait=w, inc=[(ctx.mm, 1)], tag=f"ym{b}{g2}{half}{i}")
                        def yev_f(e, s0=s0, py=py):
                            return e.tensor_copy(out=yev[:, s0:s0 + 512], in_=py[0:8, 0:512])
                        ev_prev = p.emit("vector", yev_f, wait=[tym], inc=[(ctx.ve, 1)], tag=f"ye{b}{g2}{half}{i}")
                    prev_store = p.dma("gpsimd", o_y[g2 * 8:(g2 + 1) * 8, b * SEQ + toff: b * SEQ + toff + 4096],
                                       yev[0:8, 0:4096], ctx.st, wait=[ev_prev], tag=f"yst{b}{g2}{half}")
                    prev_scan = thc
    if io is not None:
        p.barrier("l3scan_done")
        p.sbuf_release(mkS3)
        return
    def fin(e):
        return e.nop()
    p.emit("gpsimd", fin, wait=[prev_store], tag="fin")
    res = p.run()
    return np.concatenate([res.results[c]["oy"] for c in range(NCORES)], 0)


def launch_l3_post(y_full, uc, z, xn3fm, W):
    p = Prog("l3post")
    ctx = Ctx(p)
    def sl2(m, c):
        i0, i1 = core_cols(c)
        return np.ascontiguousarray(np.concatenate([m[:, i0], m[:, i1]], axis=1))
    a_y = p.inp("i_y", [sl2(y_full, c) for c in range(NCORES)])
    a_uc = p.inp("i_uc", [sl2(uc, c) for c in range(NCORES)])
    a_z = p.inp("i_z", [sl2(z, c) for c in range(NCORES)])
    a_xn = p.inp("i_xn", [fm_slice(xn3fm, c) for c in range(NCORES)])
    a_wo = p.inp("i_wo", W["L3_attn_W_out"])
    a_d = p.inp("i_d", np.ascontiguousarray(W["L3_attn_D_skip"].reshape(16, 128).T))
    o_s = p.out("s_o", [D, TOKC], F32)

    ystage = p.sbuf("ystage", [128, 16 * 512], F32)
    ucsl = p.sbuf("ucsl", [128, 16 * 512], BF16)
    zsl = p.sbuf("zsl", [128, 16 * 512], BF16)
    wout = p.sbuf("wout", [128, 16 * 1024], BF16)
    dcol = p.sbuf("dcol", [128, 16], F32)
    yz = p.sbuf("yz", [128, 16 * TOKC], BF16)
    tmp = p.sbuf("tmp", [128, 512], F32)
    xnst = [p.sbuf("xnst0", [128, 512], F32), p.sbuf("xnst1", [128, 512], F32)]
    osl = p.sbuf("osl", [128, 8 * 512], BF16)
    psA = p.psum("psA", [128, 512], F32)
    psB = p.psum("psB", [128, 512], F32)

    td = p.dma("sync", dcol[:, :], a_d, ctx.ld)
    wt = load_weight_bf16(p, ctx, a_wo, 2048, 1024, "wo", wout, ystage, prev=None)
    p.barrier("wload")

    prev_tt = None
    prev_y = wt
    for n0 in range(0, TOKC, 512):
        ty = p.dma("sync", ystage[:, :].rearrange("p (k n) -> p k n", k=16),
                   a_y[0:2048, n0:n0 + 512].rearrange("(k p) n -> p k n", p=128),
                   ctx.ld, wait=[prev_y], tag=f"yl{n0}")
        tu = p.dma("sync", ucsl[:, :].rearrange("p (k n) -> p k n", k=16),
                   a_uc[0:2048, n0:n0 + 512].rearrange("(k p) n -> p k n", p=128),
                   ctx.ld, tag=f"ul{n0}")
        tz = p.dma("sync", zsl[:, :].rearrange("p (k n) -> p k n", k=16),
                   a_z[0:2048, n0:n0 + 512].rearrange("(k p) n -> p k n", p=128),
                   ctx.ld, tag=f"zl{n0}")
        last = None
        for ch in range(16):
            def sf(e, ch=ch):
                return e.scalar_tensor_tensor(
                    out=tmp[:, :], in0=ucsl[:, ch * 512:(ch + 1) * 512],
                    scalar=dcol[:, ch:ch + 1], in1=ystage[:, ch * 512:(ch + 1) * 512],
                    op0=ALU.mult, op1=ALU.add)
            w = [ty, tu, td] + ([last] if last else []) + ([prev_tt] if prev_tt and ch == 0 else [])
            t1 = p.emit("vector", sf, wait=w, inc=[(ctx.ve, 1)], tag=f"sk{n0}_{ch}")
            def tf(e, ch=ch, n0=n0):
                return e.tensor_tensor(out=yz[:, ch * TOKC + n0: ch * TOKC + n0 + 512],
                                      in0=tmp[:, :], in1=zsl[:, ch * 512:(ch + 1) * 512],
                                      op=ALU.mult)
            last = p.emit("vector", tf, wait=[t1, tz], inc=[(ctx.ve, 1)], tag=f"tz{n0}_{ch}")
        prev_tt = last
        prev_y = last
    tyz = prev_tt
    p.barrier("yz")

    prev_st = None
    prev_x = None
    for fc in range(8):
        for i, n0 in enumerate(range(0, TOKC, 512)):
            xb = xnst[i % 2]
            tx = p.dma("sync", xb[:, :], a_xn[fc * 128:(fc + 1) * 128, n0:n0 + 512],
                       ctx.ld, wait=[prev_x] if prev_x else [], tag=f"xl{fc}_{n0}")
            ps = psA if i % 2 == 0 else psB
            def mf(e, fc=fc, n0=n0, ps=ps):
                ins = None
                for kc in range(16):
                    ins = e.matmul(out=ps[:, 0:512],
                                   lhsT=wout[:, kc * 1024 + fc * 128: kc * 1024 + (fc + 1) * 128],
                                   rhs=yz[:, kc * TOKC + n0: kc * TOKC + n0 + 512],
                                   start=(kc == 0), stop=(kc == 15))
                return ins
            w = [tyz] + ([prev_ev2] if fc + i > 0 else [])
            tm1 = p.emit("tensor", mf, wait=w, inc=[(ctx.mm, 1)], tag=f"m{fc}_{n0}")
            def ev(e, fc=fc, n0=n0, ps=ps, xb=xb):
                return e.tensor_tensor(out=osl[:, fc * 512:(fc + 1) * 512] if False else osl[:, 0:512],
                                      in0=ps[:, 0:512], in1=xb[:, :], op=ALU.add)
            def ev2(e, ps=ps, xb=xb):
                return e.tensor_tensor(out=tmp[:, :], in0=ps[:, 0:512], in1=xb[:, :], op=ALU.add)
            w2 = [tm1, tx] + ([prev_st] if prev_st else [])
            prev_ev2 = p.emit("vector", ev2, wait=w2, inc=[(ctx.ve, 1)], tag=f"e{fc}_{n0}")
            prev_x = prev_ev2
            prev_st = p.dma("gpsimd", o_s[fc * 128:(fc + 1) * 128, n0:n0 + 512],
                            tmp[:, :], ctx.st, wait=[prev_ev2], tag=f"st{fc}_{n0}")
    def fin(e):
        return e.nop()
    p.emit("gpsimd", fin, wait=[prev_st], tag="fin")
    res = p.run()
    return fm_unslice([res.results[c]["s_o"] for c in range(NCORES)])


def launch_l3_ffn(xnfm, W, _fused=None):
    if _fused is None:
        p = Prog("l3ffn")
        ctx = Ctx(p)
        io = None
        xin = p.inp("xi", [fm_slice(xnfm, c) for c in range(NCORES)])
    else:
        p, ctx, io = _fused
        mkF3 = p.sbuf_mark()
        xin = io["xn"]
    a_gw = p.inp("i_gw", W["L3_ffn_gate_w"])
    a_gb = p.inp("i_gb", np.ascontiguousarray(W["L3_ffn_gate_b"].reshape(8, 1)))
    a_w1 = p.inp("i_w1", np.ascontiguousarray(
        np.asarray(W["L3_ffn_w1"]).reshape(8 * 1024, 4096).astype(NBF)))
    a_w2 = p.inp("i_w2", np.ascontiguousarray(
        np.asarray(W["L3_ffn_w2"]).reshape(8 * 4096, 1024).astype(NBF)))
    a_b1 = p.inp("i_b1", np.ascontiguousarray(
        W["L3_ffn_b1"].reshape(8, 32, 128).transpose(0, 2, 1).reshape(8 * 128, 32)))
    a_b2 = p.inp("i_b2", np.ascontiguousarray(
        W["L3_ffn_b2"].reshape(8, 8, 128).transpose(0, 2, 1).reshape(8 * 128, 8)))
    a_io = p.inp("i_io", np.tile(np.arange(8, dtype=np.float32), (128, 1)))
    a_id8 = p.inp("i_id8", np.eye(8, dtype=np.float32))
    a_id128 = p.inp("i_id128", np.eye(128, dtype=np.float32))
    a_onesr = p.inp("i_onesr", ONES_ROW)
    sel = np.zeros((8, 8 * 128), np.float32)
    for e2 in range(8):
        sel[e2, e2 * 128:(e2 + 1) * 128] = 1
    a_sel = p.inp("i_sel8", sel)
    o_ff = io["o"] if io is not None else p.out("ff_o", [D, TOKC], F32)

    stage = p.sbuf("stage", [128, 8 * 512], F32)
    xnb = p.sbuf("xnb", [128, 8 * TOKC], BF16)
    accf = p.sbuf("accf", [128, 8 * TOKC], BF16)
    w1b = p.sbuf("w1b", [128, 8 * 2048], BF16)
    w2b = p.sbuf("w2b", [128, 16 * 1024], BF16)
    hb = p.sbuf("hb", [128, 16 * 512], BF16)
    gwsb = p.sbuf("gwsb", [128, 8 * 8], F32)
    gbc = p.sbuf("gbc", [8, 1], F32)
    lg = p.sbuf("lg", [8, TOKC], F32)
    we_row = p.sbuf("we_row", [8, TOKC], F32)
    io8 = p.sbuf("io8", [128, 8], F32)
    id8 = p.sbuf("id8", [8, 8], F32)
    id128 = p.sbuf("id128", [128, 128], F32)
    onr = p.sbuf("onr", [1, 128], F32)
    sel8 = p.sbuf("sel8", [8, 8 * 128], F32)
    b1c = p.sbuf("b1c", [128, 32], F32)
    b2c = p.sbuf("b2c", [128, 8], F32)
    lgT = p.sbuf("lgT", [128, 8], F32)
    mx8 = p.sbuf("mx8", [128, 8], F32)
    mi8 = p.sbuf("mi8", [128, 8], U32)
    dv = p.sbuf("dv", [128, 1], F32)
    wa = p.sbuf("wa", [128, 1], F32)
    wb = p.sbuf("wb", [128, 1], F32)
    i1f = p.sbuf("i1f", [128, 1], F32)
    i2f = p.sbuf("i2f", [128, 1], F32)
    oh1 = p.sbuf("oh1", [128, 8], F32)
    oh2 = p.sbuf("oh2", [128, 8], F32)
    weT = p.sbuf("weT", [128, 8], F32)
    tmp = p.sbuf("tmp", [128, 512], F32)
    tmp2 = p.sbuf("tmp2", [128, 512], F32)
    psR = p.psum("psR", [8, 512], F32)
    psT = p.psum("psT", [128, 8], F32)
    psT2 = p.psum("psT2", [8, 128], F32)
    psH0 = p.psum("psH0", [128, 512], F32)
    psH1 = p.psum("psH1", [128, 512], F32)
    psW = p.psum("psW", [128, 512], F32)

    tg = p.dma("sync", gwsb[:, :].rearrange("p (k n) -> p k n", k=8),
               a_gw[0:1024, 0:8].rearrange("(k p) n -> p k n", p=128), ctx.ld)
    for ap, sb in ((a_gb, gbc), (a_io, io8), (a_id8, id8), (a_id128, id128),
                   (a_onesr, onr), (a_sel, sel8)):
        tg = p.dma("sync", sb[:, :], ap, ctx.ld, tag="cl")
    prev = None
    prev_ev = None
    for i, n0 in enumerate(range(0, TOKC, 512)):
        tx = p.dma("sync", stage[:, :].rearrange("p (k n) -> p k n", k=8),
                   xin[0:D, n0:n0 + 512].rearrange("(k p) n -> p k n", p=128),
                   ctx.ld, wait=([prev, prev_ev] if prev else [tg]), tag=f"xl{n0}")
        def cf(e, n0=n0):
            return e.tensor_copy(out=xnb[:, :].rearrange("p (k n) -> p k n", k=8)[:, :, n0:n0 + 512],
                                in_=stage[:, :].rearrange("p (k n) -> p k n", k=8))
        prev = p.emit("vector", cf, wait=[tx], inc=[(ctx.ve, 1)], tag=f"xc{n0}")
        def rf(e):
            ins = None
            for kc in range(8):
                ins = e.matmul(out=psR[:, 0:512], lhsT=gwsb[:, kc * 8:(kc + 1) * 8],
                               rhs=stage[:, kc * 512:(kc + 1) * 512],
                               start=(kc == 0), stop=(kc == 7))
            return ins
        w = [tx] + ([prev_ev] if prev_ev else [])
        t1 = p.emit("tensor", rf, wait=w, inc=[(ctx.mm, 1)], tag=f"rm{n0}")
        def re(e, n0=n0):
            return e.tensor_scalar(out=lg[0:8, n0:n0 + 512], in0=psR[0:8, 0:512],
                                   scalar1=gbc[0:8, 0:1], scalar2=None, op0=ALU.add)
        prev_ev = p.emit("vector", re, wait=[t1], inc=[(ctx.ve, 1)], tag=f"re{n0}")
    prev_t = prev_ev
    for ti in range(16):
        t0c = ti * 128
        def tf(e, t0c=t0c):
            return e.transpose(out=psT[0:128, 0:8], in_=lg[0:8, t0c:t0c + 128],
                              identity=id8[0:8, 0:8])
        t1 = p.emit("tensor", tf, wait=[prev_t], inc=[(ctx.mm, 1)], tag=f"tp{ti}")
        def c1(e):
            return e.tensor_copy(out=lgT[:, :], in_=psT[0:128, 0:8])
        t2 = p.emit("vector", c1, wait=[t1], inc=[(ctx.ve, 1)], tag=f"tc{ti}")
        def mf(e):
            return e.max(out=mx8[:, :], in_=lgT[:, :])
        t3 = p.emit("vector", mf, wait=[t2], inc=[(ctx.ve, 1)], tag=f"mx{ti}")
        def mif(e):
            return e.max_index(out=mi8[:, :], in_max=mx8[:, :], in_values=lgT[:, :])
        t4 = p.emit("vector", mif, wait=[t3], inc=[(ctx.ve, 1)], tag=f"mi{ti}")
        def dvf(e):
            return e.tensor_tensor(out=dv[:, :], in0=mx8[:, 0:1], in1=mx8[:, 1:2],
                                  op=ALU.subtract)
        t5 = p.emit("vector", dvf, wait=[t3], inc=[(ctx.ve, 1)], tag=f"dv{ti}")
        def waf(e):
            return e.activation(out=wa[:, :], in_=dv[:, :], func=AF.Sigmoid)
        t6 = p.emit("scalar", waf, wait=[t5], inc=[(ctx.sc, 1)], tag=f"wa{ti}")
        def wbf(e):
            return e.tensor_scalar(out=wb[:, :], in0=wa[:, :], scalar1=-1.0,
                                   scalar2=1.0, op0=ALU.mult, op1=ALU.add)
        t7 = p.emit("vector", wbf, wait=[t6], inc=[(ctx.ve, 1)], tag=f"wb{ti}")
        def if1(e):
            return e.tensor_copy(out=i1f[:, :], in_=mi8[:, 0:1])
        t8 = p.emit("vector", if1, wait=[t4], inc=[(ctx.ve, 1)], tag=f"i1{ti}")
        def if2(e):
            return e.tensor_copy(out=i2f[:, :], in_=mi8[:, 1:2])
        t9 = p.emit("vector", if2, wait=[t4], inc=[(ctx.ve, 1)], tag=f"i2{ti}")
        def o1f(e):
            return e.tensor_scalar(out=oh1[:, :], in0=io8[:, :], scalar1=i1f[:, 0:1],
                                   scalar2=None, op0=ALU.is_equal)
        t10 = p.emit("vector", o1f, wait=[t8], inc=[(ctx.ve, 1)], tag=f"o1{ti}")
        def o2f(e):
            return e.tensor_scalar(out=oh2[:, :], in0=io8[:, :], scalar1=i2f[:, 0:1],
                                   scalar2=None, op0=ALU.is_equal)
        t11 = p.emit("vector", o2f, wait=[t9], inc=[(ctx.ve, 1)], tag=f"o2{ti}")
        def ow1(e):
            return e.tensor_scalar(out=oh1[:, :], in0=oh1[:, :], scalar1=wa[:, 0:1],
                                   scalar2=None, op0=ALU.mult)
        t12 = p.emit("vector", ow1, wait=[t10, t6], inc=[(ctx.ve, 1)], tag=f"ow1{ti}")
        def wef(e):
            return e.scalar_tensor_tensor(out=weT[:, :], in0=oh2[:, :], scalar=wb[:, 0:1],
                                         in1=oh1[:, :], op0=ALU.mult, op1=ALU.add)
        t13 = p.emit("vector", wef, wait=[t11, t12, t7], inc=[(ctx.ve, 1)], tag=f"we{ti}")
        def tbf(e):
            return e.transpose(out=psT2[0:8, 0:128], in_=weT[0:128, 0:8],
                              identity=id128[0:128, 0:128])
        t14 = p.emit("tensor", tbf, wait=[t13], inc=[(ctx.mm, 1)], tag=f"tb{ti}")
        def c2(e, t0c=t0c):
            return e.tensor_copy(out=we_row[0:8, t0c:t0c + 128], in_=psT2[0:8, 0:128])
        prev_t = p.emit("vector", c2, wait=[t14], inc=[(ctx.ve, 1)], tag=f"wc{ti}")
    twe = prev_t
    p.barrier("route")

    prev_acc = None
    last_hm_fc = {}
    last_om_fc2 = {}
    for e_i in range(8):
        for hh in range(2):
            r0 = e_i * 4096 + hh * 2048
            w1tok = {}
            w2tok = {}
            for fc in range(16):
                w = [last_hm_fc[fc]] if fc in last_hm_fc else []
                w1tok[fc] = p.dma(
                    "sync",
                    w1b[:, :].rearrange("p (k f) -> p k f", k=8)[:, :, fc * 128:(fc + 1) * 128],
                    a_w1[e_i * 1024:(e_i + 1) * 1024,
                         hh * 2048 + fc * 128: hh * 2048 + (fc + 1) * 128]
                    .rearrange("(k p) f -> p k f", p=128),
                    ctx.ld, wait=w, tag=f"w1l{e_i}_{hh}_{fc}")
            for fc2 in range(8):
                w = [last_om_fc2[fc2]] if fc2 in last_om_fc2 else []
                w2tok[fc2] = p.dma(
                    "sync",
                    w2b[:, :].rearrange("p (k f) -> p k f", k=16)[:, :, fc2 * 128:(fc2 + 1) * 128],
                    a_w2[r0:r0 + 2048, fc2 * 128:(fc2 + 1) * 128]
                    .rearrange("(k p) f -> p k f", p=128),
                    ctx.ld, wait=w, tag=f"w2l{e_i}_{hh}_{fc2}")
            if hh == 0:
                tb1 = p.dma("sync", b1c[:, :], a_b1[e_i * 128:(e_i + 1) * 128, 0:32], ctx.ld,
                            wait=[prev_acc] if prev_acc else [], tag=f"b1l{e_i}")
                tb2 = p.dma("sync", b2c[:, :], a_b2[e_i * 128:(e_i + 1) * 128, 0:8], ctx.ld,
                            tag=f"b2l{e_i}")
            for i, n0 in enumerate(range(0, TOKC, 512)):
                def wbc(e, n0=n0, e_i=e_i):
                    return e.matmul(out=psW[:, 0:512], lhsT=sel8[0:8, e_i * 128:(e_i + 1) * 128],
                                   rhs=we_row[0:8, n0:n0 + 512], start=True, stop=True)
                w = [twe] + ([prev_acc] if prev_acc else [])
                tw = p.emit("tensor", wbc, wait=w, inc=[(ctx.mm, 1)], tag=f"wb{e_i}_{hh}_{n0}")
                ge_prev = None
                for fc in range(16):
                    ps = psH0 if fc % 2 == 0 else psH1
                    def hm(e, fc=fc, n0=n0, ps=ps):
                        ins = None
                        for kc in range(8):
                            ins = e.matmul(out=ps[:, 0:512],
                                           lhsT=w1b[:, kc * 2048 + fc * 128: kc * 2048 + (fc + 1) * 128],
                                           rhs=xnb[:, kc * TOKC + n0: kc * TOKC + n0 + 512],
                                           start=(kc == 0), stop=(kc == 7))
                        return ins
                    w = [tw, w1tok[fc]] + ([ge_prev[0]] if ge_prev else [])
                    th = p.emit("tensor", hm, wait=w, inc=[(ctx.mm, 1)], tag=f"hm{e_i}_{hh}_{n0}_{fc}")
                    last_hm_fc[fc] = th
                    def gf(e, fc=fc, ps=ps, hh=hh):
                        return e.activation(out=hb[:, fc * 512:(fc + 1) * 512], in_=ps[:, 0:512],
                                           func=AF.Gelu, bias=b1c[:, hh * 16 + fc: hh * 16 + fc + 1])
                    tg2 = p.emit("scalar", gf, wait=[th, tb1], inc=[(ctx.sc, 1)],
                                 tag=f"g{e_i}_{hh}_{n0}_{fc}")
                    ge_prev = (tg2,)
                tgl = ge_prev[0]
                oc_prev = None
                for fc2 in range(8):
                    ps = psH0 if fc2 % 2 == 0 else psH1
                    def om(e, fc2=fc2, ps=ps):
                        ins = None
                        for kc in range(16):
                            ins = e.matmul(out=ps[:, 0:512],
                                           lhsT=w2b[:, kc * 1024 + fc2 * 128: kc * 1024 + (fc2 + 1) * 128],
                                           rhs=hb[:, kc * 512:(kc + 1) * 512],
                                           start=(kc == 0), stop=(kc == 15))
                        return ins
                    w = [tgl, w2tok[fc2]] + ([oc_prev[0]] if oc_prev else [])
                    to = p.emit("tensor", om, wait=w, inc=[(ctx.mm, 1)], tag=f"om{e_i}_{hh}_{n0}_{fc2}")
                    last_om_fc2[fc2] = to
                    if hh == 0:
                        def bf(e, fc2=fc2, ps=ps):
                            return e.tensor_scalar(out=tmp[:, :], in0=ps[:, 0:512],
                                                   scalar1=b2c[:, fc2:fc2 + 1], scalar2=None,
                                                   op0=ALU.add)
                    else:
                        def bf(e, ps=ps):
                            return e.tensor_copy(out=tmp[:, :], in_=ps[:, 0:512])
                    t1 = p.emit("vector", bf, wait=[to, tb2], inc=[(ctx.ve, 1)],
                                tag=f"bf{e_i}_{hh}_{n0}_{fc2}")
                    def wf(e):
                        return e.tensor_tensor(out=tmp2[:, :], in0=tmp[:, :], in1=psW[0:128, 0:512],
                                              op=ALU.mult)
                    t2 = p.emit("vector", wf, wait=[t1, tw], inc=[(ctx.ve, 1)],
                                tag=f"wf{e_i}_{hh}_{n0}_{fc2}")
                    if e_i == 0 and hh == 0:
                        def af(e, fc2=fc2, n0=n0):
                            return e.tensor_copy(out=accf[:, fc2 * TOKC + n0: fc2 * TOKC + n0 + 512],
                                                in_=tmp2[:, :])
                    else:
                        def af(e, fc2=fc2, n0=n0):
                            return e.tensor_tensor(out=accf[:, fc2 * TOKC + n0: fc2 * TOKC + n0 + 512],
                                                  in0=accf[:, fc2 * TOKC + n0: fc2 * TOKC + n0 + 512],
                                                  in1=tmp2[:, :], op=ALU.add)
                    t3 = p.emit("vector", af, wait=[t2], inc=[(ctx.ve, 1)],
                                tag=f"af{e_i}_{hh}_{n0}_{fc2}")
                    oc_prev = (to, t3)
                prev_acc = oc_prev[1]
    p.barrier("exp")
    prev_st = None
    prev_c = None
    for fc2 in range(8):
        for n0 in range(0, TOKC, 512):
            def cf2(e, fc2=fc2, n0=n0):
                return e.tensor_copy(out=tmp[:, :], in_=accf[:, fc2 * TOKC + n0: fc2 * TOKC + n0 + 512])
            w = ([prev_st] if prev_st else [])
            prev_c = p.emit("vector", cf2, wait=w, inc=[(ctx.ve, 1)], tag=f"ec{fc2}_{n0}")
            prev_st = p.dma("gpsimd", o_ff[fc2 * 128:(fc2 + 1) * 128, n0:n0 + 512],
                            tmp[:, :], ctx.st, wait=[prev_c], tag=f"es{fc2}_{n0}")
    if io is not None:
        p.barrier("l3ffn_done")
        p.sbuf_release(mkF3)
        return
    def fin(e):
        return e.nop()
    p.emit("gpsimd", fin, wait=[prev_st], tag="fin")
    res = p.run()
    return fm_unslice([res.results[c]["ff_o"] for c in range(NCORES)])



def kernel(**inputs):
    W = {k: np.asarray(v) for k, v in inputs.items()}
    res = run_fused(W)
    x = fm_unslice([res.results[c]["o_x1"] for c in range(NCORES)])
    return from_fm(x)



def coll(p, ctx, kind, op, in_dram, out_dram, wait=(), tag="coll"):
    def fn(e):
        return e.collective_compute(
            kind, op, replica_groups=[list(range(NCORES))],
            ins=[in_dram.ap().opt()], outs=[out_dram.ap().opt()])
    return p.emit("gpsimd", fn, wait=wait, inc=[(ctx.gp, 1)], tag=tag)


def st_l0_halo(p, ctx, io, W):
    WIN = 256
    mk0 = p.sbuf_mark()
    g_dram, h_dram = io["g_dram"], io["h_dram"]
    def sidx(c):
        idx = np.full((2048,), 10 ** 9, np.int64)
        for b in range(B):
            for j in range(WIN):
                if c > 0:
                    idx[b * 1024 + j] = (c - 1) * 1024 + b * 512 + WIN + j
                if c < NCORES - 1:
                    idx[b * 1024 + 768 + j] = (c + 1) * 1024 + b * 512 + j
        sel = np.concatenate([idx[b * 1024 + r0: b * 1024 + r0 + 256]
                              for b in range(B) for r0 in (0, 768)])
        return np.ascontiguousarray(sel.astype(np.int32).reshape(8, 128).T)
    a_si = p.inp("l0h_sidx", [sidx(c) for c in range(NCORES)])
    sic = p.sbuf("l0h_sic", [128, 8], I32)
    xt = p.sbuf("l0h_xt", [128, D], BF16)
    zb = p.sbuf("l0h_zb", [128, 4 * D], BF16)
    hx = p.sbuf("l0h_hx", [128, 2 * WIN], F32)
    htl = p.sbuf("l0h_htl", [128, D], BF16)
    sidb = p.sbuf("l0h_idb", [128, 128], BF16)
    ptrb = p.psum("l0h_ptr", [128, 1024], BF16)
    a_idb = p.inp("identb", IDENT_BF)
    t0 = p.dma("sync", sic[:, :], a_si, ctx.ld, tag="l0h_si")
    tid = p.dma("sync", sidb[:, :], a_idb, ctx.ld, tag="l0h_id")
    def mset(e):
        return e.memset(zb[:, :], 0.0)
    tz = p.emit("gpsimd", mset, inc=[(ctx.gp, 1)], tag="l0h_z")
    zt = None
    for k0 in range(0, 64, 4):
        zt = p.dma("gpsimd", g_dram.ap()[k0 * 128:(k0 + 4) * 128, 0:D]
                   .rearrange("(k p) n -> p k n", p=128),
                   zb[:, :].rearrange("p (k n) -> p k n", k=4),
                   ctx.st, wait=[tz], tag=f"l0h_z{k0}")
    rows = [(b, r0) for b in range(B) for r0 in (0, 768)]
    prev_sc = None
    for i, (b, r0) in enumerate(rows):
        for half in range(2):
            tile = 2 * i + half
            tld = p.dma("sync", xt[:, :],
                        io["xn_tm"][b * 1024 + r0 + half * 128: b * 1024 + r0 + half * 128 + 128, 0:D],
                        ctx.ld, wait=[prev_sc] if prev_sc else [zt, t0], tag=f"l0h_l{tile}")
            def scat(e, tile=tile):
                return e.indirect_dma_start(
                    out=g_dram.ap()[0:8192, 0:D],
                    out_offset=IndirectOffsetOnAxis(sic[:, tile:tile + 1], 0),
                    in_=xt[:, :], in_offset=None,
                    bounds_check=8192, oob_is_err=False)
            prev_sc = p.emit("gpsimd", scat, wait=[tld, zt], inc=[(ctx.st, 16)],
                             tag=f"l0h_s{tile}", is_dma=True)
    p.barrier("l0h_scat")
    tcc = coll(p, ctx, "ReduceScatter", ALU.add, g_dram, h_dram, tag="l0h_rs")
    xnh = io["xnh"]
    cstage = p.sbuf("l0h_cst", [128, 8 * 512], F32)
    tprev = None
    for b in range(B):
        for n0 in range(0, 1024, 512):
            tld = p.dma("sync", cstage[:, :].rearrange("p (k n) -> p k n", k=8),
                        io["xn_fm"][0:D, b * 1024 + n0: b * 1024 + n0 + 512]
                        .rearrange("(k p) n -> p k n", p=128),
                        ctx.ld, wait=[tprev] if tprev else [], tag=f"l0h_c{b}{n0}")
            tprev = p.dma("gpsimd", xnh[b, 0:D, WIN + n0: WIN + n0 + 512]
                          .rearrange("(k p) n -> p k n", p=128),
                          cstage[:, :].rearrange("p (k n) -> p k n", k=8),
                          ctx.st, wait=[tld], tag=f"l0h_cs{b}{n0}")
    prev_h = None
    prev_dma = None
    for b in range(B):
        for side in range(2):
            col0 = 0 if side == 0 else (WIN + 1024)
            for half in range(2):
                r0 = b * 512 + side * 256 + half * 128
                tld = p.dma("sync", htl[:, :], h_dram.ap()[r0:r0 + 128, 0:D],
                            ctx.ld, wait=[tcc] + ([prev_h] if prev_h else []),
                            tag=f"l0h_h{b}{side}{half}")
                def trf(e):
                    ins = None
                    for kc in range(8):
                        ins = e.transpose(out=ptrb[:, kc * 128:(kc + 1) * 128],
                                          in_=htl[:, kc * 128:(kc + 1) * 128],
                                          identity=sidb[:, :])
                    return ins
                ttr = p.emit("tensor", trf, wait=[tld, tid] + ([prev_h] if prev_h else []),
                             inc=[(ctx.mm, 1)], tag=f"l0h_t{b}{side}{half}")
                for kc in range(8):
                    def ev1(e, kc=kc):
                        return e.tensor_copy(out=hx[:, 0:128],
                                             in_=ptrb[:, kc * 128:(kc + 1) * 128])
                    w = [ttr] + ([prev_dma] if prev_dma else [])
                    te = p.emit("vector", ev1, wait=w, inc=[(ctx.ve, 1)],
                                tag=f"l0h_e{b}{side}{half}{kc}")
                    prev_dma = p.dma("gpsimd",
                                     xnh[b, kc * 128:(kc + 1) * 128,
                                         col0 + half * 128: col0 + (half + 1) * 128],
                                     hx[:, 0:128], ctx.st, wait=[te],
                                     tag=f"l0h_hs{b}{side}{half}{kc}")
                prev_h = prev_dma
    p.barrier("l0h_done")
    p.sbuf_release(mk0)


def st_l0_slot(p, ctx, io, W, rsb, slotci, gkr):
    CAP = 1024
    posr, idxr, gater = rsb["posr"], rsb["idxrow"], rsb["gaterow"]
    mk0 = p.sbuf_mark()
    a_sel = p.inp("i_sel", [np.ascontiguousarray(
        (np.arange(NCORES) < c).astype(np.float32).reshape(NCORES, 1)) for c in range(NCORES)])
    a_notme = p.inp("notme", [np.ascontiguousarray(
        (np.arange(NCORES) != c).astype(np.float32).reshape(NCORES, 1)) for c in range(NCORES)])
    a_ones16 = p.inp("ones16", np.ones((16, 1), np.float32))
    a_onesr = p.inp("onesr", ONES_ROW)
    a_idf = p.inp("identf", np.eye(128, dtype=np.float32))
    a_idb = p.inp("identb", IDENT_BF)
    a_iota16 = p.inp("iota16", np.arange(16, dtype=np.float32).reshape(16, 1))

    cnts = p.sbuf("l0e_cnts", [NCORES, 32], F32)
    sel = p.sbuf("l0e_sel", [NCORES, 1], F32)
    notme = p.sbuf("l0e_nm", [NCORES, 1], F32)
    on16 = p.sbuf("l0e_on16", [16, 1], F32)
    onr = p.sbuf("l0e_onr", [1, 128], F32)
    sidf = p.sbuf("l0e_idf", [128, 128], F32)
    sidb = p.sbuf("l0e_idb", [128, 128], BF16)
    io16 = p.sbuf("l0e_io16", [16, 1], F32)
    offs = p.sbuf("l0e_offs", [16, 2], F32)
    slotc = p.sbuf("l0e_slotc", [128, 16], F32)
    psT = p.psum("l0e_psT", [128, 512], F32)
    mk1 = p.sbuf_mark()
    ohT = p.sbuf("l0e_ohT", [16, TOKC], F32)
    tmp16 = p.sbuf("l0e_tmp16", [16, TOKC], F32)
    offtok = p.sbuf("l0e_offtok", [1, TOKC], F32)
    slotr = ohT
    gkt = tmp16
    psS = p.psum("l0e_psS", [16, 512], F32)
    psR = p.psum("l0e_psR", [1, 512], F32)

    for s_ in range(2):
        tcl = p.dma("sync", cnts[0:NCORES, s_ * 16:(s_ + 1) * 16],
                    io["cnt_ag"][0:NCORES * 16, s_:s_ + 1]
                    .rearrange("(c e) x -> c (e x)", e=16),
                    ctx.ld, tag=f"l0e_cnt{s_}", allow_slow_non_contiguous=True)
    for ap, sb in ((a_sel, sel[:, :]), (a_notme, notme[:, :]),
                   (a_ones16, on16[:, :]), (a_onesr, onr[:, :]),
                   (a_idf, sidf[:, :]), (a_idb, sidb[:, :]), (a_iota16, io16[:, :])):
        p.dma("sync", sb, ap, ctx.ld, tag="l0e_c")
    p.barrier("l0e_ld")

    def offmm(e):
        e.matmul(out=psS[0:16, 0:1], lhsT=cnts[:, 0:16], rhs=sel[:, 0:1],
                 start=True, stop=True)
        e.matmul(out=psS[0:16, 1:2], lhsT=cnts[:, 0:16], rhs=notme[:, 0:1],
                 start=True, stop=False)
        return e.matmul(out=psS[0:16, 1:2], lhsT=cnts[:, 16:32], rhs=sel[:, 0:1],
                        start=False, stop=True)
    t = p.emit("tensor", offmm, inc=[(ctx.mm, 1)], tag="l0e_offmm")
    def offev(e):
        return e.tensor_copy(out=offs[:, :], in_=psS[0:16, 0:2])
    toff = p.emit("vector", offev, wait=[t], inc=[(ctx.ve, 1)], tag="l0e_offev")
    toh = None
    for n0 in range(0, TOKC, 512):
        def ibc(e, n0=n0):
            return e.matmul(out=psS[0:16, 0:512], lhsT=onr[0:1, 0:16],
                            rhs=idxr[0:1, n0:n0 + 512], start=True, stop=True)
        w = ([toh] if toh else []) + [toff]
        tib = p.emit("tensor", ibc, wait=w, inc=[(ctx.mm, 1)], tag=f"l0e_ibc{n0}")
        def ohfn(e, n0=n0):
            return e.tensor_scalar(out=ohT[:, n0:n0 + 512], in0=psS[0:16, 0:512],
                                   scalar1=io16[0:16, 0:1], scalar2=None, op0=ALU.is_equal)
        toh = p.emit("vector", ohfn, wait=[tib], inc=[(ctx.ve, 1)], tag=f"l0e_oh{n0}")
    def otm(e):
        e.tensor_scalar(out=tmp16[:, 0:SLOC], in0=ohT[:, 0:SLOC],
                        scalar1=offs[:, 0:1], scalar2=None, op0=ALU.mult)
        return e.tensor_scalar(out=tmp16[:, SLOC:TOKC], in0=ohT[:, SLOC:TOKC],
                               scalar1=offs[:, 1:2], scalar2=None, op0=ALU.mult)
    tot = p.emit("vector", otm, wait=[toh], inc=[(ctx.ve, 1)], tag="l0e_otm")
    toe = None
    for n0 in range(0, TOKC, 512):
        def omm(e, n0=n0):
            return e.matmul(out=psR[0:1, 0:512], lhsT=on16[0:16, 0:1],
                            rhs=tmp16[:, n0:n0 + 512], start=True, stop=True)
        w = [tot] + ([toe] if toe else [])
        tom = p.emit("tensor", omm, wait=w, inc=[(ctx.mm, 1)], tag=f"l0e_omm{n0}")
        def oev(e, n0=n0):
            return e.tensor_copy(out=offtok[0:1, n0:n0 + 512], in_=psR[0:1, 0:512])
        toe = p.emit("vector", oev, wait=[tom], inc=[(ctx.ve, 1)], tag=f"l0e_oev{n0}")
    def pg1(e):
        e.tensor_tensor(out=posr[0:1, :], in0=posr[0:1, :], in1=offtok[0:1, :], op=ALU.add)
        return e.tensor_scalar(out=posr[0:1, :], in0=posr[0:1, :], scalar1=-1.0,
                               scalar2=None, op0=ALU.add)
    tpg = p.emit("vector", pg1, wait=[toe], inc=[(ctx.ve, 1)], tag="l0e_pg")
    def kp1(e):
        return e.tensor_scalar(out=offtok[0:1, :], in0=posr[0:1, :], scalar1=float(CAP),
                               scalar2=None, op0=ALU.is_lt)
    tkp = p.emit("vector", kp1, wait=[tpg], inc=[(ctx.ve, 1)], tag="l0e_kp")
    def sl1(e):
        e.tensor_scalar(out=slotr[0:1, :], in0=idxr[0:1, :], scalar1=float(CAP),
                        scalar2=None, op0=ALU.mult)
        return e.tensor_tensor(out=slotr[0:1, :], in0=slotr[0:1, :], in1=posr[0:1, :],
                               op=ALU.add)
    tsl = p.emit("vector", sl1, wait=[tkp], inc=[(ctx.ve, 1)], tag="l0e_sl")
    def sl2(e):
        e.tensor_scalar(out=gkt[0:1, :], in0=offtok[0:1, :], scalar1=-1e9, scalar2=1e9,
                        op0=ALU.mult, op1=ALU.add)
        return e.tensor_tensor(out=slotr[0:1, :], in0=slotr[0:1, :], in1=gkt[0:1, :],
                               op=ALU.add)
    tsl2 = p.emit("vector", sl2, wait=[tsl], inc=[(ctx.ve, 1)], tag="l0e_sl2")
    def gk1(e):
        return e.tensor_tensor(out=gkr[0:1, :], in0=gater[0:1, :], in1=offtok[0:1, :],
                               op=ALU.mult)
    tgk = p.emit("vector", gk1, wait=[tsl2], inc=[(ctx.ve, 1)], tag="l0e_gk")
    trs = [None] * 16
    for t_ in range(16):
        def trfn(e, t_=t_):
            return e.transpose(out=psT[:, (t_ % 4):(t_ % 4) + 1],
                               in_=slotr[0:1, t_ * 128:(t_ + 1) * 128],
                               identity=sidf[0:1, 0:1])
        w = [tsl2, tgk] + ([trs[t_ - 4]] if t_ >= 4 and trs[t_ - 4] else [])
        tt = p.emit("tensor", trfn, wait=w, inc=[(ctx.mm, 1)], tag=f"l0e_str{t_}")
        def evfn(e, t_=t_):
            return e.tensor_copy(out=slotc[:, t_:t_ + 1], in_=psT[:, (t_ % 4):(t_ % 4) + 1])
        trs[t_] = p.emit("vector", evfn, wait=[tt], inc=[(ctx.ve, 1)], tag=f"l0e_sev{t_}")
    def sli(e):
        return e.tensor_copy(out=slotci[:, :], in_=slotc[:, :])
    t2 = p.emit("vector", sli, wait=[trs[15]], inc=[(ctx.ve, 1)], tag="l0e_sli")
    p.barrier("l0e_slot")

    slab_src, slab_rs = io["slab_src"], io["slab_rs"]
    zb = p.sbuf("l0e_zb", [128, 4 * D], BF16)
    xnt = p.sbuf("l0e_xnt", [128, 16 * D], BF16)
    def mset(e):
        return e.memset(zb[:, :], 0.0)
    tz = p.emit("gpsimd", mset, inc=[(ctx.gp, 1)], tag="l0e_z")
    zt = None
    for k0 in range(0, 128, 4):
        zt = p.dma("gpsimd", slab_src.ap()[k0 * 128:(k0 + 4) * 128, 0:D]
                   .rearrange("(k p) n -> p k n", p=128),
                   zb[:, :].rearrange("p (k n) -> p k n", k=4),
                   ctx.st, wait=[tz], tag=f"l0e_z{k0}")
    tld = p.dma("sync", xnt[:, :].rearrange("p (k n) -> p k n", k=16),
                io["xn_tm"][0:TOKC, 0:D].rearrange("(k p) n -> p k n", p=128),
                ctx.ld, tag="l0e_xl")
    prev_sc = [t2, tz, zt, tld]
    for t_ in range(16):
        def scat(e, t_=t_):
            return e.indirect_dma_start(
                out=slab_src.ap()[0:TOKG, 0:D],
                out_offset=IndirectOffsetOnAxis(slotci[:, t_:t_ + 1], 0),
                in_=xnt[:, t_ * D:(t_ + 1) * D], in_offset=None,
                bounds_check=TOKG, oob_is_err=False)
        tsc = p.emit("gpsimd", scat, wait=prev_sc, inc=[(ctx.st, 16)],
                     tag=f"l0e_sc{t_}", is_dma=True)
        prev_sc = [tsc]
    p.barrier("l0e_scat")
    tcc = coll(p, ctx, "ReduceScatter", ALU.add, slab_src, slab_rs, tag="l0e_rs")
    p.barrier("l0e_rsb")
    p.sbuf_release(mk1)
    p.sbuf_release(mk0)


def st_l0_expert(p, ctx, io, W):
    slab_rs = io["slab_rs"]
    mk0 = p.sbuf_mark()
    a_w1 = p.inp("l0e_w1", [np.ascontiguousarray(
        np.asarray(W["L0_ffn_W1"][2 * c:2 * c + 2]).astype(NBF)) for c in range(NCORES)])
    a_b1 = p.inp("l0e_b1", [np.ascontiguousarray(
        W["L0_ffn_b1"][2 * c:2 * c + 2].reshape(2, 32, 128).transpose(0, 2, 1))
        for c in range(NCORES)])
    a_w2 = p.inp("l0e_w2", [np.ascontiguousarray(
        np.asarray(W["L0_ffn_W2"][2 * c:2 * c + 2]).astype(NBF)) for c in range(NCORES)])
    a_b2 = p.inp("l0e_b2", [W["L0_ffn_b2"][2 * c:2 * c + 2] for c in range(NCORES)])
    a_onesr = p.inp("onesr", ONES_ROW)
    a_idb = p.inp("identb", IDENT_BF)
    onr = p.sbuf("l0x_onr", [1, 128], F32)
    sidb = p.sbuf("l0x_idb", [128, 128], BF16)
    psT = p.psum("l0x_psT", [128, 512], F32)
    t_o = p.dma("sync", onr[:, :], a_onesr, ctx.ld, tag="l0x_or")
    t_i = p.dma("sync", sidb[:, :], a_idb, ctx.ld, tag="l0x_id")
    tcc = t_i

    o_y = io["y_local"]
    stage = p.sbuf("l0e_stage", [128, 8 * 512], F32)
    w1b = p.sbuf("l0e_w1b", [128, 8 * 4096], BF16)
    w2b = p.sbuf("l0e_w2b", [128, 32 * 1024], BF16)
    b1s = p.sbuf("l0e_b1s", [128, 32], F32)
    b2b = p.sbuf("l0e_b2b", [128, D], F32)
    b2row = None
    buf_tm = p.sbuf("l0e_btm", [128, 4 * D], BF16)
    buf_fm = p.sbuf("l0e_bfm", [128, 8 * 512], BF16)
    h_bf = p.sbuf("l0e_hbf", [128, 32 * 512], BF16)
    psE = p.psum("l0e_psE", [128, 512], F32)
    psF = p.psum("l0e_psF", [128, 512], F32)
    ptrb = p.psum("l0e_ptrb", [128, 1024], BF16)
    psY = p.psum("l0e_psY", [128, 512], F32)

    prev_dma = None
    prev_hts = None
    for e_i in range(2):
        wt = load_weight_bf16(p, ctx, a_w1[e_i, 0:D, 0:4096], D, 4096, f"l0ew1_{e_i}",
                              w1b, stage, prev=prev_dma)
        wt2 = load_weight_bf16(p, ctx, a_w2[e_i, 0:4096, 0:D], 4096, D, f"l0ew2_{e_i}",
                               w2b, stage, prev=wt)
        wdep = [prev_dma] if prev_dma else []
        tb1 = p.dma("sync", b1s[:, :], a_b1[e_i, 0:128, 0:32], ctx.ld, wait=wdep, tag="l0e_b1")
        tb2 = p.dma("sync", stage[0:1, 0:D], a_b2[e_i:e_i + 1, 0:D], ctx.ld,
                    wait=[wt2], tag="l0e_b2")
        tbb = ones_bcast_rows(p, ctx, onr, stage, psT, 512, wait=[tb2, t_o] + wdep, nrows=128)
        def bbf(e):
            return e.tensor_copy(out=b2b[:, 0:512], in_=psT[:, 0:512])
        tb3 = p.emit("vector", bbf, wait=[tbb], inc=[(ctx.ve, 1)], tag="l0e_b2e0")
        tbb2 = ones_bcast_rows(p, ctx, onr, stage, psT, 512, row_off=512, wait=[tb3], nrows=128)
        def bbf2(e):
            return e.tensor_copy(out=b2b[:, 512:1024], in_=psT[:, 0:512])
        tb4 = p.emit("vector", bbf2, wait=[tbb2], inc=[(ctx.ve, 1)], tag="l0e_b2e1")
        for g in range(2):
            g_toks = []
            for j in range(4):
                r0 = e_i * 1024 + g * 512 + j * 128
                def gat(e, j=j, r0=r0):
                    return e.dma_start(out=buf_tm[:, j * D:(j + 1) * D],
                                      in_=slab_rs.ap()[r0:r0 + 128, 0:D])
                w = [wt2, tb1, tb4, tcc] + ([prev_dma] if prev_dma else [])
                g_toks.append(p.emit("sync", gat, wait=w, inc=[(ctx.ld, 16)],
                                     tag=f"l0e_g{e_i}{g}{j}", is_dma=True))
            tr_toks = []
            prev_tr2 = None
            for j in range(4):
                def trf(e, j=j):
                    ins = None
                    for kc in range(8):
                        ins = e.transpose(out=ptrb[:, kc * 128:(kc + 1) * 128],
                                          in_=buf_tm[:, j * D + kc * 128: j * D + kc * 128 + 128],
                                          identity=sidb[:, :])
                    return ins
                w = [g_toks[j]] + ([prev_tr2] if prev_tr2 else [])
                ttj = p.emit("tensor", trf, wait=w, inc=[(ctx.mm, 1)], tag=f"l0e_bt{j}")
                def tev(e, j=j):
                    return e.tensor_copy(
                        out=buf_fm[:, 0:8 * 512].rearrange("p (k n) -> p k n", k=8)[:, :, j * 128:(j + 1) * 128],
                        in_=ptrb[:, 0:1024].rearrange("p (k n) -> p k n", k=8))
                prev_tr2 = p.emit("vector", tev, wait=[ttj], inc=[(ctx.ve, 1)], tag=f"l0e_bte{j}")
                tr_toks.append(prev_tr2)
            w_h = [tr_toks[-1], wt, tb1] + ([prev_hts] if prev_hts else [])
            hts = linear_fm(p, ctx, wsb=w1b, xsb=buf_fm, osb=None, obf=h_bf,
                            kch=8, fch=32, ntok=512, psums=[psE, psF],
                            wait=w_h, bias=b1s, act=AF.Relu,
                            evac_eng="scalar")
            prev_hts = hts[-1]
            prev_yev = None
            yts = []
            for j in range(4):
                for n0 in (0, 512):
                    def ymm(e, j=j, n0=n0):
                        ins = None
                        for fc in range(32):
                            ins = e.matmul(out=psY[:, 0:512],
                                           lhsT=h_bf[:, fc * 512 + j * 128: fc * 512 + j * 128 + 128],
                                           rhs=w2b[:, fc * 1024 + n0: fc * 1024 + n0 + 512],
                                           start=(fc == 0), stop=(fc == 31))
                        return ins
                    w = [hts[-1], wt2] + ([prev_yev] if prev_yev else [])
                    tym = p.emit("tensor", ymm, wait=w, inc=[(ctx.mm, 1)], tag=f"l0e_ym{j}_{n0}")
                    def yev(e, j=j, n0=n0):
                        return e.tensor_tensor(out=buf_tm[:, j * D + n0: j * D + n0 + 512],
                                               in0=psY[:, 0:512], in1=b2b[:, n0:n0 + 512],
                                               op=ALU.add)
                    prev_yev = p.emit("vector", yev, wait=[tym, tb4], inc=[(ctx.ve, 1)],
                                      tag=f"l0e_ye{j}_{n0}")
                    yts.append(prev_yev)
            prev_dma = p.dma("gpsimd",
                             o_y[e_i * 1024 + g * 512: e_i * 1024 + (g + 1) * 512, 0:D]
                             .rearrange("(t p) d -> p t d", p=128),
                             buf_tm[:, :].rearrange("p (t d) -> p t d", t=4),
                             ctx.st, wait=[yts[-1]], tag=f"l0e_ys{e_i}_{g}")
    p.barrier("l0e_done")
    p.sbuf_release(mk0)


def dump_dram(p, ctx, src_ap, name, shape, dtype=F32, nch=None):
    o = p.out(name, shape, dtype)
    rows = shape[0]
    cols = int(np.prod(shape[1:]))
    sb = p.sbuf(f"dump_{name}", [128, min(cols, 8192)], dtype)
    prev = None
    for r0 in range(0, rows, 128):
        rr = min(128, rows - r0)
        for c0 in range(0, cols, 8192):
            cc2 = min(8192, cols - c0)
            t1 = p.dma("sync", sb[0:rr, 0:cc2], src_ap[r0:r0 + rr, c0:c0 + cc2],
                       ctx.ld, wait=[prev] if prev else [], tag=f"dmp_{name}_{r0}_{c0}")
            prev = p.dma("gpsimd", o[r0:r0 + rr, c0:c0 + cc2], sb[0:rr, 0:cc2],
                         ctx.st, wait=[t1], tag=f"dmps_{name}_{r0}_{c0}")
    p.barrier(f"dump_{name}")
    return o


FDBG = []


def st_l1_projh(p, ctx, io, W):
    mk0 = p.sbuf_mark()
    def wsl(name):
        def arr(c):
            w = W[name][128 * c:128 * (c + 1), :].T.astype(NBF)
            return np.ascontiguousarray(
                w.reshape(8, 128, 128).transpose(1, 0, 2).reshape(128, 1024))
        return [arr(c) for c in range(NCORES)]
    a_wq = p.inp("l1_wqh", wsl("L1_attn_Wq"))
    a_wk = p.inp("l1_wkh", wsl("L1_attn_Wk"))
    a_wv = p.inp("l1_wvh", wsl("L1_attn_Wv"))
    a_idf = p.inp("identf", np.eye(128, dtype=np.float32))
    xnag = io["xnag"]

    wqh = p.sbuf("p1_wq", [128, 1024], BF16)
    wkh = p.sbuf("p1_wk", [128, 1024], BF16)
    wvh = p.sbuf("p1_wv", [128, 1024], BF16)
    idf = p.sbuf("p1_idf", [128, 128], F32)
    xblk = [p.sbuf("p1_xb0", [128, 8 * 1024], BF16),
            p.sbuf("p1_xb1", [128, 8 * 1024], BF16)]
    otile = p.sbuf("p1_ot", [128, 128], BF16)
    otile2 = p.sbuf("p1_ot2", [128, 128], BF16)
    ktile = p.sbuf("p1_kt", [128, 128], F32)
    kbf = p.sbuf("p1_kbf", [128, 128], BF16)
    sqs = p.sbuf("p1_sq", [128, 128], F32)
    nrm = p.sbuf("p1_nrm", [128, 2], F32)
    kbt = p.sbuf("p1_kbt", [128, 2], F32)
    kbT = p.sbuf("p1_kbT", [2, 128], F32)
    psA = p.psum("p1_psA", [128, 512], F32)
    psB = p.psum("p1_psB", [128, 512], F32)
    psK = p.psum("p1_psK", [2, 128], F32)

    tw = p.dma("sync", wqh[:, :], a_wq, ctx.ld, tag="p1wq")
    tw = p.dma("sync", wkh[:, :], a_wk, ctx.ld, wait=[], tag="p1wk")
    tw = p.dma("sync", wvh[:, :], a_wv, ctx.ld, wait=[], tag="p1wv")
    tw = p.dma("sync", idf[:, :], a_idf, ctx.ld, wait=[], tag="p1id")
    p.barrier("p1_w")

    lastmm = {0: None, 1: None}
    prev_ld = None
    prev_st = {}
    prev_kb = None
    for b in range(B):
        for cp in range(NCORES):
            blk = b * NCORES + cp
            par = blk % 2
            xb = xblk[par]
            wld = ([lastmm[par]] if lastmm[par] else []) + ([prev_ld] if prev_ld else [])
            prev_ld = p.dma("sync", xb[:, :].rearrange("p (k n) -> p k n", k=8),
                            xnag[cp * 1024:(cp + 1) * 1024, b * 1024:(b + 1) * 1024]
                            .rearrange("(k p) n -> p k n", p=128),
                            ctx.ld, wait=wld, tag=f"p1x{blk}")
            for j in range(8):
                g = b * 64 + cp * 8 + j
                for wi, (which, wsb) in enumerate((("q", wqh), ("v", wvh), ("k", wkh))):
                    ps = [psA, psB][wi % 2]
                    def mmf(e, xb=xb, wsb=wsb, j=j, ps=ps):
                        ins = None
                        for kc in range(8):
                            ins = e.matmul(out=ps[:, 0:128],
                                           lhsT=xb[:, kc * 1024 + j * 128: kc * 1024 + (j + 1) * 128],
                                           rhs=wsb[:, kc * 128:(kc + 1) * 128],
                                           start=(kc == 0), stop=(kc == 7))
                        return ins
                    w = [prev_ld] + ([prev_st.get(ps is psA)] if prev_st.get(ps is psA) else [])
                    tm_ = p.emit("tensor", mmf, wait=w, inc=[(ctx.mm, 1)],
                                 tag=f"p1m{g}{which}")
                    lastmm[par] = tm_
                    if which == "k":
                        def kev(e, ps=ps):
                            e.tensor_copy(out=ktile[:, :], in_=ps[:, 0:128])
                            return e.tensor_copy(out=kbf[:, :], in_=ps[:, 0:128])
                        tev = p.emit("vector", kev, wait=[tm_], inc=[(ctx.ve, 1)],
                                     tag=f"p1ke{g}")
                        tst = p.dma("gpsimd", io["k"][g * 128:(g + 1) * 128, 0:128],
                                    kbf[:, :], ctx.st, wait=[tev], tag=f"p1ks{g}")
                        prev_st[ps is psA] = tst
                        def sqf(e):
                            return e.square(out=sqs[:, :], in_=ktile[:, :])
                        t1 = p.emit("scalar", sqf, wait=[tev], inc=[(ctx.sc, 1)], tag=f"p1sq{g}")
                        def red(e):
                            ins = None
                            for hl in range(2):
                                e.tensor_reduce(out=nrm[:, hl:hl + 1],
                                                in_=sqs[:, hl * 64:(hl + 1) * 64],
                                                axis=mybir.AxisListType.X, op=ALU.add)
                                ins = e.tensor_reduce(out=kbt[:, hl:hl + 1],
                                                      in_=ktile[:, hl * 64:(hl + 1) * 64],
                                                      axis=mybir.AxisListType.X, op=ALU.add)
                            return ins
                        t2 = p.emit("vector", red, wait=[t1], inc=[(ctx.ve, 1)], tag=f"p1rd{g}")
                        def srt(e):
                            return e.sqrt(out=nrm[:, :], in_=nrm[:, :])
                        t3 = p.emit("scalar", srt, wait=[t2], inc=[(ctx.sc, 1)], tag=f"p1sr{g}")
                        def mx(e):
                            return e.tensor_scalar(out=nrm[:, :], in0=nrm[:, :],
                                                   scalar1=1e-12, scalar2=None, op0=ALU.max)
                        t4 = p.emit("vector", mx, wait=[t3], inc=[(ctx.ve, 1)], tag=f"p1mx{g}")
                        def rc(e):
                            return e.reciprocal(out=nrm[:, :], in_=nrm[:, :])
                        t5 = p.emit("vector", rc, wait=[t4], inc=[(ctx.ve, 1)], tag=f"p1rc{g}")
                        def fin2(e):
                            e.tensor_tensor(out=kbt[:, :], in0=kbt[:, :], in1=nrm[:, :],
                                            op=ALU.mult)
                            return e.tensor_scalar(out=kbt[:, :], in0=kbt[:, :],
                                                   scalar1=1.0 / 64, scalar2=None,
                                                   op0=ALU.mult)
                        t6 = p.emit("vector", fin2, wait=[t5], inc=[(ctx.ve, 1)], tag=f"p1f{g}")
                        def ktr(e):
                            return e.transpose(out=psK[0:2, 0:128], in_=kbt[:, :],
                                               identity=idf[:, :])
                        w = [t6] + ([prev_kb] if prev_kb else [])
                        t7 = p.emit("tensor", ktr, wait=w, inc=[(ctx.mm, 1)], tag=f"p1kt{g}")
                        lastmm[par] = t7
                        def kte(e):
                            return e.tensor_copy(out=kbT[:, :], in_=psK[0:2, 0:128])
                        t8 = p.emit("vector", kte, wait=[t7], inc=[(ctx.ve, 1)], tag=f"p1kc{g}")
                        prev_kb = p.dma("gpsimd", io["kb3"][0:2, g * 128:(g + 1) * 128],
                                        kbT[:, :], ctx.st, wait=[t8], tag=f"p1kbs{g}")
                    else:
                        otl = otile if which == "q" else otile2
                        def oev(e, ps=ps, otl=otl):
                            return e.tensor_copy(out=otl[:, :], in_=ps[:, 0:128])
                        tev = p.emit("vector", oev, wait=[tm_], inc=[(ctx.ve, 1)],
                                     tag=f"p1e{g}{which}")
                        tst = p.dma("gpsimd", io[which][g * 128:(g + 1) * 128, 0:128],
                                    otl[:, :], ctx.st, wait=[tev], tag=f"p1s{g}{which}")
                        prev_st[ps is psA] = tst
    p.barrier("p1_done")
    p.sbuf_release(mk0)


def st_slab_tm_to_fm(p, ctx, src_ap, dst_ap, name, add_ap=None):
    mk0 = p.sbuf_mark()
    a_idb = p.inp("identb", IDENT_BF)
    idb = p.sbuf(f"tf_{name}_id", [128, 128], BF16)
    tbuf = p.sbuf(f"tf_{name}_tb", [128, 1024], BF16)
    fmsb = p.sbuf(f"tf_{name}_fm", [128, 1024], F32)
    xnt = p.sbuf(f"tf_{name}_xn", [128, 1024], F32) if add_ap is not None else None
    psT = p.psum(f"tf_{name}_ps", [128, 1024], BF16)
    tid = p.dma("sync", idb[:, :], a_idb, ctx.ld, tag=f"tf{name}id")
    prev_ev = None
    prev_st = None
    for t in range(16):
        w = [tid] + ([prev_ev] if prev_ev else [])
        tld = p.dma("sync", tbuf[:, :], src_ap[t * 128:(t + 1) * 128, 0:D],
                    ctx.ld, wait=w, tag=f"tf{name}l{t}")
        if add_ap is not None:
            txl = p.dma("sync", xnt[:, :].rearrange("p (k n) -> p k n", k=8),
                        add_ap[0:D, t * 128:(t + 1) * 128].rearrange("(k p) n -> p k n", p=128),
                        ctx.ld, tag=f"tf{name}x{t}")
        def trf(e):
            ins = None
            for kc in range(8):
                ins = e.transpose(out=psT[:, kc * 128:(kc + 1) * 128],
                                  in_=tbuf[:, kc * 128:(kc + 1) * 128],
                                  identity=idb[:, :])
            return ins
        ttr = p.emit("tensor", trf, wait=[tld] + ([prev_ev] if prev_ev else []),
                     inc=[(ctx.mm, 1)], tag=f"tf{name}t{t}")
        def tev(e):
            return e.tensor_copy(out=fmsb[:, :], in_=psT[:, 0:1024])
        w = [ttr] + ([prev_st] if prev_st else [])
        tve = p.emit("vector", tev, wait=w, inc=[(ctx.ve, 1)], tag=f"tf{name}e{t}")
        if add_ap is not None:
            def tad(e):
                return e.tensor_tensor(out=fmsb[:, :], in0=fmsb[:, :], in1=xnt[:, :],
                                       op=ALU.add)
            tve = p.emit("vector", tad, wait=[tve, txl], inc=[(ctx.ve, 1)],
                         tag=f"tf{name}a{t}")
        prev_ev = tve
        prev_st = p.dma("gpsimd",
                        dst_ap[0:D, t * 128:(t + 1) * 128].rearrange("(k p) n -> p k n", p=128),
                        fmsb[:, :].rearrange("p (k n) -> p k n", k=8),
                        ctx.st, wait=[tve], tag=f"tf{name}s{t}")
    p.barrier(f"tf_{name}")
    p.sbuf_release(mk0)


def st_l3_pre2(p, ctx, io, W):
    mk0 = p.sbuf_mark()
    def wsl(c0):
        def arr(c):
            w = W["L3_attn_W_in"][:, c0 + 256 * c: c0 + 256 * (c + 1)].astype(NBF)
            return np.ascontiguousarray(
                w.reshape(8, 128, 256).transpose(1, 0, 2).reshape(128, 2048))
        return [arr(c) for c in range(NCORES)]
    a_wu = p.inp("l3_wu", wsl(0))
    a_wz = p.inp("l3_wz", wsl(2048))
    xnag = io["xnag"]
    wub = p.sbuf("p3_wu", [128, 2048], BF16)
    wzb = p.sbuf("p3_wz", [128, 2048], BF16)
    xblk = [p.sbuf("p3_xb0", [128, 8 * 1024], BF16),
            p.sbuf("p3_xb1", [128, 8 * 1024], BF16)]
    usl = p.sbuf("p3_usl", [128, 512], BF16)
    zsl = p.sbuf("p3_zsl", [128, 512], BF16)
    psA = p.psum("p3_psA", [128, 512], F32)
    psB = p.psum("p3_psB", [128, 512], F32)
    p.dma("sync", wub[:, :], a_wu, ctx.ld, tag="p3wu")
    p.dma("sync", wzb[:, :], a_wz, ctx.ld, tag="p3wz")
    p.barrier("p3_w")

    lastmm = {0: None, 1: None}
    prev_ld = None
    prev_st = {}
    for b in range(B):
        for cp in range(NCORES):
            blk = b * NCORES + cp
            par = blk % 2
            xb = xblk[par]
            wld = ([lastmm[par]] if lastmm[par] else []) + ([prev_ld] if prev_ld else [])
            prev_ld = p.dma("sync", xb[:, :].rearrange("p (k n) -> p k n", k=8),
                            xnag[cp * 1024:(cp + 1) * 1024, b * 1024:(b + 1) * 1024]
                            .rearrange("(k p) n -> p k n", p=128),
                            ctx.ld, wait=wld, tag=f"p3x{blk}")
            gc0 = b * 8192 + cp * 1024
            for sl in range(2):
                for chk in range(2):
                    for wi, (wsb, ob, outap) in enumerate(
                            ((wub, usl, io["u"]), (wzb, zsl, io["z"]))):
                        ps = [psA, psB][wi]
                        def mmf(e, xb=xb, wsb=wsb, sl=sl, chk=chk, ps=ps):
                            ins = None
                            for kc in range(8):
                                ins = e.matmul(
                                    out=ps[:, 0:512],
                                    lhsT=wsb[:, kc * 256 + chk * 128: kc * 256 + (chk + 1) * 128],
                                    rhs=xb[:, kc * 1024 + sl * 512: kc * 1024 + sl * 512 + 512],
                                    start=(kc == 0), stop=(kc == 7))
                            return ins
                        key = wi
                        w = [prev_ld] + ([prev_st.get(key)] if prev_st.get(key) else [])
                        tm_ = p.emit("tensor", mmf, wait=w, inc=[(ctx.mm, 1)],
                                     tag=f"p3m{blk}{sl}{chk}{wi}")
                        lastmm[par] = tm_
                        if wi == 0:
                            def ev(e, ps=ps, ob=ob):
                                return e.tensor_copy(out=ob[:, :], in_=ps[:, 0:512])
                        else:
                            def ev(e, ps=ps, ob=ob):
                                return e.activation(out=ob[:, :], in_=ps[:, 0:512],
                                                    func=AF.Silu)
                        eng = "vector" if wi == 0 else "scalar"
                        sem = ctx.ve if wi == 0 else ctx.sc
                        tev = p.emit(eng, ev, wait=[tm_], inc=[(sem, 1)],
                                     tag=f"p3e{blk}{sl}{chk}{wi}")
                        prev_st[key] = p.dma(
                            "gpsimd",
                            outap[chk * 128:(chk + 1) * 128, gc0 + sl * 512: gc0 + sl * 512 + 512],
                            ob[:, :], ctx.st, wait=[tev], tag=f"p3s{blk}{sl}{chk}{wi}")
    p.barrier("p3_done")
    p.sbuf_release(mk0)


def st_l3_conv(p, ctx, io, W):
    mk0 = p.sbuf_mark()
    a_cw = p.inp("l3_cw", [np.ascontiguousarray(
        W["L3_attn_conv_w"][256 * c:256 * (c + 1)].reshape(2, 128, 4)
        .transpose(1, 0, 2).reshape(128, 8)) for c in range(NCORES)])
    a_cb = p.inp("l3_cb", [np.ascontiguousarray(
        W["L3_attn_conv_b"][256 * c:256 * (c + 1)].reshape(2, 128).T)
        for c in range(NCORES)])
    a_wxp = p.inp("l3_wxp", [np.ascontiguousarray(
        W["L3_attn_W_xproj"][256 * c:256 * (c + 1), :].astype(NBF)
        .reshape(2, 128, 96).transpose(1, 0, 2).reshape(128, 192))
        for c in range(NCORES)])
    cw = p.sbuf("c3_cw", [128, 8], F32)
    cb = p.sbuf("c3_cb", [128, 2], F32)
    wxp = p.sbuf("c3_wxp", [128, 192], BF16)
    us = p.sbuf("c3_us", [128, SEQ + 3], BF16)
    acc = p.sbuf("c3_acc", [128, SEQ], F32)
    ucb = p.sbuf("c3_ucb", [128, 2 * SEQ], BF16)
    xsl = p.sbuf("c3_xsl", [96, 512], F32)
    psX = p.psum("c3_psX", [96, 512], F32)
    p.dma("sync", cw[:, :], a_cw, ctx.ld, tag="c3cw")
    p.dma("sync", cb[:, :], a_cb, ctx.ld, tag="c3cb")
    p.dma("sync", wxp[:, :], a_wxp, ctx.ld, tag="c3wx")
    p.barrier("c3_w")

    prev_b = None
    prev_x = None
    for b in range(B):
        prev_sil = None
        for chk in range(2):
            wld = [prev_b] if prev_b else []
            tld = p.dma("sync", us[:, 3:3 + SEQ],
                        io["u"][chk * 128:(chk + 1) * 128, b * SEQ:(b + 1) * SEQ],
                        ctx.ld, wait=wld, tag=f"c3u{b}{chk}")
            def zf(e):
                return e.memset(us[:, 0:3], 0.0)
            tz = p.emit("vector", zf, wait=wld, inc=[(ctx.ve, 1)], tag=f"c3z{b}{chk}")
            def c0(e, chk=chk):
                return e.tensor_scalar(out=acc[:, :], in0=us[:, 0:SEQ],
                                       scalar1=cw[:, chk * 4:chk * 4 + 1],
                                       scalar2=None, op0=ALU.mult)
            t1 = p.emit("vector", c0, wait=[tld, tz], inc=[(ctx.ve, 1)], tag=f"c3c0{b}{chk}")
            for k in range(1, 4):
                def ck(e, chk=chk, k=k):
                    return e.scalar_tensor_tensor(
                        out=acc[:, :], in0=us[:, k:k + SEQ],
                        scalar=cw[:, chk * 4 + k:chk * 4 + k + 1], in1=acc[:, :],
                        op0=ALU.mult, op1=ALU.add)
                t1 = p.emit("vector", ck, wait=[t1], inc=[(ctx.ve, 1)], tag=f"c3c{k}{b}{chk}")
            def sil(e, chk=chk):
                return e.activation(out=ucb[:, chk * SEQ:(chk + 1) * SEQ],
                                    in_=acc[:, :], func=AF.Silu, bias=cb[:, chk:chk + 1])
            prev_sil = p.emit("scalar", sil, wait=[t1], inc=[(ctx.sc, 1)], tag=f"c3s{b}{chk}")
            prev_b = p.dma("gpsimd",
                           io["uc"][chk * 128:(chk + 1) * 128, b * SEQ:(b + 1) * SEQ],
                           ucb[:, chk * SEQ:(chk + 1) * SEQ], ctx.st,
                           wait=[prev_sil], tag=f"c3uo{b}{chk}")
        for sl in range(16):
            def xmf(e, sl=sl):
                ins = None
                for chk in range(2):
                    ins = e.matmul(out=psX[:, 0:512],
                                   lhsT=wxp[:, chk * 96:(chk + 1) * 96],
                                   rhs=ucb[:, chk * SEQ + sl * 512: chk * SEQ + sl * 512 + 512],
                                   start=(chk == 0), stop=(chk == 1))
                return ins
            w = [prev_sil] + ([prev_x] if prev_x else [])
            t2 = p.emit("tensor", xmf, wait=w, inc=[(ctx.mm, 1)], tag=f"c3x{b}{sl}")
            def xev(e):
                return e.tensor_copy(out=xsl[:, :], in_=psX[0:96, 0:512])
            t3 = p.emit("vector", xev, wait=[t2], inc=[(ctx.ve, 1)], tag=f"c3xe{b}{sl}")
            prev_x = p.dma("gpsimd",
                           io["xdbl"][0:96, b * SEQ + sl * 512: b * SEQ + sl * 512 + 512],
                           xsl[:, :], ctx.st, wait=[t3], tag=f"c3xs{b}{sl}")
    p.barrier("c3_done")
    p.sbuf_release(mk0)


def st_l3_delta(p, ctx, io, W):
    mk0 = p.sbuf_mark()
    ccol, tcc = load_consts(p, ctx)
    a_wdt = p.inp("l3_wdt", [np.ascontiguousarray(
        W["L3_attn_W_dt"][:, 256 * c:256 * (c + 1)].astype(NBF)) for c in range(NCORES)])
    a_bdt = p.inp("l3_bdt", [np.ascontiguousarray(
        W["L3_attn_b_dt"][256 * c:256 * (c + 1)].reshape(2, 128).T) for c in range(NCORES)])
    wdt = p.sbuf("d3_wdt", [64, 256], BF16)
    bdt = p.sbuf("d3_bdt", [128, 2], F32)
    stage = p.sbuf("d3_stg", [64, 4096], F32)
    dtb = p.sbuf("d3_dtb", [64, TOKG], BF16)
    raws = p.sbuf("d3_raw", [128, 512], F32)
    expt = p.sbuf("d3_exp", [128, 512], F32)
    lnt = p.sbuf("d3_ln", [128, 512], F32)
    msk = p.sbuf("d3_msk", [128, 512], U8)
    dsl = p.sbuf("d3_dsl", [128, 512], BF16)
    psA = p.psum("d3_psA", [128, 512], F32)
    p.dma("sync", wdt[:, :], a_wdt, ctx.ld, tag="d3w")
    p.dma("sync", bdt[:, :], a_bdt, ctx.ld, tag="d3b")
    tcast = None
    for q in range(4):
        tl = p.dma("sync", stage[:, :], io["xdblar"][0:64, q * 4096:(q + 1) * 4096],
                   ctx.ld, wait=[tcast] if tcast else [], tag=f"d3l{q}")
        def cf(e, q=q):
            return e.tensor_copy(out=dtb[:, q * 4096:(q + 1) * 4096], in_=stage[:, :])
        tcast = p.emit("vector", cf, wait=[tl], inc=[(ctx.ve, 1)], tag=f"d3c{q}")
    p.barrier("d3_prep")
    prev_d = None
    for chk in range(2):
        for n0 in range(0, TOKG, 512):
            def dmf(e, chk=chk, n0=n0):
                return e.matmul(out=psA[:, 0:512],
                                lhsT=wdt[:, chk * 128:(chk + 1) * 128],
                                rhs=dtb[:, n0:n0 + 512], start=True, stop=True)
            w = [prev_d] if prev_d else []
            t1 = p.emit("tensor", dmf, wait=w, inc=[(ctx.mm, 1)], tag=f"d3m{chk}_{n0}")
            def rev(e, chk=chk):
                return e.tensor_scalar(out=raws[:, :], in0=psA[:, 0:512],
                                       scalar1=bdt[:, chk:chk + 1], scalar2=None, op0=ALU.add)
            t2 = p.emit("vector", rev, wait=[t1], inc=[(ctx.ve, 1)], tag=f"d3r{chk}_{n0}")
            def exf(e):
                return e.activation(out=expt[:, :], in_=raws[:, :], func=AF.Exp)
            t3 = p.emit("scalar", exf, wait=[t2], inc=[(ctx.sc, 1)], tag=f"d3e{chk}_{n0}")
            def lnf(e):
                return e.activation(out=lnt[:, :], in_=expt[:, :], func=AF.Ln,
                                    bias=ccol("one", 128)[0:128, 0:1])
            t4 = p.emit("scalar", lnf, wait=[t3], inc=[(ctx.sc, 1)], tag=f"d3n{chk}_{n0}")
            def cmf(e):
                return e.tensor_scalar(out=msk[:, :], in0=raws[:, :], scalar1=20.0,
                                       scalar2=None, op0=ALU.is_gt)
            t5 = p.emit("vector", cmf, wait=[t3], inc=[(ctx.ve, 1)], tag=f"d3q{chk}_{n0}")
            def slf(e):
                return e.select(out=dsl[:, :], mask=msk[:, :], on_true=raws[:, :],
                                on_false=lnt[:, :])
            t6 = p.emit("vector", slf, wait=[t4, t5], inc=[(ctx.ve, 1)], tag=f"d3s{chk}_{n0}")
            prev_d = p.dma("gpsimd", io["dl"][chk * 128:(chk + 1) * 128, n0:n0 + 512],
                           dsl[:, :], ctx.st, wait=[t6], tag=f"d3o{chk}_{n0}")
    p.barrier("d3_done")
    p.sbuf_release(mk0)


def st_l3_post1(p, ctx, io, W):
    mk0 = p.sbuf_mark()
    a_wo = p.inp("l3_wout", [np.ascontiguousarray(
        W["L3_attn_W_out"][256 * c:256 * (c + 1), :].astype(NBF)
        .reshape(2, 128, 1024).transpose(1, 0, 2).reshape(128, 2048))
        for c in range(NCORES)])
    a_dk = p.inp("l3_dsk", [np.ascontiguousarray(
        W["L3_attn_D_skip"][256 * c:256 * (c + 1)].reshape(2, 128).T)
        for c in range(NCORES)])
    wout = p.sbuf("q3_wo", [128, 2048], BF16)
    dsk = p.sbuf("q3_dk", [128, 2], F32)
    ysl = p.sbuf("q3_y", [128, 2 * 512], F32)
    ucs = p.sbuf("q3_uc", [128, 2 * 512], BF16)
    zss = p.sbuf("q3_z", [128, 2 * 512], BF16)
    yzb = p.sbuf("q3_yz", [128, 2 * 512], BF16)
    osl = p.sbuf("q3_o", [128, 1024], BF16)
    psA = p.psum("q3_psA", [128, 512], F32)
    psB = p.psum("q3_psB", [128, 512], F32)
    p.dma("sync", wout[:, :], a_wo, ctx.ld, tag="q3w")
    p.dma("sync", dsk[:, :], a_dk, ctx.ld, tag="q3d")
    p.barrier("q3_w")
    prev_yz = None
    prev_st = None
    prev_ev = None
    for b in range(B):
        for sl in range(16):
            col0 = b * SEQ + sl * 512
            wld = [prev_yz] if prev_yz else []
            tly, tlu, tlz = None, None, None
            for chk in range(2):
                tly = p.dma("sync", ysl[:, chk * 512:(chk + 1) * 512],
                            io["y"][chk * 128:(chk + 1) * 128, col0:col0 + 512],
                            ctx.ld, wait=wld if chk == 0 else [], tag=f"q3y{b}{sl}{chk}")
                tlu = p.dma("sync", ucs[:, chk * 512:(chk + 1) * 512],
                            io["uc"][chk * 128:(chk + 1) * 128, col0:col0 + 512],
                            ctx.ld, tag=f"q3u{b}{sl}{chk}")
                tlz = p.dma("sync", zss[:, chk * 512:(chk + 1) * 512],
                            io["z"][chk * 128:(chk + 1) * 128, col0:col0 + 512],
                            ctx.ld, tag=f"q3z{b}{sl}{chk}")
            lastyz = None
            for chk in range(2):
                def yz1(e, chk=chk):
                    return e.scalar_tensor_tensor(
                        out=ysl[:, chk * 512:(chk + 1) * 512],
                        in0=ucs[:, chk * 512:(chk + 1) * 512],
                        scalar=dsk[:, chk:chk + 1],
                        in1=ysl[:, chk * 512:(chk + 1) * 512],
                        op0=ALU.mult, op1=ALU.add)
                w = [tly, tlu, tlz] + ([lastyz] if lastyz else [])
                t1 = p.emit("vector", yz1, wait=w, inc=[(ctx.ve, 1)], tag=f"q3a{b}{sl}{chk}")
                def yz2(e, chk=chk):
                    return e.tensor_tensor(out=yzb[:, chk * 512:(chk + 1) * 512],
                                           in0=ysl[:, chk * 512:(chk + 1) * 512],
                                           in1=zss[:, chk * 512:(chk + 1) * 512],
                                           op=ALU.mult)
                lastyz = p.emit("vector", yz2, wait=[t1], inc=[(ctx.ve, 1)],
                                tag=f"q3b{b}{sl}{chk}")
            prev_yz = lastyz
            for tt in range(4):
                s = sl * 512 + tt * 128
                rowb = (s // 1024) * 2048 + b * 1024 + (s % 1024)
                for hf in range(2):
                    ps = [psA, psB][hf]
                    def s3m(e, tt=tt, hf=hf, ps=ps):
                        ins = None
                        for chk in range(2):
                            ins = e.matmul(
                                out=ps[:, 0:512],
                                lhsT=yzb[:, chk * 512 + tt * 128: chk * 512 + (tt + 1) * 128],
                                rhs=wout[:, chk * 1024 + hf * 512: chk * 1024 + hf * 512 + 512],
                                start=(chk == 0), stop=(chk == 1))
                        return ins
                    w = [lastyz] + ([prev_ev] if prev_ev else []) + ([prev_st] if prev_st else [])
                    tm_ = p.emit("tensor", s3m, wait=w, inc=[(ctx.mm, 1)],
                                 tag=f"q3m{b}{sl}{tt}{hf}")
                    def s3e(e, hf=hf, ps=ps):
                        return e.tensor_copy(out=osl[:, hf * 512:(hf + 1) * 512],
                                             in_=ps[:, 0:512])
                    prev_ev = p.emit("vector", s3e, wait=[tm_], inc=[(ctx.ve, 1)],
                                     tag=f"q3v{b}{sl}{tt}{hf}")
                prev_st = p.dma("gpsimd", io["slab"][rowb:rowb + 128, 0:D],
                                osl[:, :], ctx.st, wait=[prev_ev], tag=f"q3s{b}{sl}{tt}")
    p.barrier("q3_done")
    p.sbuf_release(mk0)


def build_fused_l1(p, ctx, W, i_x, o_x):
    d_xn = p.dram("d1_xn", [D, TOKC], F32)
    d_xnb = p.dram("d1_xnb", [D, TOKC], BF16)
    d_xnag = p.dram("d1_xnag", [NCORES * D, TOKC], BF16)
    d_q = p.dram("d1_q", [TOKG, 128], BF16)
    d_k = p.dram("d1_k", [TOKG, 128], BF16)
    d_v = p.dram("d1_v", [TOKG, 128], BF16)
    d_kb3 = p.dram("d1_kb3", [2, TOKG], F32)
    d_slab = p.dram("d1_slab", [TOKG, D], BF16)
    d_s1tm = p.dram("d1_s1tm", [TOKC, D], BF16)
    d_s1 = p.dram("d1_s1", [D, TOKC], F32)
    d_ff = p.dram("d1_ff", [D, TOKC], F32)
    launch_ln(None, _fused=(p, ctx, {"x": i_x, "xn_fm": d_xn.ap(),
                                     "xn_bf": d_xnb.ap()}))
    coll(p, ctx, "AllGather", ALU.bypass, d_xnb, d_xnag, tag="l1_xnag")
    p.barrier("l1_xnag")
    st_l1_projh(p, ctx, {"xnag": d_xnag.ap(), "q": d_q.ap(), "k": d_k.ap(),
                         "v": d_v.ap(), "kb3": d_kb3.ap()}, W)
    launch_l1_scan(None, None, None, None, W,
                   _fused=(p, ctx, {"q": d_q.ap(), "k": d_k.ap(), "v": d_v.ap(),
                                    "kb3": d_kb3.ap(), "s1slab": d_slab.ap()}))
    coll(p, ctx, "ReduceScatter", ALU.add, d_slab, d_s1tm, tag="l1_s1rs")
    p.barrier("l1_s1rs")
    st_slab_tm_to_fm(p, ctx, d_s1tm.ap(), d_s1.ap(), "s1")
    launch_l1_ffn(None, W, _fused=(p, ctx, {"xn": d_xn.ap(), "o": d_ff.ap()}))
    launch_combine(None, None, _fused=(p, ctx, {"xi": i_x, "si": d_s1.ap(),
                                                "ffi": d_ff.ap(), "xn_o": o_x}))


def build_fused_l2(p, ctx, W, i_x, o_x):
    d_xn = p.dram("d2_xn", [D, TOKC], F32)
    d_s2 = p.dram("d2_s", [D, TOKC], F32)
    d_ff = p.dram("d2_ff", [D, TOKC], F32)
    d_parts = p.dram("d2_parts", [128, 16], F32)
    d_partsag = p.dram("d2_partsag", [NCORES * 128, 16], F32)
    launch_ln(None, _fused=(p, ctx, {"x": i_x, "xn_fm": d_xn.ap()}))
    launch_l2_attn(None, W, _fused=(p, ctx, {"xn": d_xn.ap(), "o": d_s2.ap()}))
    launch_l2_parts(None, _fused=(p, ctx, {"xn": d_xn.ap(), "o": d_parts.ap()}))
    coll(p, ctx, "AllGather", ALU.bypass, d_parts, d_partsag, tag="l2_pag")
    p.barrier("l2_pag")
    launch_l2_ffn(None, None, W, _fused=(p, ctx, {"xn": d_xn.ap(),
                                                  "partsag": d_partsag.ap(),
                                                  "o": d_ff.ap()}))
    launch_combine(None, None, _fused=(p, ctx, {"xi": i_x, "si": d_s2.ap(),
                                                "ffi": d_ff.ap(), "xn_o": o_x}))


def build_fused_l3(p, ctx, W, i_x, o_x):
    d_xn = p.dram("d3_xn", [D, TOKC], F32)
    d_xnab = p.dram("d3_xnab", [D, TOKC], BF16)
    d_xnag = p.dram("d3_xnag", [NCORES * D, TOKC], BF16)
    d_u = p.dram("d3_u", [256, TOKG], BF16)
    d_z = p.dram("d3_z", [256, TOKG], BF16)
    d_uc = p.dram("d3_uc", [256, TOKG], BF16)
    d_xdbl = p.dram("d3_xdbl", [96, TOKG], F32)
    d_xdblar = p.dram("d3_xdblar", [96, TOKG], F32)
    d_dl = p.dram("d3_dl", [256, TOKG], BF16)
    d_y = p.dram("d3_y", [256, TOKG], F32)
    d_slab = p.dram("d3_slab", [TOKG, D], BF16)
    d_s3tm = p.dram("d3_s3tm", [TOKC, D], BF16)
    d_s3 = p.dram("d3_s3", [D, TOKC], F32)
    d_ff = p.dram("d3_ff", [D, TOKC], F32)
    a_g = p.inp("l3_g", np.ascontiguousarray(
        W["L3_attn_ln_gamma"].reshape(8, 128).T.astype(np.float32)))
    a_b = p.inp("l3_b", np.ascontiguousarray(
        W["L3_attn_ln_beta"].reshape(8, 128).T.astype(np.float32)))
    launch_ln(None, _fused=(p, ctx, {"x": i_x, "xn_fm": d_xn.ap(),
                                     "affine": (a_g, a_b), "abf": d_xnab.ap()}))
    coll(p, ctx, "AllGather", ALU.bypass, d_xnab, d_xnag, tag="l3_xnag")
    p.barrier("l3_xnag")
    st_l3_pre2(p, ctx, {"xnag": d_xnag.ap(), "u": d_u.ap(), "z": d_z.ap()}, W)
    st_l3_conv(p, ctx, {"u": d_u.ap(), "uc": d_uc.ap(), "xdbl": d_xdbl.ap()}, W)
    coll(p, ctx, "AllReduce", ALU.add, d_xdbl, d_xdblar, tag="l3_xar")
    p.barrier("l3_xar")
    st_l3_delta(p, ctx, {"xdblar": d_xdblar.ap(), "dl": d_dl.ap()}, W)
    launch_l3_scan(None, None, None, W,
                   _fused=(p, ctx, {"dl": d_dl.ap(), "uc": d_uc.ap(),
                                    "xdblar": d_xdblar.ap(), "y": d_y.ap()}))
    st_l3_post1(p, ctx, {"y": d_y.ap(), "uc": d_uc.ap(), "z": d_z.ap(),
                         "slab": d_slab.ap()}, W)
    coll(p, ctx, "ReduceScatter", ALU.add, d_slab, d_s3tm, tag="l3_s3rs")
    p.barrier("l3_s3rs")
    st_slab_tm_to_fm(p, ctx, d_s3tm.ap(), d_s3.ap(), "s3", add_ap=d_xn.ap())
    launch_l3_ffn(None, W, _fused=(p, ctx, {"xn": d_xn.ap(), "o": d_ff.ap()}))
    launch_combine(None, None, _fused=(p, ctx, {"xi": i_x, "si": d_s3.ap(),
                                                "ffi": d_ff.ap(), "xn_o": o_x}))


def build_fused_l0(p, ctx, W, i_x, o_x1, upto=99):
    d_xn0 = p.dram("d_xn0", [D, TOKC], F32)
    d_xn0tm = p.dram("d_xn0tm", [TOKC, D], BF16)
    d_xnh = p.dram("d_xnh", [B, D, 1536], F32)
    d_g = p.dram("d_g", [8192, D], BF16)
    d_h = p.dram("d_h", [1024, D], BF16)
    d_s0 = p.dram("d_s0", [D, TOKC], F32)
    d_cnt = p.dram("d_cnt", [16, 2], F32)
    d_cntag = p.dram("d_cntag", [NCORES * 16, 2], F32)
    d_slab = p.dram("d_slab", [TOKG, D], BF16)
    d_slabrs = p.dram("d_slabrs", [TOKC, D], BF16)
    d_yloc = p.dram("d_yloc", [TOKC, D], BF16)
    d_yag = p.dram("d_yag", [TOKG, D], BF16)

    slotci = p.sbuf("l0_slotci", [128, 16], I32)
    gkr = p.sbuf("l0_gkr", [1, TOKC], F32)
    launch_ln(None, _fused=(p, ctx, {"x": i_x, "xn_fm": d_xn0.ap(),
                                     "xn_tm": d_xn0tm.ap()}))
    if "xn0" in FDBG:
        dump_dram(p, ctx, d_xn0.ap(), "dbg_xn0", [D, TOKC], F32)
    if upto < 2:
        return
    st_l0_halo(p, ctx, {"xn_fm": d_xn0.ap(), "xn_tm": d_xn0tm.ap(),
                        "g_dram": d_g, "h_dram": d_h,
                        "xnh": d_xnh.ap()}, W)
    if "xnh" in FDBG:
        dump_dram(p, ctx, d_xnh.ap().rearrange("b d n -> (b d) n"), "dbg_xnh",
                  [B * D, 1536], F32)
    if upto < 3:
        return
    launch_l0_attn(None, W, _fused=(p, ctx, {"xnh": d_xnh.ap(), "s_fm": d_s0.ap()}))
    if "s0" in FDBG:
        dump_dram(p, ctx, d_s0.ap(), "dbg_s0", [D, TOKC], F32)
    if upto < 4:
        return
    mkR = p.sbuf_mark()
    rsb = launch_l0_route(None, W, _fused=(p, ctx, {"xn": d_xn0.ap(),
                                                    "cnt_dram": d_cnt.ap()}))
    coll(p, ctx, "AllGather", ALU.bypass, d_cnt, d_cntag, tag="cnt_ag")
    p.barrier("cnt_ag")
    if upto < 5:
        return
    st_l0_slot(p, ctx, {"xn_tm": d_xn0tm.ap(), "cnt_ag": d_cntag.ap(),
                        "slab_src": d_slab, "slab_rs": d_slabrs}, W, rsb,
               slotci, gkr)
    p.sbuf_release(mkR)
    st_l0_expert(p, ctx, {"slab_rs": d_slabrs, "y_local": d_yloc.ap()}, W)
    if "yloc" in FDBG:
        dump_dram(p, ctx, d_yloc.ap(), "dbg_yloc", [TOKC, D], BF16)
    coll(p, ctx, "AllGather", ALU.bypass, d_yloc, d_yag, tag="y_ag")
    p.barrier("y_ag")
    if upto < 6:
        return
    launch_combine(None, None, _fused=(p, ctx, {
        "xi": i_x, "si": d_s0.ap(), "ysl": d_yag.ap(),
        "slot_sb": slotci, "gk_sb": gkr, "xn_o": o_x1}))


def run_fused(W, upto=99, nlayers=4):
    p = Prog("fmk")
    ctx = Ctx(p)
    ccol, tcc = load_consts(p, ctx)
    x = to_fm(np.asarray(W["L0_attn_x"], np.float32))
    i_x = p.inp("i_x", [fm_slice(x, c) for c in range(NCORES)])
    o_x = p.out("o_x1", [D, TOKC], F32)
    builders = [
        lambda xi, xo: build_fused_l0(p, ctx, W, xi, xo, upto=upto),
        lambda xi, xo: build_fused_l1(p, ctx, W, xi, xo),
        lambda xi, xo: build_fused_l2(p, ctx, W, xi, xo),
        lambda xi, xo: build_fused_l3(p, ctx, W, xi, xo),
    ]
    cur = i_x
    for li in range(nlayers):
        if li == nlayers - 1:
            nxt = o_x
        else:
            nxt = p.dram(f"d_x{li + 1}", [D, TOKC], F32).ap()
        mk = p.sbuf_mark()
        builders[li](cur, nxt)
        p.sbuf_release(mk)
        cur = nxt
    def fin(e):
        return e.nop()
    p.emit("gpsimd", fin, tag="fin")
    for e in ("sync", "tensor", "vector", "scalar"):
        if p.ops[e]:
            def nf(en):
                return en.nop()
            p.emit(e, nf, tag="fin2")
    res = p.run()
    return res
